# version 30
# baseline (speedup 1.0000x reference)
import math
import numpy as np
from contextlib import ExitStack

import concourse.bass as bass
import concourse.mybir as mybir
from concourse.bass_types import DynSlice as ds
from concourse.bass_utils import run_bass_kernel_spmd
from concourse.ordered_engine_set import OrderedEngineSet
from concourse.expressions import smin

F32 = mybir.dt.float32
BF16 = mybir.dt.bfloat16
U32 = mybir.dt.uint32
AF = mybir.ActivationFunctionType
ALU = mybir.AluOpType
AX = mybir.AxisListType

D = 1024
DK = 512
DV = 1024
H = 4
HDK = 128
HDV = 256
NM = 4
NCHAN = 5
LOW = 16
NE = 8
FI = 2048
EPS = 1e-5
SCALE = HDK ** -0.5

ROLES = ("T", "V", "S", "G", "Y")
SEMS = ("sT", "sV", "sS", "sG", "sY", "sWA", "sWB", "sGC", "sGD")
BIG = float(1 << 24)


class Aff:
    __slots__ = ("c", "m")

    def __init__(self, c=0, m=None):
        self.c = c
        self.m = m or {}

    def add(self, n):
        return Aff(self.c + n, self.m)

    def addreg(self, reg, coef):
        m2 = dict(self.m)
        k = id(reg)
        r0, c0 = m2.get(k, (reg, 0))
        m2[k] = (reg, c0 + coef)
        return Aff(self.c, m2)

    def expr(self):
        e = self.c
        for reg, co in self.m.values():
            if co:
                e = reg * co + e
        return e


class Bd:

    def __init__(self, role, eng, sems, loop_delta):
        self.role = role
        self.eng = eng
        self.sems = sems
        self.cnt = {k: Aff() for k in SEMS}
        self.loop_delta = loop_delta
        self._loop_path = []
        self._r1 = None
        self._r2 = None

    def _scratch(self):
        if self._r1 is None:
            self._r1 = self.eng.alloc_register("bd_w1")
            self._r2 = self.eng.alloc_register("bd_w2")
        return self._r1, self._r2

    def bump(self, sem, n):
        self.cnt[sem] = self.cnt[sem].add(n)

    def mark(self, sem):
        return self.cnt[sem]

    def wait(self, role, sem, val=None):
        if self.role != role:
            return
        v = self.cnt[sem] if val is None else val
        if not isinstance(v, Aff):
            self.eng.wait_ge(self.sems[sem], v)
            return
        terms = [(reg, co) for reg, co in v.m.values() if co]
        if not terms:
            self.eng.wait_ge(self.sems[sem], v.c)
            return
        r1, r2 = self._scratch()
        reg0, co0 = terms[0]
        self.eng.reg_alu(r1, reg0, co0, ALU.mult)
        for reg, co in terms[1:]:
            self.eng.reg_alu(r2, reg, co, ALU.mult)
            self.eng.reg_alu(r1, r1, r2, ALU.add)
        if v.c:
            self.eng.reg_alu(r1, r1, v.c, ALU.add)
        self.eng.wait_ge(self.sems[sem], r1)

    def _fin(self, sem, inc, ins):
        if inc:
            self.bump(sem, inc)
            if ins is not None:
                ins.then_inc(self.sems[sem], inc)

    def mm(self, out, lhsT, rhs, start=True, stop=True, inc=0):
        ins = self.eng.matmul(out, lhsT, rhs, start=start, stop=stop) if self.role == "T" else None
        self._fin("sT", inc, ins)

    def tr(self, out, in_, ident, inc=0):
        ins = self.eng.transpose(out, in_, ident) if self.role == "T" else None
        self._fin("sT", inc, ins)

    def v(self, meth, *args, inc=1, **kw):
        ins = getattr(self.eng, meth)(*args, **kw) if self.role == "V" else None
        self._fin("sV", inc, ins)

    def s(self, meth, *args, inc=1, **kw):
        ins = getattr(self.eng, meth)(*args, **kw) if self.role == "S" else None
        self._fin("sS", inc, ins)

    def gv(self, meth, *args, inc=1, **kw):
        ins = getattr(self.eng, meth)(*args, **kw) if self.role == "G" else None
        self._fin("sGC", inc, ins)

    def gdma(self, out, in_, inc=16, sem="sG", **kw):
        self.wait("G", sem)
        ins = self.eng.dma_start(out=out, in_=in_, **kw) if self.role == "G" else None
        self._fin(sem, inc, ins)

    def ydma(self, out, in_, inc=16, sem="sY", **kw):
        self.wait("Y", sem)
        ins = self.eng.dma_start(out=out, in_=in_, **kw) if self.role == "Y" else None
        self._fin(sem, inc, ins)

    def sdma(self, out, in_, inc=16, sem="sS", **kw):
        self.wait("S", sem)
        ins = self.eng.dma_start(out=out, in_=in_, **kw) if self.role == "S" else None
        self._fin(sem, inc, ins)

    def gind(self, out, out_offset, in_, in_offset, bounds, inc=16, element_offset=0):
        from concourse.bass import IndirectOffsetOnAxis
        ins = None
        if self.role == "G":
            if not hasattr(self, "_bcregs"):
                self._bcregs = {}
            if bounds not in self._bcregs:
                r = self.eng.alloc_register(f"g_bc{len(self._bcregs)}")
                self.eng.reg_mov(r, bounds)
                self._bcregs[bounds] = r
            oo = IndirectOffsetOnAxis(ap=out_offset, axis=0) if out_offset is not None else None
            io = IndirectOffsetOnAxis(ap=in_offset, axis=0) if in_offset is not None else None
            ins = self.eng.indirect_dma_start(
                out=out, out_offset=oo, in_=in_, in_offset=io,
                element_offset=element_offset,
                bounds_check=self._bcregs[bounds], oob_is_err=False)
        self._fin("sGD", inc, ins)

    def eload(self, role, key, ap, maxv, par=0):
        if self.role != role:
            return 0
        nc_ = self.eng.bass
        if not hasattr(self, "_eregs"):
            self._eregs = {}
        kp = (key, par)
        if kp not in self._eregs:
            es_ = OrderedEngineSet([self.eng.engine])
            regs = nc_.alloc_registers(f"el_{self.role}_{key}_{par}", engines=es_)
            rv = nc_.snap(regs, engines=es_, min_val=0, max_val=maxv)
            self._eregs[kp] = (regs, rv)
        regs, rv = self._eregs[kp]
        self.eng.reg_load(regs.handles[0], ap)
        return rv

    def ploop(self, trips, body):
        for i in range(trips):
            body(i)

    def loop(self, key, trips, body):
        key = "/".join(self._loop_path + [key])
        start = dict(self.cnt)
        if self.role == "C":
            self._loop_path.append(key)
            body(0)
            self._loop_path.pop()
            delta = {k: self.cnt[k].c - start[k].c for k in SEMS}
            self.loop_delta[key] = delta
            self.cnt = {k: start[k].add(trips * delta[k]) for k in SEMS}
        else:
            delta = self.loop_delta[key]
            self._loop_path.append(key)
            depth = len(self._loop_path)
            nc_ = self.eng.bass
            engines = OrderedEngineSet([self.eng.engine])
            if not hasattr(self, "_lregs"):
                self._lregs = {}
            if depth not in self._lregs:
                self._lregs[depth] = nc_.alloc_registers(f"bdl_{self.role}_{depth}",
                                                         engines=engines)
            regs = self._lregs[depth]
            lid = nc_.next_id()
            lstart = f"bdl{lid}_s"
            lend = f"bdl{lid}_e"
            nc_.regs_mov(regs, 0)
            nc_.br(lstart, engines=engines)
            with nc_.body(lstart, valid_engines=engines):
                reg = nc_.snap(regs, engines=engines, min_val=0, max_val=trips - 1)
                self.cnt = {k: start[k].addreg(reg, delta[k]) for k in SEMS}
                body(reg)
                for k in SEMS:
                    got = self.cnt[k].c - start[k].c
                    assert got == delta[k], f"loop {key} sem {k}: {got} != {delta[k]}"
                nc_.regs_alu(regs, regs, 1, op=ALU.add)
                nc_.br_lt(regs, trips, on_true=lstart, on_false=lend, engines=engines)
            nc_.switch_bb(lend)
            self._loop_path.pop()
            self.cnt = {k: start[k].add(trips * delta[k]) for k in SEMS}


def build_program(B=4, S=2048):
    NCH = S // 128
    NT = S // 512
    CAP = math.ceil(S * 0.5)
    SP8 = 8 * S

    nc = bass.Bass("TRN2", target_bir_lowering=False)

    A = lambda name, shape, dt=F32: nc.dram_tensor(name, shape, dt, kind="ExternalInput").ap()
    xs = A("xs", [B * S, D])
    w_gate = A("w_gate", [D, NM])
    w_q = A("w_q", [D, DK])
    w_k = A("w_k", [NM, D, DK])
    w_v = A("w_v", [NM, D, DV])
    w_gk1T = A("w_gk1T", [NM, LOW, D])
    w_gk2 = A("w_gk2", [NM, LOW, DK])
    b_gk = A("b_gk", [NM, DK])
    w_ks = A("w_ks", [D, DK])
    w_vs = A("w_vs", [D, DV])
    w_gk1sT = A("w_gk1sT", [LOW, D])
    w_gk2s = A("w_gk2s", [LOW, DK])
    b_gks = A("b_gks", [1, DK])
    w_g = A("w_g", [D, DV])
    w_o = A("w_o", [DV, D])
    normw4 = A("normw4", [1, DV])
    fsh_gT = A("fsh_gT", [D, FI], BF16)
    fsh_uT = A("fsh_uT", [D, FI], BF16)
    fsh_dT = A("fsh_dT", [FI, D], BF16)
    fgate = A("fgate", [NE, D, FI], BF16)
    fup = A("fup", [NE, D, FI], BF16)
    fdown = A("fdown", [NE, FI, D], BF16)
    frouterT = A("frouterT", [D, NE])
    fbias = A("fbias", [1, NE])
    c_id = A("c_id", [128, 128])
    c_t16 = A("c_t16", [128, 128])
    c_tS = A("c_tS", [128, 128])
    c_or = A("c_or", [1, 128])
    c_ov16 = A("c_ov16", [128, 1])
    c_io4 = A("c_io4", [128, 4])
    c_io8 = A("c_io8", [128, 8])
    c_m4 = A("c_m4", [128, 512])
    c_misc = A("c_misc", [128, 4])

    out = nc.dram_tensor("out", [B * S, D], F32, kind="ExternalOutput").ap()

    xnT_sc = nc.dram_tensor("xnT_sc", [128, B * SP8], BF16, kind="Internal").ap()
    oacc_sc = nc.dram_tensor("oacc_sc", [128, B * NCH * DV], BF16, kind="Internal").ap()
    wgk_sc = nc.dram_tensor("wgk_sc", [128, NCHAN * 8 * DK], BF16, kind="Internal").ap()

    T_TOK = B * S
    NTILE = -(-(2 * T_TOK + NE * 511) // 512)
    NSLOT = NTILE * 512
    NGC = B * NCH
    xn_tok = nc.dram_tensor("xn_tok", [T_TOK + 1, D], BF16, kind="Internal").ap()
    disp = nc.dram_tensor("disp", [NSLOT, 4], F32, kind="Internal").ap()
    comb = nc.dram_tensor("comb", [2 * T_TOK, DV], BF16, kind="Internal").ap()
    cnt_sc = nc.dram_tensor("cnt_sc", [NGC, NE], F32, kind="Internal").ap()
    rank_sc = nc.dram_tensor("rank_sc", [128, NGC * NE], F32, kind="Internal").ap()
    vals_sc = nc.dram_tensor("vals_sc", [128, NGC * NE * 4], F32, kind="Internal").ap()
    fb_sc = nc.dram_tensor("fb_sc", [1, (NGC + 1) * NE], F32, kind="Internal").ap()
    eidf_sc = nc.dram_tensor("eidf_sc", [1, NTILE], F32, kind="Internal").ap()
    facc_sc = nc.dram_tensor("facc_sc", [128, B * NCH * DV], BF16, kind="Internal").ap()
    c_pidx = A("c_pidx", [128, 1])
    c_th = A("c_th", [128, 16])

    es = ExitStack()
    es_ab = ExitStack()
    es_p2 = ExitStack()
    sb = lambda name, shape, dt: es.enter_context(nc.sbuf_tensor(name, shape, dt))
    sba = lambda name, shape, dt: es_ab.enter_context(nc.sbuf_tensor(name, shape, dt))
    sbp = lambda name, shape, dt: es_p2.enter_context(nc.sbuf_tensor(name, shape, dt))
    pt = lambda name, shape, dt=F32: es.enter_context(nc.psum_tensor(name, shape, dt))

    k_id = sb("k_id", [128, 128], BF16)
    k_t16 = sb("k_t16", [128, 128], BF16)
    k_tS = sb("k_tS", [128, 128], F32)
    k_or = sb("k_or", [1, 128], BF16)
    k_orf = sb("k_orf", [1, 128], F32)
    k_ov16 = sb("k_ov16", [128, 1], BF16)
    k_ocf = sb("k_ocf", [128, 1], F32)
    k_io4 = sb("k_io4", [128, 4], F32)
    k_io8 = sb("k_io8", [128, 8], F32)
    k_nw = sb("k_nw", [128, DV], BF16)
    xstat = sb("xstat", [128, 8 * 128], BF16)
    k_misc = sb("k_misc", [128, 4], F32)
    lg8 = sb("lg8", [128, 8], F32)
    m8 = sb("m8", [128, 8], F32)
    idxu = sb("idxu", [128, 8], U32)
    idxf = sb("idxf", [128, 8], F32)
    d12 = sb("d12", [128, 1], F32)
    rwt = sb("rwt", [128, 1], F32)
    nc.const_aps.register(F32, EPS, k_misc[:, 0:1])
    nc.const_aps.register(F32, -0.5, k_misc[:, 1:2])
    nc.const_aps.register(F32, -1.0, k_misc[:, 2:3])
    nc.const_aps.register(F32, 1.0 / HDV, k_misc[:, 3:4])
    k_pidx = sb("k_pidx", [128, 1], F32)
    k_th = sb("k_th", [128, 16], F32)
    k_idf = sb("k_idf", [128, 128], F32)

    k_m4 = sba("k_m4", [128, 512], BF16)
    k_nwrow = sba("k_nwrow", [1, DV], F32)
    k_bgl = [sba(f"k_bg{i}", [1, DK], BF16) for i in range(NCHAN)]
    k_fb = sba("k_fb", [1, NE], BF16)
    k_g1 = sba("k_g1", [16, NCHAN * D], BF16)
    k_g2 = sba("k_g2", [16, NCHAN * DK], BF16)
    k_wgate = sba("k_wgate", [128, 8 * NM], BF16)
    k_frT = sba("k_frT", [128, 8 * NE], BF16)
    xbuf = sba("xbuf", [128, 4 * D], F32)
    bnraw = sba("bnraw", [128, 12], F32)
    bnst = sba("bnst", [128, 2], F32)
    r1c = sba("r1c", [128, 1], F32)
    rstd = sba("rstd", [128, 1], F32)
    xnT = sba("xnT", [128, SP8], BF16)
    q_b = sba("q_b", [128, NCH * 512], BF16)
    r_se1 = sba("r_se1", [128, NCH * 4], F32)
    r_se2 = sba("r_se2", [128, NCH * 4], F32)
    r_sel = sba("r_sel", [128, NCH * 4], F32)
    r_rank = sba("r_rank", [128, NCH * 4], F32)
    r_rw = sba("r_rw", [128, NCH * 2], F32)
    r_kept = sba("r_kept", [128, NCH * 4], F32)
    r_wm = sba("r_wm", [128, NCH * 4], F32)
    cnt_st = sba("cnt_st", [1, NCH * 4], F32)
    tot_r = sba("tot_r", [1, 4], F32)
    th0_r = sba("th0_r", [1, 4], F32)
    th_r = sba("th_r", [1, 4], F32)
    base_r = sba("base_r", [1, 4], F32)
    wch = sba("wch", [128, 8 * 2048], BF16)
    kcb0 = sba("kcb0", [128, 512], BF16)
    vcb0 = sba("vcb0", [128, DV], BF16)
    gkcb0 = sba("gkcb0", [128, 512], BF16)
    kcb1 = sba("kcb1", [128, 512], BF16)
    vcb1 = sba("vcb1", [128, DV], BF16)
    gkcb1 = sba("gkcb1", [128, 512], BF16)
    kgv = [(kcb0, vcb0, gkcb0), (kcb1, vcb1, gkcb1)]
    sig1 = sba("sig1", [128, 512], F32)
    sig2 = sba("sig2", [128, 512], F32)
    lam = sba("lam", [128, 512], F32)
    nlam = sba("nlam", [128, 512], F32)
    decay = sba("decay", [128, 4], F32)
    qt = sba("qt", [128, 512], BF16)
    kt = sba("kt", [128, 512], BF16)
    qtT = sba("qtT", [128, 512], BF16)
    ktT = sba("ktT", [128, 512], BF16)
    att = sba("att", [128, 512], BF16)
    Sbf = sba("Sbf", [128, 2 * 1024], BF16)
    t1 = sba("t1", [128, 512], F32)
    t1s = sba("t1s", [128, 512], F32)
    t1bs = sba("t1bs", [128, 512], F32)
    t1b = sba("t1b", [128, 512], F32)
    oac = sba("oac", [128, NCH * DV], BF16)
    flg8 = sba("flg8", [128, 8], F32)
    fm8 = m8
    fidxu = idxu
    fidxf = idxf
    fse8a = sig1[:, 0:8]
    fse8b = sig1[:, 8:16]
    fsel8 = sig1[:, 16:24]
    ftmp8 = sig1[:, 24:32]
    ftmp8b = sig1[:, 32:40]
    fsg = sig1[:, 40:42]
    fd1 = sig1[:, 42:43]
    frc = sig1[:, 43:44]
    fw2 = sig1[:, 44:46]
    tcol2 = sig1[:, 46:48]
    cnt16 = sig2[0:1, 0:NCH * 8]
    stag_rank = sba("stag_rank", [128, NCH * 8], F32)
    stag_vals = sba("stag_vals", [128, NCH * 32], F32)

    p2v = {}

    def alloc_p2():
        p2v["xnT2"] = sbp("xnT2", [128, SP8], BF16)
        p2v["gwb"] = sbp("gwb", [128, 8 * FI], BF16)
        p2v["uwb"] = sbp("uwb", [128, 8 * FI], BF16)
        p2v["dwb"] = sbp("dwb", [128, 16 * 1024], BF16)
        p2v["yg"] = sbp("yg", [128, 16 * 512], BF16)
        p2v["sgt"] = sbp("sgt", [128, 512], BF16)
        p2v["sgt2"] = sbp("sgt2", [128, 512], BF16)
        p2v["facc"] = sbp("facc", [128, NCH * DV], BF16)
        p2v["ssum"] = sbp("ssum", [128, 4], F32)
        p2v["rstd4"] = sbp("rstd4", [128, 4], F32)
        p2v["dispsb"] = sbp("dispsb", [128, 32], F32)
        p2v["widx1"] = sbp("widx1", [128, NTILE], U32)
        p2v["widx2"] = sbp("widx2", [128, NTILE], U32)
        p2v["eidb"] = sbp("eidb", [128, NTILE], F32)
        p2v["widxf"] = sbp("widxf", [128, NTILE], F32)
        p2v["eidrowF"] = sbp("eidrowF", [1, NTILE], F32)
        uwbF = p2v["uwb"].bitcast(F32)
        gwbF = p2v["gwb"].bitcast(F32)
        p2v["sq_t"] = uwbF[:, 5200:5456]
        p2v["x_t"] = uwbF[:, 6096:6096 + 2 * D]
        p2v["outsb"] = gwbF[:, 2048:2048 + 4 * D]
        p2v["oat"] = p2v["gwb"][:, 12288:12288 + 4 * DV]
        p2v["on_t"] = p2v["yg"][:, 4096:4096 + DV]
        p2v["ogT"] = p2v["yg"][:, 5120:5120 + DV]
        p2v["eidu"] = sbp("eidu", [128, 1], U32)
        p2v["posu"] = xstat.bitcast(U32)[:, 0:NGC * 8]
        p2v["fbrow"] = p2v["outsb"][0:1, 2560:2560 + (NGC + 1) * NE]
        p2v["fb2"] = p2v["outsb"][0:NE, 3084:3084 + NGC + 2]
        p2v["p8sb"] = p2v["outsb"][0:NE, 3150:3151]
        p2v["pbs"] = p2v["outsb"][0:NE, 3151:3152]
        p2v["fbTsb"] = p2v["outsb"][0:NGC + 1, 3152:3152 + NE]
        p2v["cnt64sb"] = p2v["outsb"][0:NGC, 3160:3160 + NE]
        p2v["fin16"] = p2v["sq_t"][:, 16:32]
        p2v["eidf"] = p2v["sq_t"][:, 33:34]
        p2v["ts512"] = p2v["sq_t"][:, 32:33]
        p2v["fin8b"] = p2v["sq_t"][:, 8:16]
        p2v["posf"] = p2v["sq_t"][:, 0:8]
        p2v["rankm_sb"] = p2v["outsb"][:, 2048:2560]
        p2v["valsb"] = p2v["outsb"][:, 0:2048]
        p2v["pattbig"] = p2v["x_t"][:, 0:(NSLOT // 128) * 4]
    wtmp = sba("wtmp", [128, 8 * 512], BF16)

    pf = [pt(f"pf{i}", [128, 512], F32) for i in range(7)]
    pb = pt("pb", [128, 1024], BF16)

    sems = {s_: es.enter_context(nc.semaphore(s_)) for s_ in SEMS}
    loop_delta = {}

    wk_src = [w_k[mi] for mi in range(NM)] + [w_ks]
    wv_src = [w_v[mi] for mi in range(NM)] + [w_vs]

    def describe_ab(bd: Bd):
        bd.gdma(k_id[:], c_id)
        bd.gdma(k_t16[:], c_t16)
        bd.ydma(k_tS[:], c_tS)
        bd.gdma(k_or[:], c_or)
        bd.ydma(k_orf[:], c_or)
        bd.gdma(k_ov16[:], c_ov16)
        bd.ydma(k_ocf[:], c_ov16)
        bd.ydma(k_io4[:], c_io4)
        bd.ydma(k_io8[:], c_io8)
        bd.gdma(k_m4[:], c_m4)
        for ch in range(NM):
            bd.gdma(k_bgl[ch][:], b_gk[ch:ch + 1, :])
        bd.gdma(k_bgl[4][:], b_gks)
        bd.gdma(k_fb[:], fbias)
        for ch in range(NM):
            bd.gdma(k_g1[:16, ch * D:(ch + 1) * D], w_gk1T[ch])
            bd.gdma(k_g2[:16, ch * DK:(ch + 1) * DK], w_gk2[ch])
        bd.gdma(k_g1[:16, 4 * D:5 * D], w_gk1sT)
        bd.gdma(k_g2[:16, 4 * DK:5 * DK], w_gk2s)
        bd.gdma(k_wgate[:].rearrange("p (dt e) -> p dt e", dt=8),
                w_gate.rearrange("(dt p) e -> p dt e", p=128))
        bd.gdma(k_frT[:].rearrange("p (dt e) -> p dt e", dt=8),
                frouterT.rearrange("(dt p) e -> p dt e", p=128))
        bd.ydma(k_nwrow[:], normw4)
        bd.ydma(k_misc[:], c_misc)
        bd.ydma(k_pidx[:], c_pidx)
        bd.ydma(k_th[:], c_th)
        bd.ydma(k_idf[:], c_id)
        bd.wait("V", "sY")
        bd.v("tensor_scalar", k_ocf[:], k_ocf[:], 16.0, None, ALU.mult)
        bd.v("memset", lg8[:], -1e30)
        bd.v("memset", stag_vals[:], 0.0)
        bd.v("memset", xbuf[0:1, 0:D], 0.0)
        bd.wait("G", "sV")
        bd.gdma(xn_tok[T_TOK:T_TOK + 1, :], xbuf[0:1, 0:D])
        bd.wait("Y", "sG")
        bd.wait("T", "sY")
        bd.wait("T", "sG")
        bd.wait("T", "sV")
        bd.mm(pf[0][:, 0:512], k_orf[:], k_nwrow[0:1, 0:512], inc=1)
        bd.mm(pf[1][:, 0:512], k_orf[:], k_nwrow[0:1, 512:1024], inc=1)
        bd.wait("V", "sT")
        bd.v("tensor_copy", k_nw[:, 0:512], pf[0][:, 0:512])
        bd.v("tensor_copy", k_nw[:, 512:1024], pf[1][:, 0:512])

        for ch in range(NCHAN):
            bd.wait("V", "sG")
            for dt_ in range(8):
                bd.wait("T", "sV")
                bd.mm(pf[dt_ % 2][:, 0:512], k_g1[:16, ch * D + dt_ * 128: ch * D + (dt_ + 1) * 128],
                      k_g2[:16, ch * DK:(ch + 1) * DK], inc=1)
                bd.wait("V", "sT")
                bd.v("tensor_copy", wtmp[:, dt_ * 512:(dt_ + 1) * 512], pf[dt_ % 2][:, 0:512])
            bd.wait("G", "sV")
            bd.gdma(wgk_sc[:, ch * 8 * DK:(ch + 1) * 8 * DK], wtmp[:])

        def phase_ab(b):
            bd.wait("V", "sG")
            bd.wait("V", "sT")
            bd.wait("T", "sG")
            bd.wait("S", "sG")
            bd.wait("S", "sY")

            def a_chunk(c):
                q4, c4 = c // 4, c % 4
                if c4 == 0:
                    bd.wait("Y", "sV")
                    bd.ydma(xbuf[:].rearrange("p (c d) -> p c d", c=4),
                            xs.rearrange("(b q c p) d -> p b q c d", q=NCH // 4, c=4, p=128)[:, ds(b, 1), q4, :, :])
                xb = xbuf[:, c4 * D:(c4 + 1) * D]
                bd.wait("V", "sY")
                bd.v("bn_stats", bnraw[:, 0:6], xb[:, 0:512])
                bd.v("bn_stats", bnraw[:, 6:12], xb[:, 512:1024])
                bd.wait("V", "sV")
                bd.v("bn_aggr", bnst[:], bnraw[:])
                bd.wait("S", "sV")
                bd.s("activation", r1c[:], bnst[:, 1:2], AF.Ln, bias=EPS)
                bd.wait("S", "sS")
                bd.s("activation", rstd[:], r1c[:], AF.Exp, scale=-0.5)
                bd.wait("V", "sS")
                bd.wait("V", "sG")
                xn_st = wch[:, c * D:(c + 1) * D]
                bd.v("tensor_scalar", xn_st, xb, bnst[:, 0:1], rstd[:], ALU.subtract, ALU.mult)
                bd.wait("T", "sV")
                for dt_ in range(8):
                    bd.tr(pb[:, dt_ * 128:(dt_ + 1) * 128], xn_st[:, dt_ * 128:(dt_ + 1) * 128], k_id[:],
                          inc=(1 if dt_ == 7 else 0))
                bd.wait("V", "sT")
                bd.v("tensor_copy", xstat[:], pb[:])
                bd.wait("V", "sV")
                bd.v("tensor_copy",
                     xnT[:].rearrange("p (dt c s) -> p dt c s", dt=8, s=128)[:, :, ds(c, 1), :],
                     xstat[:].rearrange("p (dt s) -> p dt s", dt=8))
                bd.wait("T", "sV")
                for dt_ in range(8):
                    bd.mm(pf[0][:, 0:4], xstat[:, dt_ * 128:(dt_ + 1) * 128],
                          k_wgate[:, dt_ * 4:(dt_ + 1) * 4], start=(dt_ == 0), stop=(dt_ == 7),
                          inc=(1 if dt_ == 7 else 0))
                bd.wait("V", "sT")
                bd.v("tensor_copy", lg8[:, 0:4], pf[0][:, 0:4])
                bd.wait("V", "sV")
                bd.v("max", m8[:], lg8[:])
                bd.wait("V", "sV")
                bd.v("max_index", idxu[:], m8[:], lg8[:])
                bd.wait("V", "sV")
                bd.v("tensor_copy", idxf[:], idxu[:])
                bd.wait("V", "sV")
                bd.v("tensor_single_scalar", r_se1[:].rearrange("p (c m) -> p c m", m=4)[:, ds(c, 1), :], k_io4[:], idxf[:, 0:1], ALU.is_equal)
                bd.v("tensor_single_scalar", r_se2[:].rearrange("p (c m) -> p c m", m=4)[:, ds(c, 1), :], k_io4[:], idxf[:, 1:2], ALU.is_equal)
                bd.v("tensor_sub", d12[:], m8[:, 0:1], m8[:, 1:2])
                bd.wait("V", "sV")
                bd.v("tensor_add", r_sel[:].rearrange("p (c m) -> p c m", m=4)[:, ds(c, 1), :], r_se1[:].rearrange("p (c m) -> p c m", m=4)[:, ds(c, 1), :], r_se2[:].rearrange("p (c m) -> p c m", m=4)[:, ds(c, 1), :])
                bd.wait("S", "sV")
                bd.s("activation", rwt[:], d12[:], AF.Sigmoid)
                bd.wait("V", "sS")
                bd.v("tensor_scalar", r_rw[:].rearrange("p (c j) -> p c j", j=2)[:, ds(c, 1), 0:1], rwt[:], SCALE, None, ALU.mult)
                bd.wait("V", "sV")
                bd.v("tensor_scalar", r_rw[:].rearrange("p (c j) -> p c j", j=2)[:, ds(c, 1), 1:2],
                     r_rw[:].rearrange("p (c j) -> p c j", j=2)[:, ds(c, 1), 0:1], -1.0, SCALE,
                     ALU.mult, ALU.add)
                bd.wait("T", "sV")
                bd.mm(pf[1][:, 0:4], k_tS[:], r_sel[:].rearrange("p (c m) -> p c m", m=4)[:, ds(c, 1), :], inc=1)
                bd.mm(pf[2][0:1, 0:4], k_ocf[:], r_sel[:].rearrange("p (c m) -> p c m", m=4)[:, ds(c, 1), :], inc=1)
                bd.wait("V", "sT")
                bd.v("tensor_copy", r_rank[:].rearrange("p (c m) -> p c m", m=4)[:, ds(c, 1), :], pf[1][:, 0:4])
                bd.v("tensor_copy", cnt_st[0:1, :].rearrange("p (c m) -> p c m", m=4)[:, ds(c, 1), :], pf[2][0:1, 0:4])

                bd.wait("T", "sV")
                for dt_ in range(8):
                    bd.mm(pf[3][:, 0:8], xstat[:, dt_ * 128:(dt_ + 1) * 128],
                          k_frT[:, dt_ * 8:(dt_ + 1) * 8], start=(dt_ == 0), stop=False)
                bd.mm(pf[3][:, 0:8], k_or[:], k_fb[:], start=False, stop=True, inc=1)
                bd.wait("V", "sT")
                bd.v("tensor_copy", flg8[:], pf[3][:, 0:8])
                bd.wait("V", "sV")
                bd.v("max", fm8[:], flg8[:])
                bd.wait("V", "sV")
                bd.v("max_index", fidxu[:], fm8[:], flg8[:])
                bd.wait("S", "sV")
                bd.s("activation", fsg[:], fm8[:, 0:2], AF.Sigmoid)
                bd.wait("V", "sV")
                bd.v("tensor_copy", fidxf[:], fidxu[:])
                bd.wait("V", "sV")
                bd.v("tensor_single_scalar", fse8a[:], k_io8[:], fidxf[:, 0:1], ALU.is_equal)
                bd.v("tensor_single_scalar", fse8b[:], k_io8[:], fidxf[:, 1:2], ALU.is_equal)
                bd.wait("V", "sS")
                bd.v("tensor_add", fd1[:], fsg[:, 0:1], fsg[:, 1:2])
                bd.wait("V", "sV")
                bd.v("reciprocal", frc[:], fd1[:])
                bd.v("tensor_add", fsel8[:], fse8a[:], fse8b[:])
                bd.wait("V", "sV")
                bd.v("tensor_scalar", fw2[:], fsg[:], frc[:], None, ALU.mult)
                bd.wait("T", "sV")
                bd.mm(pf[3][:, 0:8], k_tS[:], fsel8[:], inc=1)
                bd.mm(pf[4][0:1, 0:8], k_ocf[:], fsel8[:], inc=1)
                bd.wait("V", "sT")
                bd.v("tensor_copy", ftmp8[:], pf[3][:, 0:8])
                bd.v("tensor_copy", cnt16[0:1, c * 8:(c + 1) * 8], pf[4][0:1, 0:8])
                bd.wait("V", "sV")
                bd.v("tensor_scalar", tcol2[:, 0:1], k_pidx[:], float(c * 128), None, ALU.add)
                bd.wait("V", "sV")
                bd.v("tensor_scalar", tcol2[:, 1:2], tcol2[:, 0:1], 2.0, None, ALU.mult)
                vv = stag_vals[:].rearrange("p (c e v) -> p c v e", c=NCH, e=8, v=4)
                bd.wait("V", "sV")
                bd.v("tensor_scalar", vv[:, ds(c, 1), 0, :], k_io8[:], 0.0, tcol2[:, 0:1],
                     ALU.mult, ALU.add)
                bd.v("tensor_scalar", ftmp8b[:], fse8a[:], fw2[:, 0:1], None, ALU.mult)
                bd.wait("V", "sV")
                bd.v("scalar_tensor_tensor", vv[:, ds(c, 1), 1, :], fse8b[:], fw2[:, 1:2],
                     ftmp8b[:], ALU.mult, ALU.add)
                bd.v("tensor_scalar", vv[:, ds(c, 1), 2, :], fse8b[:], 1.0, tcol2[:, 1:2],
                     ALU.mult, ALU.add)
                bd.wait("V", "sV")
                bd.v("tensor_scalar", ftmp8[:], ftmp8[:], -BIG, None, ALU.add)
                bd.wait("V", "sV")
                bd.v("tensor_mul", ftmp8[:], ftmp8[:], fsel8[:])
                bd.wait("V", "sV")
                bd.v("tensor_scalar", stag_rank[:, c * 8:(c + 1) * 8], ftmp8[:], BIG, None, ALU.add)

            bd.ploop(NCH, a_chunk)

            bd.wait("G", "sV")
            bd.gdma(xnT_sc.rearrange("p (b x) -> p b x", x=SP8)[:, ds(b, 1), :], xnT[:])
            bd.gdma(xn_tok[0:T_TOK, :].rearrange("(b c p) h -> b p c h", c=NCH, p=128)[ds(b, 1), :, :, :],
                    wch[:].rearrange("p (c h) -> p c h", c=NCH))
            bd.gdma(cnt_sc.rearrange("(b x) e -> b (x e)", x=NCH)[ds(b, 1), :], cnt16)
            bd.gdma(rank_sc.rearrange("p (b x) -> p b x", x=NCH * 8)[:, ds(b, 1), :], stag_rank[:])
            bd.gdma(vals_sc.rearrange("p (b x) -> p b x", x=NCH * 32)[:, ds(b, 1), :], stag_vals[:])

            bd.wait("V", "sV")
            bd.v("tensor_reduce", tot_r[0:1, 0:4],
                 cnt_st[0:1, :].rearrange("p (c m) -> p m c", m=4), AX.X, ALU.add)
            bd.v("memset", base_r[:], 0.0)
            bd.wait("V", "sV")
            bd.v("tensor_scalar", th0_r[:], tot_r[:], float(-CAP), None, ALU.add)
            for c in range(NCH):
                bd.wait("V", "sV")
                bd.v("tensor_sub", th_r[:], th0_r[:], base_r[:])
                bd.wait("T", "sV")
                bd.mm(pf[3][:, 0:4], k_orf[:], th_r[0:1, 0:4], inc=1)
                bd.wait("V", "sT")
                bd.v("tensor_tensor", r_kept[:, c * 4:(c + 1) * 4], r_rank[:, c * 4:(c + 1) * 4],
                     pf[3][:, 0:4], ALU.is_ge)
                bd.wait("V", "sV")
                bd.v("tensor_mul", r_kept[:, c * 4:(c + 1) * 4], r_kept[:, c * 4:(c + 1) * 4],
                     r_sel[:, c * 4:(c + 1) * 4])
                bd.v("tensor_scalar", r_wm[:, c * 4:(c + 1) * 4], r_se1[:, c * 4:(c + 1) * 4],
                     r_rw[:, c * 2:c * 2 + 1], None, ALU.mult)
                bd.wait("V", "sV")
                bd.v("scalar_tensor_tensor", r_wm[:, c * 4:(c + 1) * 4], r_se2[:, c * 4:(c + 1) * 4],
                     r_rw[:, c * 2 + 1:c * 2 + 2], r_wm[:, c * 4:(c + 1) * 4], ALU.mult, ALU.add)
                bd.wait("V", "sV")
                bd.v("tensor_mul", r_wm[:, c * 4:(c + 1) * 4], r_wm[:, c * 4:(c + 1) * 4],
                     r_kept[:, c * 4:(c + 1) * 4])
                bd.v("tensor_add", base_r[:], base_r[:], cnt_st[0:1, c * 4:(c + 1) * 4])

            bd.wait("G", "sT")
            bd.gdma(wch[:].rearrange("p (dt k) -> p dt k", dt=8)[:, :, 0:512],
                    w_q.rearrange("(dt p) k -> p dt k", p=128))

            def q_chunk(c):
                bd.wait("V", "sT")
                bd.v("tensor_copy", xstat[:].rearrange("p (dt s) -> p dt s", dt=8),
                     xnT[:].rearrange("p (dt c s) -> p dt c s", dt=8, s=128)[:, :, ds(c, 1), :])
                bd.wait("T", "sG")
                bd.wait("T", "sV")
                for dt_ in range(8):
                    bd.mm(pf[6][:, 0:512], xstat[:, dt_ * 128:(dt_ + 1) * 128],
                          wch[:, dt_ * 2048:dt_ * 2048 + 512], start=(dt_ == 0),
                          stop=(dt_ == 7), inc=(1 if dt_ == 7 else 0))
                bd.wait("V", "sT")
                bd.v("tensor_copy", q_b[:].rearrange("p (c x) -> p c x", x=512)[:, ds(c, 1), :], pf[6][:, 0:512])

            bd.ploop(NCH, q_chunk)

            last_wch_use = [bd.mark("sT")]
            for ch in range(NCHAN):
                routed = ch < NM
                bd.wait("G", "sT", last_wch_use[0])
                bd.wait("G", "sG")
                wv4 = wch[:].rearrange("p (dt k) -> p dt k", dt=8)
                bd.gdma(wv4[:, :, 0:512], wk_src[ch].rearrange("(dt p) k -> p dt k", p=128), sem="sWA")
                bd.gdma(wv4[:, :, 512:1536], wv_src[ch].rearrange("(dt p) k -> p dt k", p=128), sem="sWB")
                bd.gdma(wv4[:, :, 1536:2048],
                        wgk_sc[:, ch * 8 * DK:(ch + 1) * 8 * DK].rearrange("p (dt k) -> p dt k", dt=8))
                wch_ld = (bd.mark("sWA"), bd.mark("sWB"), bd.mark("sG"))
                bd.wait("V", "sT")
                bd.wait("V", "sV")
                bd.v("memset", Sbf[:], 0.0)

                m = {"evac0": bd.mark("sV"), "evac1": bd.mark("sV"),
                     "sig": bd.mark("sS"), "lamS": bd.mark("sS"),
                     "gla_done0": bd.mark("sT"), "gla_done1": bd.mark("sT"),
                     "proj_done": {}, "oevac": bd.mark("sV"), "supd": bd.mark("sV"),
                     "evkg0": bd.mark("sV"), "evkg1": bd.mark("sV"),
                     "mask": bd.mark("sV"), "tcp": bd.mark("sV"), "kmm": {}, "gkmm": {},
                     "ds_ev": bd.mark("sS"), "supdA": bd.mark("sV")}

                def stage_xstat(c):
                    bd.wait("V", "sT", m["proj_done"].get(c - 1, 0))
                    bd.v("tensor_copy", xstat[:].rearrange("p (dt s) -> p dt s", dt=8),
                         xnT[:].rearrange("p (dt c s) -> p dt c s", dt=8, s=128)[:, :, ds(c, 1), :])
                    m["xstat"] = bd.mark("sV")

                def stage_gk_T(c):
                    bd.wait("T", "sWA", wch_ld[0])
                    bd.wait("T", "sWB", wch_ld[1])
                    bd.wait("T", "sG", wch_ld[2])
                    bd.wait("T", "sV", m["xstat"])
                    bd.wait("T", "sS", m["sig"])
                    for dt_ in range(8):
                        bd.mm(pf[0][:, 0:512], xstat[:, dt_ * 128:(dt_ + 1) * 128],
                              wch[:, dt_ * 2048 + 1536:(dt_ + 1) * 2048], start=(dt_ == 0), stop=False)
                    bd.mm(pf[0][:, 0:512], k_or[:], k_bgl[ch][:], start=False, stop=True, inc=1)
                    m["gkmm"][c] = bd.mark("sT")

                def stage_k_T(c):
                    bd.wait("T", "sV", m[f"evkg{1 - (c % 2)}"])
                    for dt_ in range(8):
                        bd.mm(pf[1][:, 0:512], xstat[:, dt_ * 128:(dt_ + 1) * 128],
                              wch[:, dt_ * 2048:dt_ * 2048 + 512], start=(dt_ == 0), stop=(dt_ == 7),
                              inc=(1 if dt_ == 7 else 0))
                    m["kmm"][c] = bd.mark("sT")

                def stage_v_T(c, half):
                    if half == 0:
                        bd.wait("T", "sV", m[f"evac{1 - (c % 2)}"])
                    for dt_ in range(8):
                        bd.mm(pf[2 + half][:, 0:512], xstat[:, dt_ * 128:(dt_ + 1) * 128],
                              wch[:, dt_ * 2048 + 512 + half * 512:dt_ * 2048 + 1024 + half * 512],
                              start=(dt_ == 0), stop=(dt_ == 7), inc=(1 if dt_ == 7 else 0))
                    if half == 1:
                        m["proj_done"][c] = bd.mark("sT")

                def stage_sig_S(c):
                    bd.wait("S", "sT", m["gkmm"][c])
                    bd.wait("S", "sS", m["sig"])
                    bd.wait("S", "sV", m[f"evkg{1 - (c % 2)}"])
                    bd.s("activation", sig1[:], pf[0][:, 0:512], AF.Sigmoid)
                    bd.wait("S", "sS")
                    bd.s("activation", sig2[:], sig1[:], AF.Ln)
                    m["sig"] = bd.mark("sS")

                def stage_evac_kg_V(c):
                    par = c % 2
                    kcb_, vcb_, gkcb_ = kgv[par]
                    bd.wait("V", "sS", m["sig"])
                    bd.wait("V", "sT", m["kmm"][c])
                    bd.wait("V", "sT", m[f"gla_done{par}"])
                    kpc = r_kept[:].rearrange("p (c m) -> p c m", m=4)[:, ds(c, 1), ch:ch + 1] if routed else None
                    if routed:
                        bd.v("tensor_scalar", gkcb_[:], sig2[:], kpc, None, ALU.mult)
                        bd.v("tensor_scalar", kcb_[:], pf[1][:, 0:512], kpc, None, ALU.mult, inc=1)
                    else:
                        bd.v("tensor_copy", gkcb_[:], sig2[:])
                        bd.v("tensor_copy", kcb_[:], pf[1][:, 0:512], inc=1)
                    m[f"evkg{par}"] = bd.mark("sV")

                def stage_evac_v_V(c):
                    par = c % 2
                    kcb_, vcb_, gkcb_ = kgv[par]
                    bd.wait("V", "sT", m["proj_done"][c])
                    bd.v("tensor_copy", vcb_[:, 0:512], pf[2][:, 0:512])
                    bd.v("tensor_copy", vcb_[:, 512:1024], pf[3][:, 0:512], inc=1)
                    m[f"evac{par}"] = bd.mark("sV")

                def stage_gla(c):
                    par = c % 2
                    kcb_, vcb_, gkcb_ = kgv[par]
                    have_next = c + 1 < NCH
                    bd.wait("T", "sV", m[f"evkg{par}"])
                    bd.wait("T", "sV", m[f"evac{par}"])
                    bd.wait("T", "sS", m["lamS"])
                    bd.wait("T", "sS", m["ds_ev"])
                    bd.mm(pf[4][:, 0:512], k_t16[:], gkcb_[:], inc=1)
                    for hh in range(H):
                        bd.mm(pf[5][:, hh:hh + 1], gkcb_[:, hh * 128:(hh + 1) * 128], k_ov16[:],
                              inc=(1 if hh == 3 else 0))
                    gmm = bd.mark("sT")
                    bd.wait("S", "sT", gmm)
                    bd.wait("S", "sV", m["supd"])
                    bd.s("activation", lam[:], pf[4][:, 0:512], AF.Exp)
                    bd.s("activation", nlam[:], pf[4][:, 0:512], AF.Exp, scale=-1.0)
                    bd.s("activation", decay[:], pf[5][:, 0:4], AF.Exp)
                    m["lamS"] = bd.mark("sS")
                    if have_next:
                        stage_xstat(c + 1)
                        stage_gk_T(c + 1)
                        stage_sig_S(c + 1)
                    bd.wait("V", "sS", m["lamS"])
                    bd.wait("V", "sV", m["tcp"])
                    bd.wait("V", "sT", m[f"gla_done{1 - (c % 2)}"])
                    if routed:
                        bd.v("scalar_tensor_tensor", qt[:], q_b[:].rearrange("p (c x) -> p c x", x=512)[:, ds(c, 1), :],
                             r_wm[:].rearrange("p (c m) -> p c m", m=4)[:, ds(c, 1), ch:ch + 1],
                             lam[:], ALU.mult, ALU.mult)
                    else:
                        bd.v("scalar_tensor_tensor", qt[:], q_b[:].rearrange("p (c x) -> p c x", x=512)[:, ds(c, 1), :],
                             SCALE, lam[:], ALU.mult, ALU.mult)
                    bd.v("tensor_tensor", kt[:], kcb_[:], nlam[:], ALU.mult, inc=1)
                    m["qtkt"] = bd.mark("sV")
                    bd.wait("T", "sV", m["qtkt"])
                    for hh in range(H):
                        bd.tr(pb[:, hh * 128:(hh + 1) * 128], qt[:, hh * 128:(hh + 1) * 128], k_id[:])
                        bd.tr(pb[:, 512 + hh * 128:512 + (hh + 1) * 128], kt[:, hh * 128:(hh + 1) * 128],
                              k_id[:], inc=(1 if hh == 3 else 0))
                    trm = bd.mark("sT")
                    if have_next:
                        stage_k_T(c + 1)
                    bd.wait("V", "sT", trm)
                    bd.v("tensor_copy", qtT[:], pb[:, 0:512])
                    bd.v("tensor_copy", ktT[:], pb[:, 512:1024], inc=1)
                    m["tcp"] = bd.mark("sV")
                    bd.wait("T", "sV", m["tcp"])
                    bd.wait("T", "sV", m["oevac"])
                    for hh in range(H):
                        bd.mm(pf[6][:, hh * 128:(hh + 1) * 128], ktT[:, hh * 128:(hh + 1) * 128],
                              qtT[:, hh * 128:(hh + 1) * 128], inc=(1 if hh == 3 else 0))
                    atm = bd.mark("sT")
                    if have_next:
                        stage_v_T(c + 1, 0)
                        stage_evac_kg_V(c + 1)
                    bd.wait("V", "sT", atm)
                    bd.v("tensor_tensor", att[:], pf[6][:, 0:512], k_m4[:], ALU.mult, inc=1)
                    m["mask"] = bd.mark("sV")
                    par_s = c % 2
                    scur = Sbf[:, par_s * 1024:(par_s + 1) * 1024]
                    snew = Sbf[:, (1 - par_s) * 1024:(2 - par_s) * 1024]
                    bd.wait("T", "sV", m["mask"])
                    bd.wait("T", "sS", m["lamS"])
                    for hh in range(2):
                        bd.mm(pf[6][:, hh * 256:(hh + 1) * 256], att[:, hh * 128:(hh + 1) * 128],
                              vcb_[:, hh * 256:(hh + 1) * 256], start=True, stop=False)
                        bd.mm(pf[6][:, hh * 256:(hh + 1) * 256], qtT[:, hh * 128:(hh + 1) * 128],
                              scur[:, hh * 256:(hh + 1) * 256], start=False, stop=True,
                              inc=(1 if hh == 1 else 0))
                    o01 = bd.mark("sT")
                    for hh in range(2):
                        bd.mm(pf[4][:, hh * 256:(hh + 1) * 256], kt[:, hh * 128:(hh + 1) * 128],
                              vcb_[:, hh * 256:(hh + 1) * 256], inc=(1 if hh == 1 else 0))
                    dsa = bd.mark("sT")
                    if have_next:
                        stage_v_T(c + 1, 1)
                    bd.wait("V", "sT", o01)
                    oview = oac[:].rearrange("p (c h x) -> p c h x", h=2, x=512)
                    if ch == 0:
                        bd.v("tensor_copy", oview[:, ds(c, 1), 0, :], pf[6][:, 0:512], inc=1)
                    else:
                        bd.v("tensor_tensor", oview[:, ds(c, 1), 0, :], pf[6][:, 0:512],
                             oview[:, ds(c, 1), 0, :], ALU.add, inc=1)
                    m["oevac01"] = bd.mark("sV")
                    bd.wait("T", "sV", m["oevac01"])
                    for hh in range(2, 4):
                        bd.mm(pf[6][:, (hh - 2) * 256:(hh - 1) * 256], att[:, hh * 128:(hh + 1) * 128],
                              vcb_[:, hh * 256:(hh + 1) * 256], start=True, stop=False)
                        bd.mm(pf[6][:, (hh - 2) * 256:(hh - 1) * 256], qtT[:, hh * 128:(hh + 1) * 128],
                              scur[:, hh * 256:(hh + 1) * 256], start=False, stop=True,
                              inc=(1 if hh == 3 else 0))
                    o23 = bd.mark("sT")
                    for hh in range(2, 4):
                        bd.mm(pf[5][:, (hh - 2) * 256:(hh - 1) * 256], kt[:, hh * 128:(hh + 1) * 128],
                              vcb_[:, hh * 256:(hh + 1) * 256], inc=(1 if hh == 3 else 0))
                    dsb = bd.mark("sT")
                    m[f"gla_done{par}"] = bd.mark("sT")
                    bd.wait("S", "sT", dsa)
                    bd.wait("S", "sV", m["supdA"])
                    bd.s("copy", t1s[:], pf[4][:, 0:512])
                    ds_ev1 = bd.mark("sS")
                    bd.wait("V", "sS", ds_ev1)
                    bd.v("tensor_add", t1[:], t1s[:], scur[:, 0:512])
                    bd.wait("V", "sV")
                    for hh in range(2):
                        bd.v("tensor_scalar", snew[:, hh * 256:(hh + 1) * 256],
                             t1[:, hh * 256:(hh + 1) * 256], decay[:, hh:hh + 1], None, ALU.mult)
                    bd.wait("V", "sT", o23)
                    if ch == 0:
                        bd.v("tensor_copy", oview[:, ds(c, 1), 1, :], pf[6][:, 0:512], inc=1)
                    else:
                        bd.v("tensor_tensor", oview[:, ds(c, 1), 1, :], pf[6][:, 0:512],
                             oview[:, ds(c, 1), 1, :], ALU.add, inc=1)
                    m["oevac"] = bd.mark("sV")
                    if have_next:
                        stage_evac_v_V(c + 1)
                    bd.wait("S", "sT", dsb)
                    bd.s("copy", t1bs[:], pf[5][:, 0:512])
                    m["ds_ev"] = bd.mark("sS")
                    bd.wait("V", "sS", m["ds_ev"])
                    bd.v("tensor_add", t1b[:], t1bs[:], scur[:, 512:1024])
                    bd.wait("V", "sV")
                    for hh in range(2, 4):
                        bd.v("tensor_scalar", snew[:, hh * 256:(hh + 1) * 256],
                             t1b[:, (hh - 2) * 256:(hh - 1) * 256], decay[:, hh:hh + 1], None, ALU.mult,
                             inc=(1 if hh == 3 else 0))
                    m["supd"] = bd.mark("sV")
                    m["supdA"] = m["supd"]

                stage_xstat(0)
                stage_gk_T(0)
                stage_sig_S(0)
                stage_k_T(0)
                stage_evac_kg_V(0)
                stage_v_T(0, 0)
                stage_v_T(0, 1)
                stage_evac_v_V(0)
                for c in range(NCH):
                    stage_gla(c)
                last_wch_use[0] = m["proj_done"][NCH - 1]

            bd.wait("G", "sV")
            bd.gdma(oacc_sc.rearrange("p (b x) -> p b x", x=NCH * DV)[:, ds(b, 1), :], oac[:])

        bd.loop("AB_batch", B, phase_ab)

    def describe_p2(bd: Bd):
        xnT2 = p2v["xnT2"]; gwb = p2v["gwb"]; uwb = p2v["uwb"]
        dwb = p2v["dwb"]; yg = p2v["yg"]; sgt = p2v["sgt"]; sgt2 = p2v["sgt2"]; facc = p2v["facc"]
        oat = p2v["oat"]; x_t = p2v["x_t"]; sq_t = p2v["sq_t"]; ssum = p2v["ssum"]
        rstd4 = p2v["rstd4"]; on_t = p2v["on_t"]
        og_t = p2v["on_t"]; ogT = p2v["ogT"]; outsb = p2v["outsb"]
        dispsb = p2v["dispsb"]; widx1 = p2v["widx1"]; widx2 = p2v["widx2"]; eidb = p2v["eidb"]
        cnt64sb = p2v["cnt64sb"]; fin16 = p2v["fin16"]; fb2 = p2v["fb2"]
        p8sb = p2v["p8sb"]; pbs = p2v["pbs"]; fbTsb = p2v["fbTsb"]; fbrow = p2v["fbrow"]
        eidu = p2v["eidu"]; eidf = p2v["eidf"]
        ts512 = p2v["ts512"]; fin8b = p2v["fin8b"]; posf = p2v["posf"]
        posu = p2v["posu"]; rankm_sb = p2v["rankm_sb"]; valsb = p2v["valsb"]
        pattbig = p2v["pattbig"]

        pbv = pattbig[:].rearrange("p (r v) -> p r v", v=4)
        bd.wait("V", "sG")
        bd.wait("V", "sY")
        bd.wait("S", "sG")
        bd.wait("S", "sY")
        bd.wait("V", "sV")
        bd.v("memset", pbv[:, :, 0:1], float(T_TOK))
        bd.v("memset", pbv[:, :, 1:2], 0.0)
        bd.v("memset", pbv[:, :, 2:3], BIG)
        bd.v("memset", pbv[:, :, 3:4], 0.0)
        bd.wait("Y", "sV")
        bd.ydma(disp.rearrange("(p r) v -> p (r v)", p=128), pattbig[:])
        bd.ydma(cnt64sb[:], cnt_sc)
        bd.ydma(rankm_sb[:], rank_sc)
        bd.ydma(valsb[:], vals_sc)
        bd.wait("T", "sY")
        bd.mm(pf[0][0:NE, 0:NGC], cnt64sb[:], k_tS[0:NGC, 0:NGC], inc=1)
        bd.mm(pf[1][0:NE, 0:1], cnt64sb[:], k_ocf[0:NGC, 0:1], inc=1)
        bd.wait("V", "sT")
        bd.v("tensor_copy", pbs[:], pf[1][0:NE, 0:1])
        bd.wait("V", "sV")
        bd.v("tensor_scalar", fin16[0:NE, :], k_th[0:NE, :], pbs[:, 0:1], None, ALU.is_lt)
        bd.wait("V", "sV")
        bd.v("tensor_reduce", p8sb[:], fin16[0:NE, 0:16], AX.X, ALU.add)
        bd.wait("V", "sV")
        bd.v("tensor_scalar", p8sb[:], p8sb[:], 512.0, None, ALU.mult)
        bd.wait("T", "sV")
        bd.mm(pf[2][0:NE, 0:1], k_tS[0:NE, 0:NE], p8sb[:], inc=1)
        bd.wait("V", "sT")
        bd.v("tensor_copy", pbs[:], pf[2][0:NE, 0:1])
        bd.wait("V", "sV")
        bd.v("tensor_scalar", fb2[:, 0:NGC], pf[0][0:NE, 0:NGC], pbs[:, 0:1], None, ALU.add)
        bd.wait("V", "sV")
        bd.v("tensor_scalar", fb2[:, NGC:NGC + 1], p8sb[:], pbs[:, 0:1], None, ALU.add)
        bd.wait("T", "sV")
        bd.tr(pf[3][0:NGC + 1, 0:NE], fb2[0:NE, 0:NGC + 1], k_idf[0:NE, 0:NE], inc=1)
        bd.wait("V", "sT")
        bd.v("tensor_copy", fbTsb[:], pf[3][0:NGC + 1, 0:NE])
        bd.wait("Y", "sV")
        bd.ydma(fb_sc.rearrange("o (g e) -> (o g) e", e=NE), fbTsb[:])
        bd.ydma(fbrow[:], fb_sc)
        bd.wait("T", "sY")
        bd.mm(pf[4][:, 0:NE], k_orf[:], fbrow[0:1, NGC * NE:(NGC + 1) * NE], inc=1)
        bd.wait("V", "sV")
        bd.v("tensor_scalar", ts512[:], k_pidx[:], 512.0, None, ALU.mult)
        bd.wait("V", "sT")
        bd.wait("V", "sV")
        bd.v("tensor_scalar", fin8b[:], pf[4][:, 0:NE], ts512[:, 0:1], None, ALU.is_le)
        bd.wait("V", "sV")
        bd.v("tensor_reduce", eidf[:], fin8b[:], AX.X, ALU.add)
        bd.wait("V", "sV")
        bd.v("tensor_scalar", eidf[:], eidf[:], 7.0, None, ALU.min)
        bd.wait("V", "sV")
        bd.v("tensor_copy", eidu[:], eidf[:])
        bd.wait("Y", "sV")
        bd.ydma(eidf_sc.rearrange("o (t x) -> (o t) x", x=1), eidf[0:NTILE, 0:1])
        bd.ydma(p2v["eidrowF"][:], eidf_sc)
        bd.wait("T", "sY")
        bd.mm(pf[4][:, 0:NTILE], k_orf[:], p2v["eidrowF"][0:1, 0:NTILE], inc=1)
        bd.wait("V", "sT")
        bd.v("tensor_copy", eidb[:], pf[4][:, 0:NTILE])
        bd.wait("V", "sV")
        bd.v("tensor_scalar", p2v["widxf"][:], eidb[:], 1024.0, k_pidx[:, 0:1], ALU.mult, ALU.add)
        bd.wait("V", "sV")
        bd.v("tensor_copy", widx1[:], p2v["widxf"][:])
        bd.wait("V", "sV")
        bd.v("tensor_scalar", p2v["widxf"][:], eidb[:], 2048.0, k_pidx[:, 0:1], ALU.mult, ALU.add)
        bd.wait("V", "sV")
        bd.v("tensor_copy", widx2[:], p2v["widxf"][:])
        for b_ in range(1, B):
            vbv = valsb[:].rearrange("p (b x v) -> p b x v", b=B, v=4)
            bd.wait("V", "sV")
            bd.v("tensor_scalar", vbv[:, b_, :, 0:1], vbv[:, b_, :, 0:1],
                 float(b_ * S), None, ALU.add)
            bd.wait("V", "sV")
            bd.v("tensor_scalar", vbv[:, b_, :, 2:3], vbv[:, b_, :, 2:3],
                 float(2 * b_ * S), None, ALU.add)
        pos_m = {}
        for g in range(NGC):
            bk = pf[5] if g % 2 == 0 else pf[6]
            if g >= 2:
                bd.wait("T", "sV", pos_m[g - 2])
            else:
                bd.wait("T", "sV")
            bd.mm(bk[:, 0:NE], k_orf[:], fbrow[0:1, g * NE:(g + 1) * NE], inc=1)
            bd.wait("V", "sT")
            bd.v("tensor_tensor", posf[:], bk[:, 0:NE], rankm_sb[:, g * 8:(g + 1) * 8], ALU.add)
            bd.wait("V", "sV")
            bd.v("tensor_copy", posu[:, g * 8:(g + 1) * 8], posf[:])
            pos_m[g] = bd.mark("sV")
        bd.wait("G", "sY")
        bd.wait("G", "sV")
        for g in range(NGC):
            for e in range(NE):
                bd.gind(disp, posu[:, g * 8 + e:g * 8 + e + 1],
                        valsb[:, g * 32 + e * 4:g * 32 + e * 4 + 4], None, NSLOT - 1)
        disp_scat_done = bd.mark("sGD")

        def phase_shared(b):
            bd.wait("S", "sT")
            bd.sdma(xnT2[:], xnT_sc.rearrange("p (b x) -> p b x", x=SP8)[:, ds(b, 1), :],
                    sem="sGC")
            xld = bd.mark("sGC")
            bd.wait("T", "sGC", xld)
            gT, uT, dT = fsh_gT, fsh_uT, fsh_dT
            psem = ["sWA", "sWB"]
            wa0 = bd.mark("sWA")
            wb0 = bd.mark("sWB")
            mk = {"dwb_use": bd.mark("sT"), "guse0": bd.mark("sT"),
                  "guse1": bd.mark("sT"), "down": bd.mark("sT"),
                  "silu0": bd.mark("sS"), "silu1": bd.mark("sS"),
                  "mult0": bd.mark("sV"), "mult1": bd.mark("sV"),
                  "evac": bd.mark("sV"), "evac0d": bd.mark("sV"),
                  "evac1d": bd.mark("sV"), "gu_par": 0, "dn_par": 0}
            bd.wait("S", "sT", mk["dwb_use"])
            bd.sdma(dwb[:].rearrange("p (it h) -> p it h", it=16),
                    dT.rearrange("(it p) h -> p it h", p=128), sem="sGC")
            mk["dwb_ld"] = bd.mark("sGC")
            bd.wait("V", "sGC")
            for qr in range(4):
                for g8 in range(2):
                    par = g8 % 2
                    sw = psem[par]
                    if qr == 0:
                        ld_mark = wa0 if par == 0 else wb0
                    else:
                        bd.wait("Y", "sT", mk.get(f"guse{par}", 0))
                        bd.ydma(gwb[:, par * 2048:(par + 1) * 2048].rearrange("p (dt i) -> p dt i", dt=8),
                                gT[:, qr * 512 + g8 * 256: qr * 512 + (g8 + 1) * 256]
                                .rearrange("(dt p) i -> p dt i", p=128), sem=sw)
                        bd.ydma(uwb[:, par * 2048:(par + 1) * 2048].rearrange("p (dt i) -> p dt i", dt=8),
                                uT[:, qr * 512 + g8 * 256: qr * 512 + (g8 + 1) * 256]
                                .rearrange("(dt p) i -> p dt i", p=128), sem=sw)
                        ld_mark = bd.mark(sw)
                    for il in range(2):
                        islq = g8 * 2 + il
                        for tau in range(NT):
                            gp = mk.setdefault("gu_par", 0)
                            pg, pu = (pf[0], pf[1]) if gp == 0 else (pf[2], pf[3])
                            sgp = sgt[:, 0:512] if gp == 0 else sgt2[:, 0:512]
                            bd.wait("T", sw, ld_mark)
                            bd.wait("T", "sS", mk.get(f"silu{gp}", 0))
                            bd.wait("T", "sV", mk.get(f"mult{gp}", 0))
                            if gp == 1:
                                bd.wait("T", "sV", mk.get("evac", 0))
                            for dt_ in range(8):
                                bd.mm(pg[:, 0:512],
                                      gwb[:, par * 2048 + dt_ * 256 + il * 128:
                                          par * 2048 + dt_ * 256 + (il + 1) * 128],
                                      xnT2[:, dt_ * S + tau * 512: dt_ * S + (tau + 1) * 512],
                                      start=(dt_ == 0), stop=(dt_ == 7),
                                      inc=(1 if dt_ == 7 else 0))
                            gate_mark = bd.mark("sT")
                            for dt_ in range(8):
                                bd.mm(pu[:, 0:512],
                                      uwb[:, par * 2048 + dt_ * 256 + il * 128:
                                          par * 2048 + dt_ * 256 + (il + 1) * 128],
                                      xnT2[:, dt_ * S + tau * 512: dt_ * S + (tau + 1) * 512],
                                      start=(dt_ == 0), stop=(dt_ == 7),
                                      inc=(1 if dt_ == 7 else 0))
                            mk[f"guse{par}"] = bd.mark("sT")
                            bd.wait("S", "sT", gate_mark)
                            bd.wait("S", "sV", mk.get(f"mult{gp}", 0))
                            bd.s("activation", sgp, pg[:, 0:512], AF.Silu)
                            mk[f"silu{gp}"] = bd.mark("sS")
                            bd.wait("V", "sS", mk[f"silu{gp}"])
                            bd.wait("V", "sT", mk[f"guse{par}"])
                            bd.wait("V", "sT", mk.get("down", 0))
                            bd.v("tensor_tensor", yg[:, (islq * NT + tau) * 512:(islq * NT + tau + 1) * 512],
                                 sgp, pu[:, 0:512], ALU.mult)
                            mk[f"mult{gp}"] = bd.mark("sV")
                            if g8 == 1 and il == 1:
                                mk[f"multtau{tau}"] = bd.mark("sV")
                            mk["gu_par"] = 1 - gp
                bd.wait("T", "sGC", mk["dwb_ld"])
                first_dn = [True]
                for hh2 in range(2):
                    for tau in range(NT):
                        bd.wait("T", "sV", mk.get(f"multtau{tau}", 0))
                        for tp in range(2):
                            dp = mk.setdefault("dn_par", 0)
                            bank = 2 if dp == 0 else 4
                            if first_dn[0] and dp == 0:
                                bd.wait("T", "sS", mk.get("silu1", 0))
                                bd.wait("T", "sV", mk.get("mult1", 0))
                                first_dn[0] = False
                            bd.wait("T", "sV", mk.get(f"evac{dp}d", 0))
                            for islq in range(4):
                                for t2 in range(2):
                                    ts_ = tp * 2 + t2
                                    isl = qr * 4 + islq
                                    bd.mm(pf[bank + t2][:, 0:512],
                                          yg[:, (islq * NT + tau) * 512 + ts_ * 128:
                                             (islq * NT + tau) * 512 + (ts_ + 1) * 128],
                                          dwb[:, isl * 1024 + hh2 * 512:isl * 1024 + (hh2 + 1) * 512],
                                          start=(islq == 0), stop=(islq == 3),
                                          inc=(1 if (islq == 3 and t2 == 1) else 0))
                            mk["down"] = bd.mark("sT")
                            bd.wait("V", "sT", mk["down"])
                            for t2 in range(2):
                                ts_ = tp * 2 + t2
                                dst = facc[:, (tau * 4 + ts_) * 1024 + hh2 * 512:
                                            (tau * 4 + ts_) * 1024 + (hh2 + 1) * 512]
                                if qr == 0:
                                    bd.v("tensor_copy", dst, pf[bank + t2][:, 0:512],
                                         inc=(1 if t2 == 1 else 0))
                                else:
                                    bd.v("tensor_tensor", dst, pf[bank + t2][:, 0:512], dst, ALU.add,
                                         inc=(1 if t2 == 1 else 0))
                            mk[f"evac{dp}d"] = bd.mark("sV")
                            mk["evac"] = mk[f"evac{dp}d"]
                            mk["dn_par"] = 1 - dp
                mk["dwb_use"] = mk["down"]
                if qr == 3:
                    for g8 in range(2):
                        sw = psem[g8 % 2]
                        bd.wait("Y", "sT", mk.get(f"guse{g8 % 2}", 0))
                        bd.ydma(gwb[:, (g8 % 2) * 2048:(g8 % 2 + 1) * 2048].rearrange("p (dt i) -> p dt i", dt=8),
                                fsh_gT[:, g8 * 256:(g8 + 1) * 256].rearrange("(dt p) i -> p dt i", p=128),
                                sem=sw)
                        bd.ydma(uwb[:, (g8 % 2) * 2048:(g8 % 2 + 1) * 2048].rearrange("p (dt i) -> p dt i", dt=8),
                                fsh_uT[:, g8 * 256:(g8 + 1) * 256].rearrange("(dt p) i -> p dt i", p=128),
                                sem=sw)
            bd.wait("S", "sV", mk["evac"])
            bd.sdma(facc_sc.rearrange("p (b x) -> p b x", x=NCH * DV)[:, ds(b, 1), :], facc[:],
                    sem="sGC")
            bd.wait("S", "sGC")

        for g8 in range(2):
            sw = "sWA" if g8 == 0 else "sWB"
            bd.wait("Y", "sT")
            bd.ydma(gwb[:, g8 * 2048:(g8 + 1) * 2048].rearrange("p (dt i) -> p dt i", dt=8),
                    fsh_gT[:, g8 * 256:(g8 + 1) * 256].rearrange("(dt p) i -> p dt i", p=128),
                    sem=sw)
            bd.ydma(uwb[:, g8 * 2048:(g8 + 1) * 2048].rearrange("p (dt i) -> p dt i", dt=8),
                    fsh_uT[:, g8 * 256:(g8 + 1) * 256].rearrange("(dt p) i -> p dt i", p=128),
                    sem=sw)

        bd.loop("P2_shared", B, phase_shared)

        XG_OF = [0, 4096]
        XGT_OF = 8192
        CMB_OF = 12288
        dispv = dispsb
        tm = {"cast": {}, "gath": {}, "trd": {}, "dnd": {}, "dwb_ld": {}, "dsb": {},
              "wdone": {}, "scatlast": {}, "guall": {}, "wg_ld": {}}
        gu = {"par": 0, "silu0": bd.mark("sS"), "silu1": bd.mark("sS"),
              "mult0": bd.mark("sV"), "mult1": bd.mark("sV"),
              "guse0": bd.mark("sT"), "guse1": bd.mark("sT")}
        dnm = {"par": 0, "ev0": bd.mark("sV"), "ev1": bd.mark("sV"),
               "down": bd.mark("sT"), "scat0": bd.mark("sGD"), "scat1": bd.mark("sGD")}
        bd.wait("G", "sGD")
        bd.wait("G", "sT")
        bd.wait("G", "sY")
        bd.wait("G", "sGC")
        bd.wait("S", "sY")

        fgate_rows = fgate.rearrange("e d i -> (e d) i")
        fup_rows = fup.rearrange("e d i -> (e d) i")
        fdown_rows = fdown.rearrange("e i h -> (e i) h")

        def tile_fetch(t):
            if t >= NTILE:
                return
            par = t % 2
            bd.wait("G", "sV", tm["wdone"].get(t - 2, 0))
            bd.gdma(dispv[:, par * 16:(par + 1) * 16],
                    disp[t * 512:(t + 1) * 512, :].rearrange("(p r) v -> p (r v)", p=128))
            tm["dsb"][t] = bd.mark("sG")
            bd.wait("V", "sG", tm["dsb"][t])
            bd.wait("V", "sGD", tm["scatlast"].get(t - 2, disp_scat_done))
            dvv = dispv[:, par * 16:(par + 1) * 16].rearrange("p (r v) -> p r v", v=4)
            bd.wait("V", "sV")
            bd.v("tensor_copy", posu[:, par * 16:par * 16 + 4], dvv[:, :, 0])
            bd.v("tensor_copy", posu[:, par * 16 + 4:par * 16 + 8], dvv[:, :, 2])
            tm["cast"][t] = bd.mark("sV")
            bd.wait("G", "sV", tm["cast"][t])
            bd.wait("G", "sT", tm["trd"].get(t - 2, 0))
            for r in range(4):
                bd.gind(xnT2[:, XG_OF[par] + r * 1024:XG_OF[par] + (r + 1) * 1024], None,
                        xn_tok, posu[:, par * 16 + r:par * 16 + r + 1], T_TOK)
            tm["gath"][t] = bd.mark("sGD")

        def wg_fetch(t):
            if t >= NTILE:
                return
            bd.wait("G", "sT", tm["guall"].get(t - 1, 0))
            for j in range(8):
                bd.gind(gwb[:, j * FI:(j + 1) * FI], None, fgate_rows,
                        widx1[:, t:t + 1], NE * 1024 - 1, element_offset=j * 128 * FI)
            for j in range(8):
                bd.gind(uwb[:, j * FI:(j + 1) * FI], None, fup_rows,
                        widx1[:, t:t + 1], NE * 1024 - 1, element_offset=j * 128 * FI)
            tm["wg_ld"][t] = bd.mark("sGD")

        def dwb_fetch(t):
            if t >= NTILE:
                return
            bd.wait("G", "sT", tm["dnd"].get(t - 1, 0))
            for j in range(16):
                bd.gind(dwb[:, j * 1024:(j + 1) * 1024], None, fdown_rows,
                        widx2[:, t:t + 1], NE * FI - 1, element_offset=j * 128 * 1024)
            tm["dwb_ld"][t] = bd.mark("sGD")

        def tile_compute(t):
            par = t % 2
            prev_ev = 0
            for r in range(4):
                bd.wait("T", "sGD", tm["gath"][t])
                bd.wait("T", "sV", prev_ev)
                for dt_ in range(8):
                    bd.tr(pb[:, dt_ * 128:(dt_ + 1) * 128],
                          xnT2[:, XG_OF[par] + r * 1024 + dt_ * 128:XG_OF[par] + r * 1024 + (dt_ + 1) * 128],
                          k_id[:], inc=(1 if dt_ == 7 else 0))
                trm = bd.mark("sT")
                bd.wait("V", "sT", trm)
                for dt_ in range(8):
                    bd.v("tensor_copy",
                         xnT2[:, XGT_OF + dt_ * 512 + r * 128:XGT_OF + dt_ * 512 + (r + 1) * 128],
                         pb[:, dt_ * 128:(dt_ + 1) * 128], inc=(1 if dt_ == 7 else 0))
                prev_ev = bd.mark("sV")
            tm["trd"][t] = bd.mark("sT")
            xgev_all = prev_ev
            for qr in range(4):
                for g8 in range(2):
                    qg = qr * 2 + g8
                    for il in range(2):
                        isl = qr * 4 + g8 * 2 + il
                        gp = gu["par"]
                        pg, pu = (pf[0], pf[1]) if gp == 0 else (pf[2], pf[3])
                        sgp = sgt[:, 0:512] if gp == 0 else sgt2[:, 0:512]
                        bd.wait("T", "sGD", tm["wg_ld"][t])
                        bd.wait("T", "sV", xgev_all)
                        bd.wait("T", "sS", gu[f"silu{gp}"])
                        bd.wait("T", "sV", gu[f"mult{gp}"])
                        for dt_ in range(8):
                            bd.mm(pg[:, 0:512],
                                  gwb[:, dt_ * FI + qg * 256 + il * 128:
                                      dt_ * FI + qg * 256 + (il + 1) * 128],
                                  xnT2[:, XGT_OF + dt_ * 512:XGT_OF + (dt_ + 1) * 512],
                                  start=(dt_ == 0), stop=(dt_ == 7), inc=(1 if dt_ == 7 else 0))
                        gate_mark = bd.mark("sT")
                        for dt_ in range(8):
                            bd.mm(pu[:, 0:512],
                                  uwb[:, dt_ * FI + qg * 256 + il * 128:
                                      dt_ * FI + qg * 256 + (il + 1) * 128],
                                  xnT2[:, XGT_OF + dt_ * 512:XGT_OF + (dt_ + 1) * 512],
                                  start=(dt_ == 0), stop=(dt_ == 7), inc=(1 if dt_ == 7 else 0))
                        bd.wait("S", "sT", gate_mark)
                        bd.wait("S", "sV", gu[f"mult{gp}"])
                        bd.s("activation", sgp, pg[:, 0:512], AF.Silu)
                        gu[f"silu{gp}"] = bd.mark("sS")
                        bd.wait("V", "sS", gu[f"silu{gp}"])
                        bd.wait("V", "sT", bd.mark("sT"))
                        bd.wait("V", "sT", dnm["down"])
                        bd.v("tensor_tensor", yg[:, isl * 512:(isl + 1) * 512], sgp, pu[:, 0:512],
                             ALU.mult)
                        gu[f"mult{gp}"] = bd.mark("sV")
                        gu["par"] = 1 - gp
            tm["guall"][t] = bd.mark("sT")
            wg_fetch(t + 1)
            yg_done = bd.mark("sV")
            bd.wait("T", "sGD", tm["dwb_ld"][t])
            bd.wait("T", "sV", yg_done)
            for q in range(4):
                cpar = q % 2
                for hh2 in range(2):
                    dp = dnm["par"]
                    bank = pf[4] if dp == 0 else pf[5]
                    bd.wait("T", "sV", dnm[f"ev{dp}"])
                    for isl in range(16):
                        bd.mm(bank[:, 0:512],
                              yg[:, isl * 512 + q * 128:isl * 512 + (q + 1) * 128],
                              dwb[:, isl * 1024 + hh2 * 512:isl * 1024 + (hh2 + 1) * 512],
                              start=(isl == 0), stop=(isl == 15), inc=(1 if isl == 15 else 0))
                    dnm["down"] = bd.mark("sT")
                    bd.wait("V", "sT", dnm["down"])
                    bd.wait("V", "sGD", dnm[f"scat{cpar}"])
                    wc = dispv[:, par * 16 + q * 4 + 1:par * 16 + q * 4 + 2]
                    bd.v("tensor_scalar",
                         xnT2[:, CMB_OF + cpar * 1024 + hh2 * 512:CMB_OF + cpar * 1024 + (hh2 + 1) * 512],
                         bank[:, 0:512], wc, None, ALU.mult)
                    dnm[f"ev{dp}"] = bd.mark("sV")
                    dnm["par"] = 1 - dp
                bd.wait("G", "sV", dnm[f"ev{1 - dnm['par']}"])
                bd.gind(comb, posu[:, par * 16 + 4 + q:par * 16 + 5 + q],
                        xnT2[:, CMB_OF + cpar * 1024:CMB_OF + (cpar + 1) * 1024], None,
                        2 * T_TOK - 1)
                dnm[f"scat{cpar}"] = bd.mark("sGD")
            tm["scatlast"][t] = bd.mark("sGD")
            tm["wdone"][t] = dnm[f"ev{1 - dnm['par']}"]
            tm["dnd"][t] = bd.mark("sT")
            dwb_fetch(t + 1)

        bd.wait("G", "sV")
        tile_fetch(0)
        wg_fetch(0)
        dwb_fetch(0)
        tile_fetch(1)
        for t in range(NTILE):
            tile_compute(t)
            tile_fetch(t + 2)

        gwbF2 = gwb.bitcast(F32)
        uwbF2 = uwb.bitcast(F32)
        tl = {
            "outsb": [gwbF2[:, 0:1024], gwbF2[:, 1024:2048]],
            "x": [gwbF2[:, 2048:3072], gwbF2[:, 3072:4096]],
            "oat": [gwb[:, 8192:9216], gwb[:, 9216:10240]],
            "xst": [gwb[:, 10240:11264], gwb[:, 11264:12288]],
            "on": [gwb[:, 12288:13312], gwb[:, 13312:14336]],
            "og": [gwb[:, 14336:15360], gwb[:, 15360:16384]],
            "facc": [uwb[:, 0:1024], uwb[:, 1024:2048]],
            "comb": [uwb[:, 2048:4096], uwb[:, 4096:6144]],
            "ogT": [uwb[:, 6144:7168], uwb[:, 7168:8192]],
            "sq": [uwbF2[:, 4096:4352], uwbF2[:, 4352:4608]],
            "ssum": [uwbF2[:, 4608:4612], uwbF2[:, 4612:4616]],
            "rstd": [uwbF2[:, 4616:4620], uwbF2[:, 4620:4624]],
            "sgt": [sgt[:, 0:512], sgt2[:, 0:512]],
        }
        bd.wait("G", "sT")
        bd.wait("G", "sGD")
        bd.gdma(dwb[:, 0:8 * 1024].rearrange("p (dt v) -> p dt v", dt=8),
                w_g.rearrange("(dt p) v -> p dt v", p=128))
        bd.gdma(dwb[:, 8 * 1024:16 * 1024].rearrange("p (vt d) -> p vt d", vt=8),
                w_o.rearrange("(vt p) d -> p vt d", p=128))
        wgwo_ld = bd.mark("sG")
        tmk = {"yld": {}, "gld": {}, "use": {}, "gmm": {}, "womm": {}, "silu": {},
               "oguse": {}, "outw": {}}
        xnv = xnT_sc.rearrange("p (b dt s k) -> p b dt s k", b=B, dt=8, s=NCH)
        oav = oacc_sc.rearrange("p (b s v) -> p b s v", b=B, s=NCH)
        fav = facc_sc.rearrange("p (b s v) -> p b s v", b=B, s=NCH)
        cmv = comb.rearrange("(b s p k) h -> b s p (k h)", s=NCH, p=128, k=2)
        xsv = xs.rearrange("(b s p) d -> b s p d", s=NCH, p=128)
        ouv = out.rearrange("(b s p) d -> b s p d", s=NCH, p=128)

        def tail_slice(idx):
            b_, s_ = idx % B, idx // B
            ln = idx % 2
            bd.wait("Y", "sV", tmk["use"].get(idx - 2, 0))
            bd.ydma(tl["xst"][ln], xnv[:, b_, :, s_, :], sem="sY")
            bd.ydma(tl["x"][ln], xsv[b_, s_, :, :], sem="sY")
            tmk["yld"][idx] = bd.mark("sY")
            bd.wait("G", "sV", tmk["use"].get(idx - 2, 0))
            bd.gdma(tl["oat"][ln], oav[:, b_, s_, :])
            bd.gdma(tl["facc"][ln], fav[:, b_, s_, :])
            bd.gdma(tl["comb"][ln], cmv[b_, s_, :, :])
            tmk["gld"][idx] = bd.mark("sG")
            b0 = 0 if ln == 0 else 4
            bd.wait("T", "sG", wgwo_ld)
            bd.wait("T", "sY", tmk["yld"][idx])
            bd.wait("T", "sS", tmk["silu"].get(idx - 2, 0))
            for half in range(2):
                for dt_ in range(8):
                    bd.mm(pf[b0 + half][:, 0:512], tl["xst"][ln][:, dt_ * 128:(dt_ + 1) * 128],
                          dwb[:, dt_ * 1024 + half * 512:dt_ * 1024 + (half + 1) * 512],
                          start=(dt_ == 0), stop=(dt_ == 7), inc=(1 if dt_ == 7 else 0))
            tmk["gmm"][idx] = bd.mark("sT")
            bd.wait("S", "sG", tmk["gld"][idx])
            bd.wait("S", "sV", tmk["use"].get(idx - 2, 0))
            for hh in range(H):
                bd.wait("S", "sS")
                bd.s("activation", tl["sq"][ln], tl["oat"][ln][:, hh * 256:(hh + 1) * 256],
                     AF.Square, accum_out=tl["ssum"][ln][:, hh:hh + 1])
            bd.wait("S", "sS")
            bd.s("activation", tl["rstd"][ln], tl["ssum"][ln], AF.Ln, scale=1.0 / HDV, bias=EPS)
            bd.wait("S", "sS")
            bd.s("activation", tl["rstd"][ln], tl["rstd"][ln], AF.Exp, scale=-0.5)
            rsm = bd.mark("sS")
            bd.wait("V", "sS", rsm)
            for hh in range(H):
                bd.v("tensor_scalar", tl["on"][ln][:, hh * 256:(hh + 1) * 256],
                     tl["oat"][ln][:, hh * 256:(hh + 1) * 256],
                     tl["rstd"][ln][:, hh:hh + 1], None, ALU.mult)
            bd.wait("V", "sV")
            bd.v("tensor_mul", tl["on"][ln][:, 0:512], tl["on"][ln][:, 0:512], k_nw[:, 0:512])
            bd.v("tensor_mul", tl["on"][ln][:, 512:1024], tl["on"][ln][:, 512:1024], k_nw[:, 512:1024])
            onm = bd.mark("sV")
            bd.wait("S", "sT", tmk["gmm"][idx])
            bd.wait("S", "sV", tmk["oguse"].get(idx - 2, 0))
            bd.s("activation", tl["sgt"][ln], pf[b0][:, 0:512], AF.Silu)
            s1 = bd.mark("sS")
            bd.wait("V", "sS", s1)
            bd.wait("V", "sV", onm)
            bd.v("tensor_tensor", tl["og"][ln][:, 0:512], tl["on"][ln][:, 0:512], tl["sgt"][ln], ALU.mult)
            bd.wait("S", "sV")
            bd.s("activation", tl["sgt"][ln], pf[b0 + 1][:, 0:512], AF.Silu)
            tmk["silu"][idx] = bd.mark("sS")
            bd.wait("V", "sS", tmk["silu"][idx])
            bd.v("tensor_tensor", tl["og"][ln][:, 512:1024], tl["on"][ln][:, 512:1024], tl["sgt"][ln], ALU.mult)
            ogm = bd.mark("sV")
            bd.wait("T", "sV", ogm)
            bd.wait("T", "sV", tmk["oguse"].get(idx - 1, 0))
            for vt in range(8):
                bd.tr(pb[:, vt * 128:(vt + 1) * 128], tl["og"][ln][:, vt * 128:(vt + 1) * 128],
                      k_id[:], inc=(1 if vt == 7 else 0))
            trm = bd.mark("sT")
            bd.wait("V", "sT", trm)
            bd.v("tensor_copy", tl["ogT"][ln], pb[:])
            tmk["oguse"][idx] = bd.mark("sV")
            bd.wait("T", "sV", tmk["oguse"][idx])
            bd.wait("T", "sV", tmk["outw"].get(idx - 1, 0))
            for half in range(2):
                for vt in range(8):
                    bd.mm(pf[2 + half][:, 0:512], tl["ogT"][ln][:, vt * 128:(vt + 1) * 128],
                          dwb[:, 8 * 1024 + vt * 1024 + half * 512:8 * 1024 + vt * 1024 + (half + 1) * 512],
                          start=(vt == 0), stop=(vt == 7), inc=(1 if vt == 7 else 0))
            tmk["womm"][idx] = bd.mark("sT")
            ob = tl["outsb"][ln]
            bd.wait("V", "sT", tmk["womm"][idx])
            bd.wait("V", "sY", tmk["yld"][idx])
            bd.wait("V", "sG", tmk["gld"][idx])
            for half in range(2):
                bd.v("tensor_tensor", ob[:, half * 512:(half + 1) * 512], pf[2 + half][:, 0:512],
                     tl["x"][ln][:, half * 512:(half + 1) * 512], ALU.add)
            bd.wait("V", "sV")
            for half in range(2):
                bd.v("tensor_tensor", ob[:, half * 512:(half + 1) * 512],
                     ob[:, half * 512:(half + 1) * 512],
                     tl["facc"][ln][:, half * 512:(half + 1) * 512], ALU.add)
            bd.wait("V", "sV")
            for half in range(2):
                bd.v("tensor_tensor", tl["on"][ln][:, half * 512:(half + 1) * 512],
                     tl["comb"][ln][:, half * 512:(half + 1) * 512],
                     tl["comb"][ln][:, 1024 + half * 512:1024 + (half + 1) * 512], ALU.add)
            bd.wait("V", "sV")
            for half in range(2):
                bd.v("tensor_tensor", ob[:, half * 512:(half + 1) * 512],
                     ob[:, half * 512:(half + 1) * 512],
                     tl["on"][ln][:, half * 512:(half + 1) * 512], ALU.add)
            tmk["use"][idx] = bd.mark("sV")
            bd.wait("Y", "sV", tmk["use"][idx])
            bd.ydma(ouv[b_, s_, :, :], ob, sem="sY")
            tmk["outw"][idx] = tmk["use"][idx]

        bd.wait("T", "sV")
        bd.wait("S", "sV")
        bd.wait("Y", "sT")
        bd.wait("Y", "sGD")
        for idx in range(B * NCH):
            tail_slice(idx)

        bd.wait("Y", "sY", bd.mark("sY"))

    bd0 = Bd("C", None, {k: None for k in SEMS}, loop_delta)
    bds = {role: None for role in ROLES}

    def run_role(role, fn):
        def f(eng):
            if bds[role] is None:
                bds[role] = Bd(role, eng, sems, loop_delta)
            else:
                bds[role].eng = eng
            fn(bds[role])
        return f

    with es:
        with es_ab:
            describe_ab(bd0)
            with nc.Block() as block:
                block.tensor(run_role("T", describe_ab))
                block.vector(run_role("V", describe_ab))
                block.scalar(run_role("S", describe_ab))
                block.gpsimd(run_role("G", describe_ab))
                block.sync(run_role("Y", describe_ab))
        with es_p2:
            alloc_p2()
            describe_p2(bd0)
            with nc.Block() as block:
                block.tensor(run_role("T", describe_p2))
                block.vector(run_role("V", describe_p2))
                block.scalar(run_role("S", describe_p2))
                block.gpsimd(run_role("G", describe_p2))
                block.sync(run_role("Y", describe_p2))
    return nc


_CACHE = {}


def _consts():
    ident = np.eye(128, dtype=np.float32)
    t = np.tril(np.ones((128, 128), np.float32)).T
    tri16 = (t / 16.0).astype(np.float32)
    triS = (t - np.eye(128, dtype=np.float32))
    onesr = np.ones((1, 128), np.float32)
    ov16 = np.full((128, 1), 1 / 16.0, np.float32)
    io4 = np.tile(np.arange(4, dtype=np.float32)[None, :], (128, 1))
    io8 = np.tile(np.arange(8, dtype=np.float32)[None, :], (128, 1))
    m4 = np.ascontiguousarray(np.tile(t, (1, 4)))
    misc = np.tile(np.array([[1e-5, -0.5, -1.0, 1.0 / 256.0]], np.float32), (128, 1))
    pidx = np.arange(128, dtype=np.float32).reshape(128, 1)
    th = np.tile((np.arange(16, dtype=np.float32) * 512.0)[None, :], (128, 1))
    return dict(c_id=ident, c_t16=tri16, c_tS=triS, c_or=onesr, c_ov16=ov16,
                c_io4=io4, c_io8=io8, c_m4=m4, c_misc=misc, c_pidx=pidx, c_th=th)


def make_inmap(x_shard, ins, B, S):
    c = np.ascontiguousarray
    m = dict(
        xs=c(x_shard.reshape(B * S, D)),
        w_gate=ins["L0_attn_w_gate"],
        w_q=ins["L0_attn_w_q"],
        w_k=ins["L0_attn_w_k"],
        w_v=ins["L0_attn_w_v"],
        w_gk1T=c(np.asarray(ins["L0_attn_w_gk1"]).transpose(0, 2, 1)),
        w_gk2=ins["L0_attn_w_gk2"],
        b_gk=ins["L0_attn_b_gk"],
        w_ks=ins["L0_attn_w_ks"],
        w_vs=ins["L0_attn_w_vs"],
        w_gk1sT=c(np.asarray(ins["L0_attn_w_gk1s"]).T),
        w_gk2s=ins["L0_attn_w_gk2s"],
        b_gks=np.asarray(ins["L0_attn_b_gks"]).reshape(1, -1),
        w_g=ins["L0_attn_w_g"],
        w_o=ins["L0_attn_w_o"],
        normw4=np.tile(np.asarray(ins["L0_attn_norm_w"]), 4).reshape(1, -1),
        frouterT=c(np.asarray(ins["L0_ffn_router_w"]).T),
        fbias=np.asarray(ins["L0_ffn_routing_bias"]).reshape(1, -1),
    )
    m.update(_consts())
    m = {k: np.ascontiguousarray(np.asarray(v), dtype=np.float32) for k, v in m.items()}
    import ml_dtypes
    bf = lambda a: np.ascontiguousarray(np.asarray(a, dtype=np.float32)).astype(ml_dtypes.bfloat16)
    m["fsh_gT"] = bf(np.asarray(ins["L0_ffn_shared_gate"]).T)
    m["fsh_uT"] = bf(np.asarray(ins["L0_ffn_shared_up"]).T)
    m["fsh_dT"] = bf(np.asarray(ins["L0_ffn_shared_down"]).T)
    m["fgate"] = bf(ins["L0_ffn_gate_w"])
    m["fup"] = bf(ins["L0_ffn_up_w"])
    m["fdown"] = bf(ins["L0_ffn_down_w"])
    return m


def kernel(**inputs):
    NCORES = 8
    x = np.asarray(inputs["L0_attn_x"])
    BT, S = x.shape[0], x.shape[1]
    B = BT // NCORES
    key = (B, S)
    if key not in _CACHE:
        _CACHE[key] = build_program(B, S)
    nc = _CACHE[key]
    in_maps = [make_inmap(x[i * B:(i + 1) * B], inputs, B, S) for i in range(NCORES)]
    res = run_bass_kernel_spmd(nc, in_maps, core_ids=list(range(NCORES)))
    return np.concatenate([r["out"].reshape(B, S, D) for r in res.results], axis=0)


# version 31
# speedup vs baseline: 1.0001x; 1.0001x over previous
import math
import numpy as np
from contextlib import ExitStack

import concourse.bass as bass
import concourse.mybir as mybir
from concourse.bass_types import DynSlice as ds
from concourse.bass_utils import run_bass_kernel_spmd
from concourse.ordered_engine_set import OrderedEngineSet
from concourse.expressions import smin

F32 = mybir.dt.float32
BF16 = mybir.dt.bfloat16
U32 = mybir.dt.uint32
AF = mybir.ActivationFunctionType
ALU = mybir.AluOpType
AX = mybir.AxisListType

D = 1024
DK = 512
DV = 1024
H = 4
HDK = 128
HDV = 256
NM = 4
NCHAN = 5
LOW = 16
NE = 8
FI = 2048
EPS = 1e-5
SCALE = HDK ** -0.5

ROLES = ("T", "V", "S", "G", "Y")
SEMS = ("sT", "sV", "sS", "sG", "sY", "sWA", "sWB", "sGC", "sGD")
BIG = float(1 << 24)


class Aff:
    __slots__ = ("c", "m")

    def __init__(self, c=0, m=None):
        self.c = c
        self.m = m or {}

    def add(self, n):
        return Aff(self.c + n, self.m)

    def addreg(self, reg, coef):
        m2 = dict(self.m)
        k = id(reg)
        r0, c0 = m2.get(k, (reg, 0))
        m2[k] = (reg, c0 + coef)
        return Aff(self.c, m2)

    def expr(self):
        e = self.c
        for reg, co in self.m.values():
            if co:
                e = reg * co + e
        return e


class Bd:

    def __init__(self, role, eng, sems, loop_delta):
        self.role = role
        self.eng = eng
        self.sems = sems
        self.cnt = {k: Aff() for k in SEMS}
        self.loop_delta = loop_delta
        self._loop_path = []
        self._r1 = None
        self._r2 = None

    def _scratch(self):
        if self._r1 is None:
            self._r1 = self.eng.alloc_register("bd_w1")
            self._r2 = self.eng.alloc_register("bd_w2")
        return self._r1, self._r2

    def bump(self, sem, n):
        self.cnt[sem] = self.cnt[sem].add(n)

    def mark(self, sem):
        return self.cnt[sem]

    def wait(self, role, sem, val=None):
        if self.role != role:
            return
        v = self.cnt[sem] if val is None else val
        if not isinstance(v, Aff):
            self.eng.wait_ge(self.sems[sem], v)
            return
        terms = [(reg, co) for reg, co in v.m.values() if co]
        if not terms:
            self.eng.wait_ge(self.sems[sem], v.c)
            return
        r1, r2 = self._scratch()
        reg0, co0 = terms[0]
        self.eng.reg_alu(r1, reg0, co0, ALU.mult)
        for reg, co in terms[1:]:
            self.eng.reg_alu(r2, reg, co, ALU.mult)
            self.eng.reg_alu(r1, r1, r2, ALU.add)
        if v.c:
            self.eng.reg_alu(r1, r1, v.c, ALU.add)
        self.eng.wait_ge(self.sems[sem], r1)

    def _fin(self, sem, inc, ins):
        if inc:
            self.bump(sem, inc)
            if ins is not None:
                ins.then_inc(self.sems[sem], inc)

    def mm(self, out, lhsT, rhs, start=True, stop=True, inc=0):
        ins = self.eng.matmul(out, lhsT, rhs, start=start, stop=stop) if self.role == "T" else None
        self._fin("sT", inc, ins)

    def tr(self, out, in_, ident, inc=0):
        ins = self.eng.transpose(out, in_, ident) if self.role == "T" else None
        self._fin("sT", inc, ins)

    def v(self, meth, *args, inc=1, **kw):
        ins = getattr(self.eng, meth)(*args, **kw) if self.role == "V" else None
        self._fin("sV", inc, ins)

    def s(self, meth, *args, inc=1, **kw):
        ins = getattr(self.eng, meth)(*args, **kw) if self.role == "S" else None
        self._fin("sS", inc, ins)

    def gv(self, meth, *args, inc=1, **kw):
        ins = getattr(self.eng, meth)(*args, **kw) if self.role == "G" else None
        self._fin("sGC", inc, ins)

    def gdma(self, out, in_, inc=16, sem="sG", **kw):
        self.wait("G", sem)
        ins = self.eng.dma_start(out=out, in_=in_, **kw) if self.role == "G" else None
        self._fin(sem, inc, ins)

    def ydma(self, out, in_, inc=16, sem="sY", **kw):
        self.wait("Y", sem)
        ins = self.eng.dma_start(out=out, in_=in_, **kw) if self.role == "Y" else None
        self._fin(sem, inc, ins)

    def sdma(self, out, in_, inc=16, sem="sS", **kw):
        self.wait("S", sem)
        ins = self.eng.dma_start(out=out, in_=in_, **kw) if self.role == "S" else None
        self._fin(sem, inc, ins)

    def gind(self, out, out_offset, in_, in_offset, bounds, inc=16, element_offset=0):
        from concourse.bass import IndirectOffsetOnAxis
        ins = None
        if self.role == "G":
            if not hasattr(self, "_bcregs"):
                self._bcregs = {}
            if bounds not in self._bcregs:
                r = self.eng.alloc_register(f"g_bc{len(self._bcregs)}")
                self.eng.reg_mov(r, bounds)
                self._bcregs[bounds] = r
            oo = IndirectOffsetOnAxis(ap=out_offset, axis=0) if out_offset is not None else None
            io = IndirectOffsetOnAxis(ap=in_offset, axis=0) if in_offset is not None else None
            ins = self.eng.indirect_dma_start(
                out=out, out_offset=oo, in_=in_, in_offset=io,
                element_offset=element_offset,
                bounds_check=self._bcregs[bounds], oob_is_err=False)
        self._fin("sGD", inc, ins)

    def eload(self, role, key, ap, maxv, par=0):
        if self.role != role:
            return 0
        nc_ = self.eng.bass
        if not hasattr(self, "_eregs"):
            self._eregs = {}
        kp = (key, par)
        if kp not in self._eregs:
            es_ = OrderedEngineSet([self.eng.engine])
            regs = nc_.alloc_registers(f"el_{self.role}_{key}_{par}", engines=es_)
            rv = nc_.snap(regs, engines=es_, min_val=0, max_val=maxv)
            self._eregs[kp] = (regs, rv)
        regs, rv = self._eregs[kp]
        self.eng.reg_load(regs.handles[0], ap)
        return rv

    def ploop(self, trips, body):
        for i in range(trips):
            body(i)

    def loop(self, key, trips, body):
        key = "/".join(self._loop_path + [key])
        start = dict(self.cnt)
        if self.role == "C":
            self._loop_path.append(key)
            body(0)
            self._loop_path.pop()
            delta = {k: self.cnt[k].c - start[k].c for k in SEMS}
            self.loop_delta[key] = delta
            self.cnt = {k: start[k].add(trips * delta[k]) for k in SEMS}
        else:
            delta = self.loop_delta[key]
            self._loop_path.append(key)
            depth = len(self._loop_path)
            nc_ = self.eng.bass
            engines = OrderedEngineSet([self.eng.engine])
            if not hasattr(self, "_lregs"):
                self._lregs = {}
            if depth not in self._lregs:
                self._lregs[depth] = nc_.alloc_registers(f"bdl_{self.role}_{depth}",
                                                         engines=engines)
            regs = self._lregs[depth]
            lid = nc_.next_id()
            lstart = f"bdl{lid}_s"
            lend = f"bdl{lid}_e"
            nc_.regs_mov(regs, 0)
            nc_.br(lstart, engines=engines)
            with nc_.body(lstart, valid_engines=engines):
                reg = nc_.snap(regs, engines=engines, min_val=0, max_val=trips - 1)
                self.cnt = {k: start[k].addreg(reg, delta[k]) for k in SEMS}
                body(reg)
                for k in SEMS:
                    got = self.cnt[k].c - start[k].c
                    assert got == delta[k], f"loop {key} sem {k}: {got} != {delta[k]}"
                nc_.regs_alu(regs, regs, 1, op=ALU.add)
                nc_.br_lt(regs, trips, on_true=lstart, on_false=lend, engines=engines)
            nc_.switch_bb(lend)
            self._loop_path.pop()
            self.cnt = {k: start[k].add(trips * delta[k]) for k in SEMS}


def build_program(B=4, S=2048):
    NCH = S // 128
    NT = S // 512
    CAP = math.ceil(S * 0.5)
    SP8 = 8 * S

    nc = bass.Bass("TRN2", target_bir_lowering=False)

    A = lambda name, shape, dt=F32: nc.dram_tensor(name, shape, dt, kind="ExternalInput").ap()
    xs = A("xs", [B * S, D])
    w_gate = A("w_gate", [D, NM])
    w_q = A("w_q", [D, DK])
    w_k = A("w_k", [NM, D, DK])
    w_v = A("w_v", [NM, D, DV])
    w_gk1T = A("w_gk1T", [NM, LOW, D])
    w_gk2 = A("w_gk2", [NM, LOW, DK])
    b_gk = A("b_gk", [NM, DK])
    w_ks = A("w_ks", [D, DK])
    w_vs = A("w_vs", [D, DV])
    w_gk1sT = A("w_gk1sT", [LOW, D])
    w_gk2s = A("w_gk2s", [LOW, DK])
    b_gks = A("b_gks", [1, DK])
    w_g = A("w_g", [D, DV])
    w_o = A("w_o", [DV, D])
    normw4 = A("normw4", [1, DV])
    fsh_gT = A("fsh_gT", [D, FI], BF16)
    fsh_uT = A("fsh_uT", [D, FI], BF16)
    fsh_dT = A("fsh_dT", [FI, D], BF16)
    fgate = A("fgate", [NE, D, FI], BF16)
    fup = A("fup", [NE, D, FI], BF16)
    fdown = A("fdown", [NE, FI, D], BF16)
    frouterT = A("frouterT", [D, NE])
    fbias = A("fbias", [1, NE])
    c_id = A("c_id", [128, 128])
    c_t16 = A("c_t16", [128, 128])
    c_tS = A("c_tS", [128, 128])
    c_or = A("c_or", [1, 128])
    c_ov16 = A("c_ov16", [128, 1])
    c_io4 = A("c_io4", [128, 4])
    c_io8 = A("c_io8", [128, 8])
    c_m4 = A("c_m4", [128, 512])
    c_misc = A("c_misc", [128, 4])

    out = nc.dram_tensor("out", [B * S, D], F32, kind="ExternalOutput").ap()

    xnT_sc = nc.dram_tensor("xnT_sc", [128, B * SP8], BF16, kind="Internal").ap()
    oacc_sc = nc.dram_tensor("oacc_sc", [128, B * NCH * DV], BF16, kind="Internal").ap()
    wgk_sc = nc.dram_tensor("wgk_sc", [128, NCHAN * 8 * DK], BF16, kind="Internal").ap()

    T_TOK = B * S
    NTILE = -(-(2 * T_TOK + NE * 511) // 512)
    NSLOT = NTILE * 512
    NGC = B * NCH
    xn_tok = nc.dram_tensor("xn_tok", [T_TOK + 1, D], BF16, kind="Internal").ap()
    disp = nc.dram_tensor("disp", [NSLOT, 4], F32, kind="Internal").ap()
    comb = nc.dram_tensor("comb", [2 * T_TOK, DV], BF16, kind="Internal").ap()
    cnt_sc = nc.dram_tensor("cnt_sc", [NGC, NE], F32, kind="Internal").ap()
    rank_sc = nc.dram_tensor("rank_sc", [128, NGC * NE], F32, kind="Internal").ap()
    vals_sc = nc.dram_tensor("vals_sc", [128, NGC * NE * 4], F32, kind="Internal").ap()
    fb_sc = nc.dram_tensor("fb_sc", [1, (NGC + 1) * NE], F32, kind="Internal").ap()
    eidf_sc = nc.dram_tensor("eidf_sc", [1, NTILE], F32, kind="Internal").ap()
    facc_sc = nc.dram_tensor("facc_sc", [128, B * NCH * DV], BF16, kind="Internal").ap()
    c_pidx = A("c_pidx", [128, 1])
    c_th = A("c_th", [128, 16])

    es = ExitStack()
    es_ab = ExitStack()
    es_p2 = ExitStack()
    sb = lambda name, shape, dt: es.enter_context(nc.sbuf_tensor(name, shape, dt))
    sba = lambda name, shape, dt: es_ab.enter_context(nc.sbuf_tensor(name, shape, dt))
    sbp = lambda name, shape, dt: es_p2.enter_context(nc.sbuf_tensor(name, shape, dt))
    pt = lambda name, shape, dt=F32: es.enter_context(nc.psum_tensor(name, shape, dt))

    k_id = sb("k_id", [128, 128], BF16)
    k_t16 = sb("k_t16", [128, 128], BF16)
    k_tS = sb("k_tS", [128, 128], F32)
    k_or = sb("k_or", [1, 128], BF16)
    k_orf = sb("k_orf", [1, 128], F32)
    k_ov16 = sb("k_ov16", [128, 1], BF16)
    k_ocf = sb("k_ocf", [128, 1], F32)
    k_io4 = sb("k_io4", [128, 4], F32)
    k_io8 = sb("k_io8", [128, 8], F32)
    k_nw = sb("k_nw", [128, DV], BF16)
    xstat = sb("xstat", [128, 8 * 128], BF16)
    k_misc = sb("k_misc", [128, 4], F32)
    lg8 = sb("lg8", [128, 8], F32)
    m8 = sb("m8", [128, 8], F32)
    idxu = sb("idxu", [128, 8], U32)
    idxf = sb("idxf", [128, 8], F32)
    d12 = sb("d12", [128, 1], F32)
    rwt = sb("rwt", [128, 1], F32)
    nc.const_aps.register(F32, EPS, k_misc[:, 0:1])
    nc.const_aps.register(F32, -0.5, k_misc[:, 1:2])
    nc.const_aps.register(F32, -1.0, k_misc[:, 2:3])
    nc.const_aps.register(F32, 1.0 / HDV, k_misc[:, 3:4])
    k_pidx = sb("k_pidx", [128, 1], F32)
    k_th = sb("k_th", [128, 16], F32)
    k_idf = sb("k_idf", [128, 128], F32)

    k_m4 = sba("k_m4", [128, 512], BF16)
    k_nwrow = sba("k_nwrow", [1, DV], F32)
    k_bgl = [sba(f"k_bg{i}", [1, DK], BF16) for i in range(NCHAN)]
    k_fb = sba("k_fb", [1, NE], BF16)
    k_g1 = sba("k_g1", [16, NCHAN * D], BF16)
    k_g2 = sba("k_g2", [16, NCHAN * DK], BF16)
    k_wgate = sba("k_wgate", [128, 8 * NM], BF16)
    k_frT = sba("k_frT", [128, 8 * NE], BF16)
    xbuf = sba("xbuf", [128, 4 * D], F32)
    bnraw = sba("bnraw", [128, 12], F32)
    bnst = sba("bnst", [128, 2], F32)
    r1c = sba("r1c", [128, 1], F32)
    rstd = sba("rstd", [128, 1], F32)
    xnT = sba("xnT", [128, SP8], BF16)
    q_b = sba("q_b", [128, NCH * 512], BF16)
    r_se1 = sba("r_se1", [128, NCH * 4], F32)
    r_se2 = sba("r_se2", [128, NCH * 4], F32)
    r_sel = sba("r_sel", [128, NCH * 4], F32)
    r_rank = sba("r_rank", [128, NCH * 4], F32)
    r_rw = sba("r_rw", [128, NCH * 2], F32)
    r_kept = sba("r_kept", [128, NCH * 4], F32)
    r_wm = sba("r_wm", [128, NCH * 4], F32)
    cnt_st = sba("cnt_st", [1, NCH * 4], F32)
    tot_r = sba("tot_r", [1, 4], F32)
    th0_r = sba("th0_r", [1, 4], F32)
    th_r = sba("th_r", [1, 4], F32)
    base_r = sba("base_r", [1, 4], F32)
    wch = sba("wch", [128, 8 * 2048], BF16)
    kcb0 = sba("kcb0", [128, 512], BF16)
    vcb0 = sba("vcb0", [128, DV], BF16)
    gkcb0 = sba("gkcb0", [128, 512], BF16)
    kcb1 = sba("kcb1", [128, 512], BF16)
    vcb1 = sba("vcb1", [128, DV], BF16)
    gkcb1 = sba("gkcb1", [128, 512], BF16)
    kgv = [(kcb0, vcb0, gkcb0), (kcb1, vcb1, gkcb1)]
    sig1 = sba("sig1", [128, 512], F32)
    sig2 = sba("sig2", [128, 512], F32)
    lam = sba("lam", [128, 512], F32)
    nlam = sba("nlam", [128, 512], F32)
    decay = sba("decay", [128, 4], F32)
    qt = sba("qt", [128, 512], BF16)
    kt = sba("kt", [128, 512], BF16)
    qtT = sba("qtT", [128, 512], BF16)
    ktT = sba("ktT", [128, 512], BF16)
    att = sba("att", [128, 512], BF16)
    Sbf = sba("Sbf", [128, 2 * 1024], BF16)
    t1 = sba("t1", [128, 512], F32)
    t1s = sba("t1s", [128, 512], F32)
    t1bs = sba("t1bs", [128, 512], F32)
    t1b = sba("t1b", [128, 512], F32)
    oac = sba("oac", [128, NCH * DV], BF16)
    flg8 = sba("flg8", [128, 8], F32)
    fm8 = m8
    fidxu = idxu
    fidxf = idxf
    fse8a = sig1[:, 0:8]
    fse8b = sig1[:, 8:16]
    fsel8 = sig1[:, 16:24]
    ftmp8 = sig1[:, 24:32]
    ftmp8b = sig1[:, 32:40]
    fsg = sig1[:, 40:42]
    fd1 = sig1[:, 42:43]
    frc = sig1[:, 43:44]
    fw2 = sig1[:, 44:46]
    tcol2 = sig1[:, 46:48]
    cnt16 = sig2[0:1, 0:NCH * 8]
    stag_rank = sba("stag_rank", [128, NCH * 8], F32)
    stag_vals = sba("stag_vals", [128, NCH * 32], F32)

    p2v = {}

    def alloc_p2():
        p2v["xnT2"] = sbp("xnT2", [128, SP8], BF16)
        p2v["gwb"] = sbp("gwb", [128, 8 * FI], BF16)
        p2v["uwb"] = sbp("uwb", [128, 8 * FI], BF16)
        p2v["dwb"] = sbp("dwb", [128, 16 * 1024], BF16)
        p2v["yg"] = sbp("yg", [128, 16 * 512], BF16)
        p2v["sgt"] = sbp("sgt", [128, 512], BF16)
        p2v["sgt2"] = sbp("sgt2", [128, 512], BF16)
        p2v["facc"] = sbp("facc", [128, NCH * DV], BF16)
        p2v["ssum"] = sbp("ssum", [128, 4], F32)
        p2v["rstd4"] = sbp("rstd4", [128, 4], F32)
        p2v["dispsb"] = sbp("dispsb", [128, 32], F32)
        p2v["widx1"] = sbp("widx1", [128, NTILE], U32)
        p2v["widx2"] = sbp("widx2", [128, NTILE], U32)
        p2v["eidb"] = sbp("eidb", [128, NTILE], F32)
        p2v["widxf"] = sbp("widxf", [128, NTILE], F32)
        p2v["eidrowF"] = sbp("eidrowF", [1, NTILE], F32)
        uwbF = p2v["uwb"].bitcast(F32)
        gwbF = p2v["gwb"].bitcast(F32)
        p2v["sq_t"] = uwbF[:, 5200:5456]
        p2v["x_t"] = uwbF[:, 6096:6096 + 2 * D]
        p2v["outsb"] = gwbF[:, 2048:2048 + 4 * D]
        p2v["oat"] = p2v["gwb"][:, 12288:12288 + 4 * DV]
        p2v["on_t"] = p2v["yg"][:, 4096:4096 + DV]
        p2v["ogT"] = p2v["yg"][:, 5120:5120 + DV]
        p2v["eidu"] = sbp("eidu", [128, 1], U32)
        p2v["posu"] = xstat.bitcast(U32)[:, 0:NGC * 8]
        p2v["fbrow"] = p2v["outsb"][0:1, 2560:2560 + (NGC + 1) * NE]
        p2v["fb2"] = p2v["outsb"][0:NE, 3084:3084 + NGC + 2]
        p2v["p8sb"] = p2v["outsb"][0:NE, 3150:3151]
        p2v["pbs"] = p2v["outsb"][0:NE, 3151:3152]
        p2v["fbTsb"] = p2v["outsb"][0:NGC + 1, 3152:3152 + NE]
        p2v["cnt64sb"] = p2v["outsb"][0:NGC, 3160:3160 + NE]
        p2v["fin16"] = p2v["sq_t"][:, 16:32]
        p2v["eidf"] = p2v["sq_t"][:, 33:34]
        p2v["ts512"] = p2v["sq_t"][:, 32:33]
        p2v["fin8b"] = p2v["sq_t"][:, 8:16]
        p2v["posf"] = p2v["sq_t"][:, 0:8]
        p2v["rankm_sb"] = p2v["outsb"][:, 2048:2560]
        p2v["valsb"] = p2v["outsb"][:, 0:2048]
        p2v["pattbig"] = p2v["x_t"][:, 0:(NSLOT // 128) * 4]
    wtmp = sba("wtmp", [128, 8 * 512], BF16)

    pf = [pt(f"pf{i}", [128, 512], F32) for i in range(7)]
    pb = pt("pb", [128, 1024], BF16)

    sems = {s_: es.enter_context(nc.semaphore(s_)) for s_ in SEMS}
    loop_delta = {}

    wk_src = [w_k[mi] for mi in range(NM)] + [w_ks]
    wv_src = [w_v[mi] for mi in range(NM)] + [w_vs]

    def describe_ab(bd: Bd):
        bd.gdma(k_id[:], c_id)
        bd.gdma(k_t16[:], c_t16)
        bd.ydma(k_tS[:], c_tS)
        bd.gdma(k_or[:], c_or)
        bd.ydma(k_orf[:], c_or)
        bd.gdma(k_ov16[:], c_ov16)
        bd.ydma(k_ocf[:], c_ov16)
        bd.ydma(k_io4[:], c_io4)
        bd.ydma(k_io8[:], c_io8)
        bd.gdma(k_m4[:], c_m4)
        for ch in range(NM):
            bd.gdma(k_bgl[ch][:], b_gk[ch:ch + 1, :])
        bd.gdma(k_bgl[4][:], b_gks)
        bd.gdma(k_fb[:], fbias)
        for ch in range(NM):
            bd.gdma(k_g1[:16, ch * D:(ch + 1) * D], w_gk1T[ch])
            bd.gdma(k_g2[:16, ch * DK:(ch + 1) * DK], w_gk2[ch])
        bd.gdma(k_g1[:16, 4 * D:5 * D], w_gk1sT)
        bd.gdma(k_g2[:16, 4 * DK:5 * DK], w_gk2s)
        bd.gdma(k_wgate[:].rearrange("p (dt e) -> p dt e", dt=8),
                w_gate.rearrange("(dt p) e -> p dt e", p=128))
        bd.gdma(k_frT[:].rearrange("p (dt e) -> p dt e", dt=8),
                frouterT.rearrange("(dt p) e -> p dt e", p=128))
        bd.ydma(k_nwrow[:], normw4)
        bd.ydma(k_misc[:], c_misc)
        bd.ydma(k_pidx[:], c_pidx)
        bd.ydma(k_th[:], c_th)
        bd.ydma(k_idf[:], c_id)
        bd.wait("V", "sY")
        bd.v("tensor_scalar", k_ocf[:], k_ocf[:], 16.0, None, ALU.mult)
        bd.v("memset", lg8[:], -1e30)
        bd.v("memset", stag_vals[:], 0.0)
        bd.v("memset", xbuf[0:1, 0:D], 0.0)
        bd.wait("G", "sV")
        bd.gdma(xn_tok[T_TOK:T_TOK + 1, :], xbuf[0:1, 0:D])
        bd.wait("Y", "sG")
        bd.wait("T", "sY")
        bd.wait("T", "sG")
        bd.wait("T", "sV")
        bd.mm(pf[0][:, 0:512], k_orf[:], k_nwrow[0:1, 0:512], inc=1)
        bd.mm(pf[1][:, 0:512], k_orf[:], k_nwrow[0:1, 512:1024], inc=1)
        bd.wait("V", "sT")
        bd.v("tensor_copy", k_nw[:, 0:512], pf[0][:, 0:512])
        bd.v("tensor_copy", k_nw[:, 512:1024], pf[1][:, 0:512])

        for ch in range(NCHAN):
            bd.wait("V", "sG")
            for dt_ in range(8):
                bd.wait("T", "sV")
                bd.mm(pf[dt_ % 2][:, 0:512], k_g1[:16, ch * D + dt_ * 128: ch * D + (dt_ + 1) * 128],
                      k_g2[:16, ch * DK:(ch + 1) * DK], inc=1)
                bd.wait("V", "sT")
                bd.v("tensor_copy", wtmp[:, dt_ * 512:(dt_ + 1) * 512], pf[dt_ % 2][:, 0:512])
            bd.wait("G", "sV")
            bd.gdma(wgk_sc[:, ch * 8 * DK:(ch + 1) * 8 * DK], wtmp[:])

        def phase_ab(b):
            bd.wait("V", "sG")
            bd.wait("V", "sT")
            bd.wait("T", "sG")
            bd.wait("S", "sG")
            bd.wait("S", "sY")

            def a_chunk(c):
                q4, c4 = c // 4, c % 4
                if c4 == 0:
                    bd.wait("Y", "sV")
                    bd.ydma(xbuf[:].rearrange("p (c d) -> p c d", c=4),
                            xs.rearrange("(b q c p) d -> p b q c d", q=NCH // 4, c=4, p=128)[:, ds(b, 1), q4, :, :])
                xb = xbuf[:, c4 * D:(c4 + 1) * D]
                bd.wait("V", "sY")
                bd.v("bn_stats", bnraw[:, 0:6], xb[:, 0:512])
                bd.v("bn_stats", bnraw[:, 6:12], xb[:, 512:1024])
                bd.wait("V", "sV")
                bd.v("bn_aggr", bnst[:], bnraw[:])
                bd.wait("S", "sV")
                bd.s("activation", r1c[:], bnst[:, 1:2], AF.Ln, bias=EPS)
                bd.wait("S", "sS")
                bd.s("activation", rstd[:], r1c[:], AF.Exp, scale=-0.5)
                bd.wait("V", "sS")
                bd.wait("V", "sG")
                xn_st = wch[:, c * D:(c + 1) * D]
                bd.v("tensor_scalar", xn_st, xb, bnst[:, 0:1], rstd[:], ALU.subtract, ALU.mult)
                bd.wait("T", "sV")
                for dt_ in range(8):
                    bd.tr(pb[:, dt_ * 128:(dt_ + 1) * 128], xn_st[:, dt_ * 128:(dt_ + 1) * 128], k_id[:],
                          inc=(1 if dt_ == 7 else 0))
                bd.wait("V", "sT")
                bd.v("tensor_copy", xstat[:], pb[:])
                bd.wait("V", "sV")
                bd.v("tensor_copy",
                     xnT[:].rearrange("p (dt c s) -> p dt c s", dt=8, s=128)[:, :, ds(c, 1), :],
                     xstat[:].rearrange("p (dt s) -> p dt s", dt=8))
                bd.wait("T", "sV")
                for dt_ in range(8):
                    bd.mm(pf[0][:, 0:4], xstat[:, dt_ * 128:(dt_ + 1) * 128],
                          k_wgate[:, dt_ * 4:(dt_ + 1) * 4], start=(dt_ == 0), stop=(dt_ == 7),
                          inc=(1 if dt_ == 7 else 0))
                bd.wait("V", "sT")
                bd.v("tensor_copy", lg8[:, 0:4], pf[0][:, 0:4])
                bd.wait("V", "sV")
                bd.v("max", m8[:], lg8[:])
                bd.wait("V", "sV")
                bd.v("max_index", idxu[:], m8[:], lg8[:])
                bd.wait("V", "sV")
                bd.v("tensor_copy", idxf[:], idxu[:])
                bd.wait("V", "sV")
                bd.v("tensor_single_scalar", r_se1[:].rearrange("p (c m) -> p c m", m=4)[:, ds(c, 1), :], k_io4[:], idxf[:, 0:1], ALU.is_equal)
                bd.v("tensor_single_scalar", r_se2[:].rearrange("p (c m) -> p c m", m=4)[:, ds(c, 1), :], k_io4[:], idxf[:, 1:2], ALU.is_equal)
                bd.v("tensor_sub", d12[:], m8[:, 0:1], m8[:, 1:2])
                bd.wait("V", "sV")
                bd.v("tensor_add", r_sel[:].rearrange("p (c m) -> p c m", m=4)[:, ds(c, 1), :], r_se1[:].rearrange("p (c m) -> p c m", m=4)[:, ds(c, 1), :], r_se2[:].rearrange("p (c m) -> p c m", m=4)[:, ds(c, 1), :])
                bd.wait("S", "sV")
                bd.s("activation", rwt[:], d12[:], AF.Sigmoid)
                bd.wait("V", "sS")
                bd.v("tensor_scalar", r_rw[:].rearrange("p (c j) -> p c j", j=2)[:, ds(c, 1), 0:1], rwt[:], SCALE, None, ALU.mult)
                bd.wait("V", "sV")
                bd.v("tensor_scalar", r_rw[:].rearrange("p (c j) -> p c j", j=2)[:, ds(c, 1), 1:2],
                     r_rw[:].rearrange("p (c j) -> p c j", j=2)[:, ds(c, 1), 0:1], -1.0, SCALE,
                     ALU.mult, ALU.add)
                bd.wait("T", "sV")
                bd.mm(pf[1][:, 0:4], k_tS[:], r_sel[:].rearrange("p (c m) -> p c m", m=4)[:, ds(c, 1), :], inc=1)
                bd.mm(pf[2][0:1, 0:4], k_ocf[:], r_sel[:].rearrange("p (c m) -> p c m", m=4)[:, ds(c, 1), :], inc=1)
                bd.wait("V", "sT")
                bd.v("tensor_copy", r_rank[:].rearrange("p (c m) -> p c m", m=4)[:, ds(c, 1), :], pf[1][:, 0:4])
                bd.v("tensor_copy", cnt_st[0:1, :].rearrange("p (c m) -> p c m", m=4)[:, ds(c, 1), :], pf[2][0:1, 0:4])

                bd.wait("T", "sV")
                for dt_ in range(8):
                    bd.mm(pf[3][:, 0:8], xstat[:, dt_ * 128:(dt_ + 1) * 128],
                          k_frT[:, dt_ * 8:(dt_ + 1) * 8], start=(dt_ == 0), stop=False)
                bd.mm(pf[3][:, 0:8], k_or[:], k_fb[:], start=False, stop=True, inc=1)
                bd.wait("V", "sT")
                bd.v("tensor_copy", flg8[:], pf[3][:, 0:8])
                bd.wait("V", "sV")
                bd.v("max", fm8[:], flg8[:])
                bd.wait("V", "sV")
                bd.v("max_index", fidxu[:], fm8[:], flg8[:])
                bd.wait("S", "sV")
                bd.s("activation", fsg[:], fm8[:, 0:2], AF.Sigmoid)
                bd.wait("V", "sV")
                bd.v("tensor_copy", fidxf[:], fidxu[:])
                bd.wait("V", "sV")
                bd.v("tensor_single_scalar", fse8a[:], k_io8[:], fidxf[:, 0:1], ALU.is_equal)
                bd.v("tensor_single_scalar", fse8b[:], k_io8[:], fidxf[:, 1:2], ALU.is_equal)
                bd.wait("V", "sS")
                bd.v("tensor_add", fd1[:], fsg[:, 0:1], fsg[:, 1:2])
                bd.wait("V", "sV")
                bd.v("reciprocal", frc[:], fd1[:])
                bd.v("tensor_add", fsel8[:], fse8a[:], fse8b[:])
                bd.wait("V", "sV")
                bd.v("tensor_scalar", fw2[:], fsg[:], frc[:], None, ALU.mult)
                bd.wait("T", "sV")
                bd.mm(pf[3][:, 0:8], k_tS[:], fsel8[:], inc=1)
                bd.mm(pf[4][0:1, 0:8], k_ocf[:], fsel8[:], inc=1)
                bd.wait("V", "sT")
                bd.v("tensor_copy", ftmp8[:], pf[3][:, 0:8])
                bd.v("tensor_copy", cnt16[0:1, c * 8:(c + 1) * 8], pf[4][0:1, 0:8])
                bd.wait("V", "sV")
                bd.v("tensor_scalar", tcol2[:, 0:1], k_pidx[:], float(c * 128), None, ALU.add)
                bd.wait("V", "sV")
                bd.v("tensor_scalar", tcol2[:, 1:2], tcol2[:, 0:1], 2.0, None, ALU.mult)
                vv = stag_vals[:].rearrange("p (c e v) -> p c v e", c=NCH, e=8, v=4)
                bd.wait("V", "sV")
                bd.v("tensor_scalar", vv[:, ds(c, 1), 0, :], k_io8[:], 0.0, tcol2[:, 0:1],
                     ALU.mult, ALU.add)
                bd.v("tensor_scalar", ftmp8b[:], fse8a[:], fw2[:, 0:1], None, ALU.mult)
                bd.wait("V", "sV")
                bd.v("scalar_tensor_tensor", vv[:, ds(c, 1), 1, :], fse8b[:], fw2[:, 1:2],
                     ftmp8b[:], ALU.mult, ALU.add)
                bd.v("tensor_scalar", vv[:, ds(c, 1), 2, :], fse8b[:], 1.0, tcol2[:, 1:2],
                     ALU.mult, ALU.add)
                bd.wait("V", "sV")
                bd.v("tensor_scalar", ftmp8[:], ftmp8[:], -BIG, None, ALU.add)
                bd.wait("V", "sV")
                bd.v("tensor_mul", ftmp8[:], ftmp8[:], fsel8[:])
                bd.wait("V", "sV")
                bd.v("tensor_scalar", stag_rank[:, c * 8:(c + 1) * 8], ftmp8[:], BIG, None, ALU.add)

            bd.ploop(NCH, a_chunk)

            bd.wait("G", "sV")
            bd.gdma(xnT_sc.rearrange("p (b x) -> p b x", x=SP8)[:, ds(b, 1), :], xnT[:])
            bd.gdma(xn_tok[0:T_TOK, :].rearrange("(b c p) h -> b p c h", c=NCH, p=128)[ds(b, 1), :, :, :],
                    wch[:].rearrange("p (c h) -> p c h", c=NCH))
            bd.gdma(cnt_sc.rearrange("(b x) e -> b (x e)", x=NCH)[ds(b, 1), :], cnt16)
            bd.gdma(rank_sc.rearrange("p (b x) -> p b x", x=NCH * 8)[:, ds(b, 1), :], stag_rank[:])
            bd.gdma(vals_sc.rearrange("p (b x) -> p b x", x=NCH * 32)[:, ds(b, 1), :], stag_vals[:])

            bd.wait("V", "sV")
            bd.v("tensor_reduce", tot_r[0:1, 0:4],
                 cnt_st[0:1, :].rearrange("p (c m) -> p m c", m=4), AX.X, ALU.add)
            bd.v("memset", base_r[:], 0.0)
            bd.wait("V", "sV")
            bd.v("tensor_scalar", th0_r[:], tot_r[:], float(-CAP), None, ALU.add)
            for c in range(NCH):
                bd.wait("V", "sV")
                bd.v("tensor_sub", th_r[:], th0_r[:], base_r[:])
                bd.wait("T", "sV")
                bd.mm(pf[3][:, 0:4], k_orf[:], th_r[0:1, 0:4], inc=1)
                bd.wait("V", "sT")
                bd.v("tensor_tensor", r_kept[:, c * 4:(c + 1) * 4], r_rank[:, c * 4:(c + 1) * 4],
                     pf[3][:, 0:4], ALU.is_ge)
                bd.wait("V", "sV")
                bd.v("tensor_mul", r_kept[:, c * 4:(c + 1) * 4], r_kept[:, c * 4:(c + 1) * 4],
                     r_sel[:, c * 4:(c + 1) * 4])
                bd.v("tensor_scalar", r_wm[:, c * 4:(c + 1) * 4], r_se1[:, c * 4:(c + 1) * 4],
                     r_rw[:, c * 2:c * 2 + 1], None, ALU.mult)
                bd.wait("V", "sV")
                bd.v("scalar_tensor_tensor", r_wm[:, c * 4:(c + 1) * 4], r_se2[:, c * 4:(c + 1) * 4],
                     r_rw[:, c * 2 + 1:c * 2 + 2], r_wm[:, c * 4:(c + 1) * 4], ALU.mult, ALU.add)
                bd.wait("V", "sV")
                bd.v("tensor_mul", r_wm[:, c * 4:(c + 1) * 4], r_wm[:, c * 4:(c + 1) * 4],
                     r_kept[:, c * 4:(c + 1) * 4])
                bd.v("tensor_add", base_r[:], base_r[:], cnt_st[0:1, c * 4:(c + 1) * 4])

            bd.wait("G", "sT")
            bd.gdma(wch[:].rearrange("p (dt k) -> p dt k", dt=8)[:, :, 0:512],
                    w_q.rearrange("(dt p) k -> p dt k", p=128))

            def q_chunk(c):
                bd.wait("V", "sT")
                bd.v("tensor_copy", xstat[:].rearrange("p (dt s) -> p dt s", dt=8),
                     xnT[:].rearrange("p (dt c s) -> p dt c s", dt=8, s=128)[:, :, ds(c, 1), :])
                bd.wait("T", "sG")
                bd.wait("T", "sV")
                for dt_ in range(8):
                    bd.mm(pf[6][:, 0:512], xstat[:, dt_ * 128:(dt_ + 1) * 128],
                          wch[:, dt_ * 2048:dt_ * 2048 + 512], start=(dt_ == 0),
                          stop=(dt_ == 7), inc=(1 if dt_ == 7 else 0))
                bd.wait("V", "sT")
                bd.v("tensor_copy", q_b[:].rearrange("p (c x) -> p c x", x=512)[:, ds(c, 1), :], pf[6][:, 0:512])

            bd.ploop(NCH, q_chunk)

            last_wch_use = [bd.mark("sT")]
            for ch in range(NCHAN):
                routed = ch < NM
                bd.wait("G", "sT", last_wch_use[0])
                bd.wait("G", "sG")
                wv4 = wch[:].rearrange("p (dt k) -> p dt k", dt=8)
                bd.gdma(wv4[:, :, 0:512], wk_src[ch].rearrange("(dt p) k -> p dt k", p=128), sem="sWA")
                bd.gdma(wv4[:, :, 512:1536], wv_src[ch].rearrange("(dt p) k -> p dt k", p=128), sem="sWB")
                bd.gdma(wv4[:, :, 1536:2048],
                        wgk_sc[:, ch * 8 * DK:(ch + 1) * 8 * DK].rearrange("p (dt k) -> p dt k", dt=8))
                wch_ld = (bd.mark("sWA"), bd.mark("sWB"), bd.mark("sG"))
                bd.wait("V", "sT")
                bd.wait("V", "sV")
                bd.v("memset", Sbf[:], 0.0)

                m = {"evac0": bd.mark("sV"), "evac1": bd.mark("sV"),
                     "sig": bd.mark("sS"), "lamS": bd.mark("sS"),
                     "gla_done0": bd.mark("sT"), "gla_done1": bd.mark("sT"),
                     "proj_done": {}, "oevac": bd.mark("sV"), "supd": bd.mark("sV"),
                     "evkg0": bd.mark("sV"), "evkg1": bd.mark("sV"),
                     "mask": bd.mark("sV"), "tcp": bd.mark("sV"), "kmm": {}, "gkmm": {},
                     "ds_ev": bd.mark("sS"), "supdA": bd.mark("sV")}

                def stage_xstat(c):
                    bd.wait("V", "sT", m["proj_done"].get(c - 1, 0))
                    bd.v("tensor_copy", xstat[:].rearrange("p (dt s) -> p dt s", dt=8),
                         xnT[:].rearrange("p (dt c s) -> p dt c s", dt=8, s=128)[:, :, ds(c, 1), :])
                    m["xstat"] = bd.mark("sV")

                def stage_gk_T(c):
                    bd.wait("T", "sWA", wch_ld[0])
                    bd.wait("T", "sWB", wch_ld[1])
                    bd.wait("T", "sG", wch_ld[2])
                    bd.wait("T", "sV", m["xstat"])
                    bd.wait("T", "sS", m["sig"])
                    for dt_ in range(8):
                        bd.mm(pf[0][:, 0:512], xstat[:, dt_ * 128:(dt_ + 1) * 128],
                              wch[:, dt_ * 2048 + 1536:(dt_ + 1) * 2048], start=(dt_ == 0), stop=False)
                    bd.mm(pf[0][:, 0:512], k_or[:], k_bgl[ch][:], start=False, stop=True, inc=1)
                    m["gkmm"][c] = bd.mark("sT")

                def stage_k_T(c):
                    bd.wait("T", "sV", m[f"evkg{1 - (c % 2)}"])
                    for dt_ in range(8):
                        bd.mm(pf[1][:, 0:512], xstat[:, dt_ * 128:(dt_ + 1) * 128],
                              wch[:, dt_ * 2048:dt_ * 2048 + 512], start=(dt_ == 0), stop=(dt_ == 7),
                              inc=(1 if dt_ == 7 else 0))
                    m["kmm"][c] = bd.mark("sT")

                def stage_v_T(c, half):
                    if half == 0:
                        bd.wait("T", "sV", m[f"evac{1 - (c % 2)}"])
                    for dt_ in range(8):
                        bd.mm(pf[2 + half][:, 0:512], xstat[:, dt_ * 128:(dt_ + 1) * 128],
                              wch[:, dt_ * 2048 + 512 + half * 512:dt_ * 2048 + 1024 + half * 512],
                              start=(dt_ == 0), stop=(dt_ == 7), inc=(1 if dt_ == 7 else 0))
                    if half == 1:
                        m["proj_done"][c] = bd.mark("sT")

                def stage_sig_S(c):
                    bd.wait("S", "sT", m["gkmm"][c])
                    bd.wait("S", "sS", m["sig"])
                    bd.wait("S", "sV", m[f"evkg{1 - (c % 2)}"])
                    bd.s("activation", sig1[:], pf[0][:, 0:512], AF.Sigmoid)
                    bd.wait("S", "sS")
                    bd.s("activation", sig2[:], sig1[:], AF.Ln)
                    m["sig"] = bd.mark("sS")

                def stage_evac_kg_V(c):
                    par = c % 2
                    kcb_, vcb_, gkcb_ = kgv[par]
                    bd.wait("V", "sS", m["sig"])
                    bd.wait("V", "sT", m["kmm"][c])
                    bd.wait("V", "sT", m[f"gla_done{par}"])
                    kpc = r_kept[:].rearrange("p (c m) -> p c m", m=4)[:, ds(c, 1), ch:ch + 1] if routed else None
                    if routed:
                        bd.v("tensor_scalar", gkcb_[:], sig2[:], kpc, None, ALU.mult)
                        bd.v("tensor_scalar", kcb_[:], pf[1][:, 0:512], kpc, None, ALU.mult, inc=1)
                    else:
                        bd.v("tensor_copy", gkcb_[:], sig2[:])
                        bd.v("tensor_copy", kcb_[:], pf[1][:, 0:512], inc=1)
                    m[f"evkg{par}"] = bd.mark("sV")

                def stage_evac_v_V(c):
                    par = c % 2
                    kcb_, vcb_, gkcb_ = kgv[par]
                    bd.wait("V", "sT", m["proj_done"][c])
                    bd.v("tensor_copy", vcb_[:, 0:512], pf[2][:, 0:512])
                    bd.v("tensor_copy", vcb_[:, 512:1024], pf[3][:, 0:512], inc=1)
                    m[f"evac{par}"] = bd.mark("sV")

                def stage_gla(c):
                    par = c % 2
                    kcb_, vcb_, gkcb_ = kgv[par]
                    have_next = c + 1 < NCH
                    bd.wait("T", "sV", m[f"evkg{par}"])
                    bd.wait("T", "sV", m[f"evac{par}"])
                    bd.wait("T", "sS", m["lamS"])
                    bd.wait("T", "sS", m["ds_ev"])
                    bd.mm(pf[4][:, 0:512], k_t16[:], gkcb_[:], inc=1)
                    for hh in range(H):
                        bd.mm(pf[5][:, hh:hh + 1], gkcb_[:, hh * 128:(hh + 1) * 128], k_ov16[:],
                              inc=(1 if hh == 3 else 0))
                    gmm = bd.mark("sT")
                    bd.wait("S", "sT", gmm)
                    bd.wait("S", "sV", m["supd"])
                    bd.s("activation", lam[:], pf[4][:, 0:512], AF.Exp)
                    bd.s("activation", nlam[:], pf[4][:, 0:512], AF.Exp, scale=-1.0)
                    bd.s("activation", decay[:], pf[5][:, 0:4], AF.Exp)
                    m["lamS"] = bd.mark("sS")
                    if have_next:
                        stage_xstat(c + 1)
                        stage_gk_T(c + 1)
                        stage_sig_S(c + 1)
                    bd.wait("V", "sS", m["lamS"])
                    bd.wait("V", "sV", m["tcp"])
                    bd.wait("V", "sT", m[f"gla_done{1 - (c % 2)}"])
                    if routed:
                        bd.v("scalar_tensor_tensor", qt[:], q_b[:].rearrange("p (c x) -> p c x", x=512)[:, ds(c, 1), :],
                             r_wm[:].rearrange("p (c m) -> p c m", m=4)[:, ds(c, 1), ch:ch + 1],
                             lam[:], ALU.mult, ALU.mult)
                    else:
                        bd.v("scalar_tensor_tensor", qt[:], q_b[:].rearrange("p (c x) -> p c x", x=512)[:, ds(c, 1), :],
                             SCALE, lam[:], ALU.mult, ALU.mult)
                    bd.v("tensor_tensor", kt[:], kcb_[:], nlam[:], ALU.mult, inc=1)
                    m["qtkt"] = bd.mark("sV")
                    bd.wait("T", "sV", m["qtkt"])
                    for hh in range(H):
                        bd.tr(pb[:, hh * 128:(hh + 1) * 128], qt[:, hh * 128:(hh + 1) * 128], k_id[:])
                        bd.tr(pb[:, 512 + hh * 128:512 + (hh + 1) * 128], kt[:, hh * 128:(hh + 1) * 128],
                              k_id[:], inc=(1 if hh == 3 else 0))
                    trm = bd.mark("sT")
                    if have_next:
                        stage_k_T(c + 1)
                    bd.wait("V", "sT", trm)
                    bd.v("tensor_copy", qtT[:], pb[:, 0:512])
                    bd.v("tensor_copy", ktT[:], pb[:, 512:1024], inc=1)
                    m["tcp"] = bd.mark("sV")
                    bd.wait("T", "sV", m["tcp"])
                    bd.wait("T", "sV", m["oevac"])
                    for hh in range(H):
                        bd.mm(pf[6][:, hh * 128:(hh + 1) * 128], ktT[:, hh * 128:(hh + 1) * 128],
                              qtT[:, hh * 128:(hh + 1) * 128], inc=(1 if hh == 3 else 0))
                    atm = bd.mark("sT")
                    if have_next:
                        stage_v_T(c + 1, 0)
                        stage_evac_kg_V(c + 1)
                    bd.wait("V", "sT", atm)
                    bd.v("tensor_tensor", att[:], pf[6][:, 0:512], k_m4[:], ALU.mult, inc=1)
                    m["mask"] = bd.mark("sV")
                    par_s = c % 2
                    scur = Sbf[:, par_s * 1024:(par_s + 1) * 1024]
                    snew = Sbf[:, (1 - par_s) * 1024:(2 - par_s) * 1024]
                    bd.wait("T", "sV", m["mask"])
                    bd.wait("T", "sS", m["lamS"])
                    for hh in range(2):
                        bd.mm(pf[6][:, hh * 256:(hh + 1) * 256], att[:, hh * 128:(hh + 1) * 128],
                              vcb_[:, hh * 256:(hh + 1) * 256], start=True, stop=False)
                        bd.mm(pf[6][:, hh * 256:(hh + 1) * 256], qtT[:, hh * 128:(hh + 1) * 128],
                              scur[:, hh * 256:(hh + 1) * 256], start=False, stop=True,
                              inc=(1 if hh == 1 else 0))
                    o01 = bd.mark("sT")
                    for hh in range(2):
                        bd.mm(pf[4][:, hh * 256:(hh + 1) * 256], kt[:, hh * 128:(hh + 1) * 128],
                              vcb_[:, hh * 256:(hh + 1) * 256], inc=(1 if hh == 1 else 0))
                    dsa = bd.mark("sT")
                    if have_next:
                        stage_v_T(c + 1, 1)
                    bd.wait("V", "sT", o01)
                    oview = oac[:].rearrange("p (c h x) -> p c h x", h=2, x=512)
                    if ch == 0:
                        bd.v("tensor_copy", oview[:, ds(c, 1), 0, :], pf[6][:, 0:512], inc=1)
                    else:
                        bd.v("tensor_tensor", oview[:, ds(c, 1), 0, :], pf[6][:, 0:512],
                             oview[:, ds(c, 1), 0, :], ALU.add, inc=1)
                    m["oevac01"] = bd.mark("sV")
                    bd.wait("T", "sV", m["oevac01"])
                    for hh in range(2, 4):
                        bd.mm(pf[6][:, (hh - 2) * 256:(hh - 1) * 256], att[:, hh * 128:(hh + 1) * 128],
                              vcb_[:, hh * 256:(hh + 1) * 256], start=True, stop=False)
                        bd.mm(pf[6][:, (hh - 2) * 256:(hh - 1) * 256], qtT[:, hh * 128:(hh + 1) * 128],
                              scur[:, hh * 256:(hh + 1) * 256], start=False, stop=True,
                              inc=(1 if hh == 3 else 0))
                    o23 = bd.mark("sT")
                    for hh in range(2, 4):
                        bd.mm(pf[5][:, (hh - 2) * 256:(hh - 1) * 256], kt[:, hh * 128:(hh + 1) * 128],
                              vcb_[:, hh * 256:(hh + 1) * 256], inc=(1 if hh == 3 else 0))
                    dsb = bd.mark("sT")
                    m[f"gla_done{par}"] = bd.mark("sT")
                    bd.wait("S", "sT", dsa)
                    bd.wait("S", "sV", m["supdA"])
                    bd.s("copy", t1s[:], pf[4][:, 0:512])
                    ds_ev1 = bd.mark("sS")
                    bd.wait("V", "sS", ds_ev1)
                    bd.v("tensor_add", t1[:], t1s[:], scur[:, 0:512])
                    bd.wait("V", "sV")
                    for hh in range(2):
                        bd.v("tensor_scalar", snew[:, hh * 256:(hh + 1) * 256],
                             t1[:, hh * 256:(hh + 1) * 256], decay[:, hh:hh + 1], None, ALU.mult)
                    bd.wait("V", "sT", o23)
                    if ch == 0:
                        bd.v("tensor_copy", oview[:, ds(c, 1), 1, :], pf[6][:, 0:512], inc=1)
                    else:
                        bd.v("tensor_tensor", oview[:, ds(c, 1), 1, :], pf[6][:, 0:512],
                             oview[:, ds(c, 1), 1, :], ALU.add, inc=1)
                    m["oevac"] = bd.mark("sV")
                    if have_next:
                        stage_evac_v_V(c + 1)
                    bd.wait("S", "sT", dsb)
                    bd.s("copy", t1bs[:], pf[5][:, 0:512])
                    m["ds_ev"] = bd.mark("sS")
                    bd.wait("V", "sS", m["ds_ev"])
                    bd.v("tensor_add", t1b[:], t1bs[:], scur[:, 512:1024])
                    bd.wait("V", "sV")
                    for hh in range(2, 4):
                        bd.v("tensor_scalar", snew[:, hh * 256:(hh + 1) * 256],
                             t1b[:, (hh - 2) * 256:(hh - 1) * 256], decay[:, hh:hh + 1], None, ALU.mult,
                             inc=(1 if hh == 3 else 0))
                    m["supd"] = bd.mark("sV")
                    m["supdA"] = m["supd"]

                stage_xstat(0)
                stage_gk_T(0)
                stage_sig_S(0)
                stage_k_T(0)
                stage_evac_kg_V(0)
                stage_v_T(0, 0)
                stage_v_T(0, 1)
                stage_evac_v_V(0)
                for c in range(NCH):
                    stage_gla(c)
                last_wch_use[0] = m["proj_done"][NCH - 1]

            bd.wait("G", "sV")
            bd.gdma(oacc_sc.rearrange("p (b x) -> p b x", x=NCH * DV)[:, ds(b, 1), :], oac[:])

        bd.loop("AB_batch", B, phase_ab)

    def describe_p2(bd: Bd):
        xnT2 = p2v["xnT2"]; gwb = p2v["gwb"]; uwb = p2v["uwb"]
        dwb = p2v["dwb"]; yg = p2v["yg"]; sgt = p2v["sgt"]; sgt2 = p2v["sgt2"]; facc = p2v["facc"]
        oat = p2v["oat"]; x_t = p2v["x_t"]; sq_t = p2v["sq_t"]; ssum = p2v["ssum"]
        rstd4 = p2v["rstd4"]; on_t = p2v["on_t"]
        og_t = p2v["on_t"]; ogT = p2v["ogT"]; outsb = p2v["outsb"]
        dispsb = p2v["dispsb"]; widx1 = p2v["widx1"]; widx2 = p2v["widx2"]; eidb = p2v["eidb"]
        cnt64sb = p2v["cnt64sb"]; fin16 = p2v["fin16"]; fb2 = p2v["fb2"]
        p8sb = p2v["p8sb"]; pbs = p2v["pbs"]; fbTsb = p2v["fbTsb"]; fbrow = p2v["fbrow"]
        eidu = p2v["eidu"]; eidf = p2v["eidf"]
        ts512 = p2v["ts512"]; fin8b = p2v["fin8b"]; posf = p2v["posf"]
        posu = p2v["posu"]; rankm_sb = p2v["rankm_sb"]; valsb = p2v["valsb"]
        pattbig = p2v["pattbig"]

        pbv = pattbig[:].rearrange("p (r v) -> p r v", v=4)
        bd.wait("V", "sG")
        bd.wait("V", "sY")
        bd.wait("S", "sG")
        bd.wait("S", "sY")
        bd.wait("V", "sV")
        bd.v("memset", pbv[:, :, 0:1], float(T_TOK))
        bd.v("memset", pbv[:, :, 1:2], 0.0)
        bd.v("memset", pbv[:, :, 2:3], BIG)
        bd.v("memset", pbv[:, :, 3:4], 0.0)
        bd.wait("Y", "sV")
        bd.ydma(disp.rearrange("(p r) v -> p (r v)", p=128), pattbig[:])
        bd.ydma(cnt64sb[:], cnt_sc)
        bd.ydma(rankm_sb[:], rank_sc)
        bd.ydma(valsb[:], vals_sc)
        bd.wait("T", "sY")
        bd.mm(pf[0][0:NE, 0:NGC], cnt64sb[:], k_tS[0:NGC, 0:NGC], inc=1)
        bd.mm(pf[1][0:NE, 0:1], cnt64sb[:], k_ocf[0:NGC, 0:1], inc=1)
        bd.wait("V", "sT")
        bd.v("tensor_copy", pbs[:], pf[1][0:NE, 0:1])
        bd.wait("V", "sV")
        bd.v("tensor_scalar", fin16[0:NE, :], k_th[0:NE, :], pbs[:, 0:1], None, ALU.is_lt)
        bd.wait("V", "sV")
        bd.v("tensor_reduce", p8sb[:], fin16[0:NE, 0:16], AX.X, ALU.add)
        bd.wait("V", "sV")
        bd.v("tensor_scalar", p8sb[:], p8sb[:], 512.0, None, ALU.mult)
        bd.wait("T", "sV")
        bd.mm(pf[2][0:NE, 0:1], k_tS[0:NE, 0:NE], p8sb[:], inc=1)
        bd.wait("V", "sT")
        bd.v("tensor_copy", pbs[:], pf[2][0:NE, 0:1])
        bd.wait("V", "sV")
        bd.v("tensor_scalar", fb2[:, 0:NGC], pf[0][0:NE, 0:NGC], pbs[:, 0:1], None, ALU.add)
        bd.wait("V", "sV")
        bd.v("tensor_scalar", fb2[:, NGC:NGC + 1], p8sb[:], pbs[:, 0:1], None, ALU.add)
        bd.wait("T", "sV")
        bd.tr(pf[3][0:NGC + 1, 0:NE], fb2[0:NE, 0:NGC + 1], k_idf[0:NE, 0:NE], inc=1)
        bd.wait("V", "sT")
        bd.v("tensor_copy", fbTsb[:], pf[3][0:NGC + 1, 0:NE])
        bd.wait("Y", "sV")
        bd.ydma(fb_sc.rearrange("o (g e) -> (o g) e", e=NE), fbTsb[:])
        bd.ydma(fbrow[:], fb_sc)
        bd.wait("T", "sY")
        bd.mm(pf[4][:, 0:NE], k_orf[:], fbrow[0:1, NGC * NE:(NGC + 1) * NE], inc=1)
        bd.wait("V", "sV")
        bd.v("tensor_scalar", ts512[:], k_pidx[:], 512.0, None, ALU.mult)
        bd.wait("V", "sT")
        bd.wait("V", "sV")
        bd.v("tensor_scalar", fin8b[:], pf[4][:, 0:NE], ts512[:, 0:1], None, ALU.is_le)
        bd.wait("V", "sV")
        bd.v("tensor_reduce", eidf[:], fin8b[:], AX.X, ALU.add)
        bd.wait("V", "sV")
        bd.v("tensor_scalar", eidf[:], eidf[:], 7.0, None, ALU.min)
        bd.wait("V", "sV")
        bd.v("tensor_copy", eidu[:], eidf[:])
        bd.wait("Y", "sV")
        bd.ydma(eidf_sc.rearrange("o (t x) -> (o t) x", x=1), eidf[0:NTILE, 0:1])
        bd.ydma(p2v["eidrowF"][:], eidf_sc)
        bd.wait("T", "sY")
        bd.mm(pf[4][:, 0:NTILE], k_orf[:], p2v["eidrowF"][0:1, 0:NTILE], inc=1)
        bd.wait("V", "sT")
        bd.v("tensor_copy", eidb[:], pf[4][:, 0:NTILE])
        bd.wait("V", "sV")
        bd.v("tensor_scalar", p2v["widxf"][:], eidb[:], 1024.0, k_pidx[:, 0:1], ALU.mult, ALU.add)
        bd.wait("V", "sV")
        bd.v("tensor_copy", widx1[:], p2v["widxf"][:])
        bd.wait("V", "sV")
        bd.v("tensor_scalar", p2v["widxf"][:], eidb[:], 2048.0, k_pidx[:, 0:1], ALU.mult, ALU.add)
        bd.wait("V", "sV")
        bd.v("tensor_copy", widx2[:], p2v["widxf"][:])
        for b_ in range(1, B):
            vbv = valsb[:].rearrange("p (b x v) -> p b x v", b=B, v=4)
            bd.wait("V", "sV")
            bd.v("tensor_scalar", vbv[:, b_, :, 0:1], vbv[:, b_, :, 0:1],
                 float(b_ * S), None, ALU.add)
            bd.wait("V", "sV")
            bd.v("tensor_scalar", vbv[:, b_, :, 2:3], vbv[:, b_, :, 2:3],
                 float(2 * b_ * S), None, ALU.add)
        pos_m = {}
        for g in range(NGC):
            bk = pf[5] if g % 2 == 0 else pf[6]
            if g >= 2:
                bd.wait("T", "sV", pos_m[g - 2])
            else:
                bd.wait("T", "sV")
            bd.mm(bk[:, 0:NE], k_orf[:], fbrow[0:1, g * NE:(g + 1) * NE], inc=1)
            bd.wait("V", "sT")
            bd.wait("V", "sV")
            bd.v("tensor_tensor", posf[:], bk[:, 0:NE], rankm_sb[:, g * 8:(g + 1) * 8], ALU.add)
            bd.wait("V", "sV")
            bd.v("tensor_copy", posu[:, g * 8:(g + 1) * 8], posf[:])
            pos_m[g] = bd.mark("sV")
        bd.wait("G", "sY")
        bd.wait("G", "sV")
        for g in range(NGC):
            for e in range(NE):
                bd.gind(disp, posu[:, g * 8 + e:g * 8 + e + 1],
                        valsb[:, g * 32 + e * 4:g * 32 + e * 4 + 4], None, NSLOT - 1)
        disp_scat_done = bd.mark("sGD")

        def phase_shared(b):
            bd.wait("S", "sT")
            bd.sdma(xnT2[:], xnT_sc.rearrange("p (b x) -> p b x", x=SP8)[:, ds(b, 1), :],
                    sem="sGC")
            xld = bd.mark("sGC")
            bd.wait("T", "sGC", xld)
            gT, uT, dT = fsh_gT, fsh_uT, fsh_dT
            psem = ["sWA", "sWB"]
            wa0 = bd.mark("sWA")
            wb0 = bd.mark("sWB")
            mk = {"dwb_use": bd.mark("sT"), "guse0": bd.mark("sT"),
                  "guse1": bd.mark("sT"), "down": bd.mark("sT"),
                  "silu0": bd.mark("sS"), "silu1": bd.mark("sS"),
                  "mult0": bd.mark("sV"), "mult1": bd.mark("sV"),
                  "evac": bd.mark("sV"), "evac0d": bd.mark("sV"),
                  "evac1d": bd.mark("sV"), "gu_par": 0, "dn_par": 0}
            bd.wait("S", "sT", mk["dwb_use"])
            bd.sdma(dwb[:].rearrange("p (it h) -> p it h", it=16),
                    dT.rearrange("(it p) h -> p it h", p=128), sem="sGC")
            mk["dwb_ld"] = bd.mark("sGC")
            bd.wait("V", "sGC")
            for qr in range(4):
                for g8 in range(2):
                    par = g8 % 2
                    sw = psem[par]
                    if qr == 0:
                        ld_mark = wa0 if par == 0 else wb0
                    else:
                        bd.wait("Y", "sT", mk.get(f"guse{par}", 0))
                        bd.ydma(gwb[:, par * 2048:(par + 1) * 2048].rearrange("p (dt i) -> p dt i", dt=8),
                                gT[:, qr * 512 + g8 * 256: qr * 512 + (g8 + 1) * 256]
                                .rearrange("(dt p) i -> p dt i", p=128), sem=sw)
                        bd.ydma(uwb[:, par * 2048:(par + 1) * 2048].rearrange("p (dt i) -> p dt i", dt=8),
                                uT[:, qr * 512 + g8 * 256: qr * 512 + (g8 + 1) * 256]
                                .rearrange("(dt p) i -> p dt i", p=128), sem=sw)
                        ld_mark = bd.mark(sw)
                    for il in range(2):
                        islq = g8 * 2 + il
                        for tau in range(NT):
                            gp = mk.setdefault("gu_par", 0)
                            pg, pu = (pf[0], pf[1]) if gp == 0 else (pf[2], pf[3])
                            sgp = sgt[:, 0:512] if gp == 0 else sgt2[:, 0:512]
                            bd.wait("T", sw, ld_mark)
                            bd.wait("T", "sS", mk.get(f"silu{gp}", 0))
                            bd.wait("T", "sV", mk.get(f"mult{gp}", 0))
                            if gp == 1:
                                bd.wait("T", "sV", mk.get("evac", 0))
                            for dt_ in range(8):
                                bd.mm(pg[:, 0:512],
                                      gwb[:, par * 2048 + dt_ * 256 + il * 128:
                                          par * 2048 + dt_ * 256 + (il + 1) * 128],
                                      xnT2[:, dt_ * S + tau * 512: dt_ * S + (tau + 1) * 512],
                                      start=(dt_ == 0), stop=(dt_ == 7),
                                      inc=(1 if dt_ == 7 else 0))
                            gate_mark = bd.mark("sT")
                            for dt_ in range(8):
                                bd.mm(pu[:, 0:512],
                                      uwb[:, par * 2048 + dt_ * 256 + il * 128:
                                          par * 2048 + dt_ * 256 + (il + 1) * 128],
                                      xnT2[:, dt_ * S + tau * 512: dt_ * S + (tau + 1) * 512],
                                      start=(dt_ == 0), stop=(dt_ == 7),
                                      inc=(1 if dt_ == 7 else 0))
                            mk[f"guse{par}"] = bd.mark("sT")
                            bd.wait("S", "sT", gate_mark)
                            bd.wait("S", "sV", mk.get(f"mult{gp}", 0))
                            bd.s("activation", sgp, pg[:, 0:512], AF.Silu)
                            mk[f"silu{gp}"] = bd.mark("sS")
                            bd.wait("V", "sS", mk[f"silu{gp}"])
                            bd.wait("V", "sT", mk[f"guse{par}"])
                            bd.wait("V", "sT", mk.get("down", 0))
                            bd.v("tensor_tensor", yg[:, (islq * NT + tau) * 512:(islq * NT + tau + 1) * 512],
                                 sgp, pu[:, 0:512], ALU.mult)
                            mk[f"mult{gp}"] = bd.mark("sV")
                            if g8 == 1 and il == 1:
                                mk[f"multtau{tau}"] = bd.mark("sV")
                            mk["gu_par"] = 1 - gp
                bd.wait("T", "sGC", mk["dwb_ld"])
                first_dn = [True]
                for hh2 in range(2):
                    for tau in range(NT):
                        bd.wait("T", "sV", mk.get(f"multtau{tau}", 0))
                        for tp in range(2):
                            dp = mk.setdefault("dn_par", 0)
                            bank = 2 if dp == 0 else 4
                            if first_dn[0] and dp == 0:
                                bd.wait("T", "sS", mk.get("silu1", 0))
                                bd.wait("T", "sV", mk.get("mult1", 0))
                                first_dn[0] = False
                            bd.wait("T", "sV", mk.get(f"evac{dp}d", 0))
                            for islq in range(4):
                                for t2 in range(2):
                                    ts_ = tp * 2 + t2
                                    isl = qr * 4 + islq
                                    bd.mm(pf[bank + t2][:, 0:512],
                                          yg[:, (islq * NT + tau) * 512 + ts_ * 128:
                                             (islq * NT + tau) * 512 + (ts_ + 1) * 128],
                                          dwb[:, isl * 1024 + hh2 * 512:isl * 1024 + (hh2 + 1) * 512],
                                          start=(islq == 0), stop=(islq == 3),
                                          inc=(1 if (islq == 3 and t2 == 1) else 0))
                            mk["down"] = bd.mark("sT")
                            bd.wait("V", "sT", mk["down"])
                            for t2 in range(2):
                                ts_ = tp * 2 + t2
                                dst = facc[:, (tau * 4 + ts_) * 1024 + hh2 * 512:
                                            (tau * 4 + ts_) * 1024 + (hh2 + 1) * 512]
                                if qr == 0:
                                    bd.v("tensor_copy", dst, pf[bank + t2][:, 0:512],
                                         inc=(1 if t2 == 1 else 0))
                                else:
                                    bd.v("tensor_tensor", dst, pf[bank + t2][:, 0:512], dst, ALU.add,
                                         inc=(1 if t2 == 1 else 0))
                            mk[f"evac{dp}d"] = bd.mark("sV")
                            mk["evac"] = mk[f"evac{dp}d"]
                            mk["dn_par"] = 1 - dp
                mk["dwb_use"] = mk["down"]
                if qr == 3:
                    for g8 in range(2):
                        sw = psem[g8 % 2]
                        bd.wait("Y", "sT", mk.get(f"guse{g8 % 2}", 0))
                        bd.ydma(gwb[:, (g8 % 2) * 2048:(g8 % 2 + 1) * 2048].rearrange("p (dt i) -> p dt i", dt=8),
                                fsh_gT[:, g8 * 256:(g8 + 1) * 256].rearrange("(dt p) i -> p dt i", p=128),
                                sem=sw)
                        bd.ydma(uwb[:, (g8 % 2) * 2048:(g8 % 2 + 1) * 2048].rearrange("p (dt i) -> p dt i", dt=8),
                                fsh_uT[:, g8 * 256:(g8 + 1) * 256].rearrange("(dt p) i -> p dt i", p=128),
                                sem=sw)
            bd.wait("S", "sV", mk["evac"])
            bd.sdma(facc_sc.rearrange("p (b x) -> p b x", x=NCH * DV)[:, ds(b, 1), :], facc[:],
                    sem="sGC")
            bd.wait("S", "sGC")

        for g8 in range(2):
            sw = "sWA" if g8 == 0 else "sWB"
            bd.wait("Y", "sT")
            bd.ydma(gwb[:, g8 * 2048:(g8 + 1) * 2048].rearrange("p (dt i) -> p dt i", dt=8),
                    fsh_gT[:, g8 * 256:(g8 + 1) * 256].rearrange("(dt p) i -> p dt i", p=128),
                    sem=sw)
            bd.ydma(uwb[:, g8 * 2048:(g8 + 1) * 2048].rearrange("p (dt i) -> p dt i", dt=8),
                    fsh_uT[:, g8 * 256:(g8 + 1) * 256].rearrange("(dt p) i -> p dt i", p=128),
                    sem=sw)

        bd.loop("P2_shared", B, phase_shared)

        XG_OF = [0, 4096]
        XGT_OF = 8192
        CMB_OF = 12288
        dispv = dispsb
        tm = {"cast": {}, "gath": {}, "trd": {}, "dnd": {}, "dwb_ld": {}, "dsb": {},
              "wdone": {}, "scatlast": {}, "guall": {}, "wg_ld": {}}
        gu = {"par": 0, "silu0": bd.mark("sS"), "silu1": bd.mark("sS"),
              "mult0": bd.mark("sV"), "mult1": bd.mark("sV"),
              "guse0": bd.mark("sT"), "guse1": bd.mark("sT")}
        dnm = {"par": 0, "ev0": bd.mark("sV"), "ev1": bd.mark("sV"),
               "down": bd.mark("sT"), "scat0": bd.mark("sGD"), "scat1": bd.mark("sGD")}
        bd.wait("G", "sGD")
        bd.wait("G", "sT")
        bd.wait("G", "sY")
        bd.wait("G", "sGC")
        bd.wait("S", "sY")

        fgate_rows = fgate.rearrange("e d i -> (e d) i")
        fup_rows = fup.rearrange("e d i -> (e d) i")
        fdown_rows = fdown.rearrange("e i h -> (e i) h")

        def tile_fetch(t):
            if t >= NTILE:
                return
            par = t % 2
            bd.wait("G", "sV", tm["wdone"].get(t - 2, 0))
            bd.gdma(dispv[:, par * 16:(par + 1) * 16],
                    disp[t * 512:(t + 1) * 512, :].rearrange("(p r) v -> p (r v)", p=128))
            tm["dsb"][t] = bd.mark("sG")
            bd.wait("V", "sG", tm["dsb"][t])
            bd.wait("V", "sGD", tm["scatlast"].get(t - 2, disp_scat_done))
            dvv = dispv[:, par * 16:(par + 1) * 16].rearrange("p (r v) -> p r v", v=4)
            bd.wait("V", "sV")
            bd.v("tensor_copy", posu[:, par * 16:par * 16 + 4], dvv[:, :, 0])
            bd.v("tensor_copy", posu[:, par * 16 + 4:par * 16 + 8], dvv[:, :, 2])
            tm["cast"][t] = bd.mark("sV")
            bd.wait("G", "sV", tm["cast"][t])
            bd.wait("G", "sT", tm["trd"].get(t - 2, 0))
            for r in range(4):
                bd.gind(xnT2[:, XG_OF[par] + r * 1024:XG_OF[par] + (r + 1) * 1024], None,
                        xn_tok, posu[:, par * 16 + r:par * 16 + r + 1], T_TOK)
            tm["gath"][t] = bd.mark("sGD")

        def wg_fetch(t):
            if t >= NTILE:
                return
            bd.wait("G", "sT", tm["guall"].get(t - 1, 0))
            for j in range(8):
                bd.gind(gwb[:, j * FI:(j + 1) * FI], None, fgate_rows,
                        widx1[:, t:t + 1], NE * 1024 - 1, element_offset=j * 128 * FI)
            for j in range(8):
                bd.gind(uwb[:, j * FI:(j + 1) * FI], None, fup_rows,
                        widx1[:, t:t + 1], NE * 1024 - 1, element_offset=j * 128 * FI)
            tm["wg_ld"][t] = bd.mark("sGD")

        def dwb_fetch(t):
            if t >= NTILE:
                return
            bd.wait("G", "sT", tm["dnd"].get(t - 1, 0))
            for j in range(16):
                bd.gind(dwb[:, j * 1024:(j + 1) * 1024], None, fdown_rows,
                        widx2[:, t:t + 1], NE * FI - 1, element_offset=j * 128 * 1024)
            tm["dwb_ld"][t] = bd.mark("sGD")

        def tile_compute(t):
            par = t % 2
            prev_ev = 0
            for r in range(4):
                bd.wait("T", "sGD", tm["gath"][t])
                bd.wait("T", "sV", prev_ev)
                for dt_ in range(8):
                    bd.tr(pb[:, dt_ * 128:(dt_ + 1) * 128],
                          xnT2[:, XG_OF[par] + r * 1024 + dt_ * 128:XG_OF[par] + r * 1024 + (dt_ + 1) * 128],
                          k_id[:], inc=(1 if dt_ == 7 else 0))
                trm = bd.mark("sT")
                bd.wait("V", "sT", trm)
                for dt_ in range(8):
                    bd.v("tensor_copy",
                         xnT2[:, XGT_OF + dt_ * 512 + r * 128:XGT_OF + dt_ * 512 + (r + 1) * 128],
                         pb[:, dt_ * 128:(dt_ + 1) * 128], inc=(1 if dt_ == 7 else 0))
                prev_ev = bd.mark("sV")
            tm["trd"][t] = bd.mark("sT")
            xgev_all = prev_ev
            for qr in range(4):
                for g8 in range(2):
                    qg = qr * 2 + g8
                    for il in range(2):
                        isl = qr * 4 + g8 * 2 + il
                        gp = gu["par"]
                        pg, pu = (pf[0], pf[1]) if gp == 0 else (pf[2], pf[3])
                        sgp = sgt[:, 0:512] if gp == 0 else sgt2[:, 0:512]
                        bd.wait("T", "sGD", tm["wg_ld"][t])
                        bd.wait("T", "sV", xgev_all)
                        bd.wait("T", "sS", gu[f"silu{gp}"])
                        bd.wait("T", "sV", gu[f"mult{gp}"])
                        for dt_ in range(8):
                            bd.mm(pg[:, 0:512],
                                  gwb[:, dt_ * FI + qg * 256 + il * 128:
                                      dt_ * FI + qg * 256 + (il + 1) * 128],
                                  xnT2[:, XGT_OF + dt_ * 512:XGT_OF + (dt_ + 1) * 512],
                                  start=(dt_ == 0), stop=(dt_ == 7), inc=(1 if dt_ == 7 else 0))
                        gate_mark = bd.mark("sT")
                        for dt_ in range(8):
                            bd.mm(pu[:, 0:512],
                                  uwb[:, dt_ * FI + qg * 256 + il * 128:
                                      dt_ * FI + qg * 256 + (il + 1) * 128],
                                  xnT2[:, XGT_OF + dt_ * 512:XGT_OF + (dt_ + 1) * 512],
                                  start=(dt_ == 0), stop=(dt_ == 7), inc=(1 if dt_ == 7 else 0))
                        bd.wait("S", "sT", gate_mark)
                        bd.wait("S", "sV", gu[f"mult{gp}"])
                        bd.s("activation", sgp, pg[:, 0:512], AF.Silu)
                        gu[f"silu{gp}"] = bd.mark("sS")
                        bd.wait("V", "sS", gu[f"silu{gp}"])
                        bd.wait("V", "sT", bd.mark("sT"))
                        bd.wait("V", "sT", dnm["down"])
                        bd.v("tensor_tensor", yg[:, isl * 512:(isl + 1) * 512], sgp, pu[:, 0:512],
                             ALU.mult)
                        gu[f"mult{gp}"] = bd.mark("sV")
                        gu["par"] = 1 - gp
            tm["guall"][t] = bd.mark("sT")
            wg_fetch(t + 1)
            yg_done = bd.mark("sV")
            bd.wait("T", "sGD", tm["dwb_ld"][t])
            bd.wait("T", "sV", yg_done)
            for q in range(4):
                cpar = q % 2
                for hh2 in range(2):
                    dp = dnm["par"]
                    bank = pf[4] if dp == 0 else pf[5]
                    bd.wait("T", "sV", dnm[f"ev{dp}"])
                    for isl in range(16):
                        bd.mm(bank[:, 0:512],
                              yg[:, isl * 512 + q * 128:isl * 512 + (q + 1) * 128],
                              dwb[:, isl * 1024 + hh2 * 512:isl * 1024 + (hh2 + 1) * 512],
                              start=(isl == 0), stop=(isl == 15), inc=(1 if isl == 15 else 0))
                    dnm["down"] = bd.mark("sT")
                    bd.wait("V", "sT", dnm["down"])
                    bd.wait("V", "sGD", dnm[f"scat{cpar}"])
                    wc = dispv[:, par * 16 + q * 4 + 1:par * 16 + q * 4 + 2]
                    bd.v("tensor_scalar",
                         xnT2[:, CMB_OF + cpar * 1024 + hh2 * 512:CMB_OF + cpar * 1024 + (hh2 + 1) * 512],
                         bank[:, 0:512], wc, None, ALU.mult)
                    dnm[f"ev{dp}"] = bd.mark("sV")
                    dnm["par"] = 1 - dp
                bd.wait("G", "sV", dnm[f"ev{1 - dnm['par']}"])
                bd.gind(comb, posu[:, par * 16 + 4 + q:par * 16 + 5 + q],
                        xnT2[:, CMB_OF + cpar * 1024:CMB_OF + (cpar + 1) * 1024], None,
                        2 * T_TOK - 1)
                dnm[f"scat{cpar}"] = bd.mark("sGD")
            tm["scatlast"][t] = bd.mark("sGD")
            tm["wdone"][t] = dnm[f"ev{1 - dnm['par']}"]
            tm["dnd"][t] = bd.mark("sT")
            dwb_fetch(t + 1)

        bd.wait("G", "sV")
        tile_fetch(0)
        wg_fetch(0)
        dwb_fetch(0)
        tile_fetch(1)
        for t in range(NTILE):
            tile_compute(t)
            tile_fetch(t + 2)

        gwbF2 = gwb.bitcast(F32)
        uwbF2 = uwb.bitcast(F32)
        tl = {
            "outsb": [gwbF2[:, 0:1024], gwbF2[:, 1024:2048]],
            "x": [gwbF2[:, 2048:3072], gwbF2[:, 3072:4096]],
            "oat": [gwb[:, 8192:9216], gwb[:, 9216:10240]],
            "xst": [gwb[:, 10240:11264], gwb[:, 11264:12288]],
            "on": [gwb[:, 12288:13312], gwb[:, 13312:14336]],
            "og": [gwb[:, 14336:15360], gwb[:, 15360:16384]],
            "facc": [uwb[:, 0:1024], uwb[:, 1024:2048]],
            "comb": [uwb[:, 2048:4096], uwb[:, 4096:6144]],
            "ogT": [uwb[:, 6144:7168], uwb[:, 7168:8192]],
            "sq": [uwbF2[:, 4096:4352], uwbF2[:, 4352:4608]],
            "ssum": [uwbF2[:, 4608:4612], uwbF2[:, 4612:4616]],
            "rstd": [uwbF2[:, 4616:4620], uwbF2[:, 4620:4624]],
            "sgt": [sgt[:, 0:512], sgt2[:, 0:512]],
        }
        bd.wait("G", "sT")
        bd.wait("G", "sGD")
        bd.gdma(dwb[:, 0:8 * 1024].rearrange("p (dt v) -> p dt v", dt=8),
                w_g.rearrange("(dt p) v -> p dt v", p=128))
        bd.gdma(dwb[:, 8 * 1024:16 * 1024].rearrange("p (vt d) -> p vt d", vt=8),
                w_o.rearrange("(vt p) d -> p vt d", p=128))
        wgwo_ld = bd.mark("sG")
        tmk = {"yld": {}, "gld": {}, "use": {}, "gmm": {}, "womm": {}, "silu": {},
               "oguse": {}, "outw": {}}
        xnv = xnT_sc.rearrange("p (b dt s k) -> p b dt s k", b=B, dt=8, s=NCH)
        oav = oacc_sc.rearrange("p (b s v) -> p b s v", b=B, s=NCH)
        fav = facc_sc.rearrange("p (b s v) -> p b s v", b=B, s=NCH)
        cmv = comb.rearrange("(b s p k) h -> b s p (k h)", s=NCH, p=128, k=2)
        xsv = xs.rearrange("(b s p) d -> b s p d", s=NCH, p=128)
        ouv = out.rearrange("(b s p) d -> b s p d", s=NCH, p=128)

        def tail_slice(idx):
            b_, s_ = idx % B, idx // B
            ln = idx % 2
            bd.wait("Y", "sV", tmk["use"].get(idx - 2, 0))
            bd.ydma(tl["xst"][ln], xnv[:, b_, :, s_, :], sem="sY")
            bd.ydma(tl["x"][ln], xsv[b_, s_, :, :], sem="sY")
            tmk["yld"][idx] = bd.mark("sY")
            bd.wait("G", "sV", tmk["use"].get(idx - 2, 0))
            bd.gdma(tl["oat"][ln], oav[:, b_, s_, :])
            bd.gdma(tl["facc"][ln], fav[:, b_, s_, :])
            bd.gdma(tl["comb"][ln], cmv[b_, s_, :, :])
            tmk["gld"][idx] = bd.mark("sG")
            b0 = 0 if ln == 0 else 4
            bd.wait("T", "sG", wgwo_ld)
            bd.wait("T", "sY", tmk["yld"][idx])
            bd.wait("T", "sS", tmk["silu"].get(idx - 2, 0))
            for half in range(2):
                for dt_ in range(8):
                    bd.mm(pf[b0 + half][:, 0:512], tl["xst"][ln][:, dt_ * 128:(dt_ + 1) * 128],
                          dwb[:, dt_ * 1024 + half * 512:dt_ * 1024 + (half + 1) * 512],
                          start=(dt_ == 0), stop=(dt_ == 7), inc=(1 if dt_ == 7 else 0))
            tmk["gmm"][idx] = bd.mark("sT")
            bd.wait("S", "sG", tmk["gld"][idx])
            bd.wait("S", "sV", tmk["use"].get(idx - 2, 0))
            for hh in range(H):
                bd.wait("S", "sS")
                bd.s("activation", tl["sq"][ln], tl["oat"][ln][:, hh * 256:(hh + 1) * 256],
                     AF.Square, accum_out=tl["ssum"][ln][:, hh:hh + 1])
            bd.wait("S", "sS")
            bd.s("activation", tl["rstd"][ln], tl["ssum"][ln], AF.Ln, scale=1.0 / HDV, bias=EPS)
            bd.wait("S", "sS")
            bd.s("activation", tl["rstd"][ln], tl["rstd"][ln], AF.Exp, scale=-0.5)
            rsm = bd.mark("sS")
            bd.wait("V", "sS", rsm)
            for hh in range(H):
                bd.v("tensor_scalar", tl["on"][ln][:, hh * 256:(hh + 1) * 256],
                     tl["oat"][ln][:, hh * 256:(hh + 1) * 256],
                     tl["rstd"][ln][:, hh:hh + 1], None, ALU.mult)
            bd.wait("V", "sV")
            bd.v("tensor_mul", tl["on"][ln][:, 0:512], tl["on"][ln][:, 0:512], k_nw[:, 0:512])
            bd.v("tensor_mul", tl["on"][ln][:, 512:1024], tl["on"][ln][:, 512:1024], k_nw[:, 512:1024])
            onm = bd.mark("sV")
            bd.wait("S", "sT", tmk["gmm"][idx])
            bd.wait("S", "sV", tmk["oguse"].get(idx - 2, 0))
            bd.s("activation", tl["sgt"][ln], pf[b0][:, 0:512], AF.Silu)
            s1 = bd.mark("sS")
            bd.wait("V", "sS", s1)
            bd.wait("V", "sV", onm)
            bd.v("tensor_tensor", tl["og"][ln][:, 0:512], tl["on"][ln][:, 0:512], tl["sgt"][ln], ALU.mult)
            bd.wait("S", "sV")
            bd.s("activation", tl["sgt"][ln], pf[b0 + 1][:, 0:512], AF.Silu)
            tmk["silu"][idx] = bd.mark("sS")
            bd.wait("V", "sS", tmk["silu"][idx])
            bd.v("tensor_tensor", tl["og"][ln][:, 512:1024], tl["on"][ln][:, 512:1024], tl["sgt"][ln], ALU.mult)
            ogm = bd.mark("sV")
            bd.wait("T", "sV", ogm)
            bd.wait("T", "sV", tmk["oguse"].get(idx - 1, 0))
            for vt in range(8):
                bd.tr(pb[:, vt * 128:(vt + 1) * 128], tl["og"][ln][:, vt * 128:(vt + 1) * 128],
                      k_id[:], inc=(1 if vt == 7 else 0))
            trm = bd.mark("sT")
            bd.wait("V", "sT", trm)
            bd.v("tensor_copy", tl["ogT"][ln], pb[:])
            tmk["oguse"][idx] = bd.mark("sV")
            bd.wait("T", "sV", tmk["oguse"][idx])
            bd.wait("T", "sV", tmk["outw"].get(idx - 1, 0))
            for half in range(2):
                for vt in range(8):
                    bd.mm(pf[2 + half][:, 0:512], tl["ogT"][ln][:, vt * 128:(vt + 1) * 128],
                          dwb[:, 8 * 1024 + vt * 1024 + half * 512:8 * 1024 + vt * 1024 + (half + 1) * 512],
                          start=(vt == 0), stop=(vt == 7), inc=(1 if vt == 7 else 0))
            tmk["womm"][idx] = bd.mark("sT")
            ob = tl["outsb"][ln]
            bd.wait("V", "sT", tmk["womm"][idx])
            bd.wait("V", "sY", tmk["yld"][idx])
            bd.wait("V", "sG", tmk["gld"][idx])
            for half in range(2):
                bd.v("tensor_tensor", ob[:, half * 512:(half + 1) * 512], pf[2 + half][:, 0:512],
                     tl["x"][ln][:, half * 512:(half + 1) * 512], ALU.add)
            bd.wait("V", "sV")
            for half in range(2):
                bd.v("tensor_tensor", ob[:, half * 512:(half + 1) * 512],
                     ob[:, half * 512:(half + 1) * 512],
                     tl["facc"][ln][:, half * 512:(half + 1) * 512], ALU.add)
            bd.wait("V", "sV")
            for half in range(2):
                bd.v("tensor_tensor", tl["on"][ln][:, half * 512:(half + 1) * 512],
                     tl["comb"][ln][:, half * 512:(half + 1) * 512],
                     tl["comb"][ln][:, 1024 + half * 512:1024 + (half + 1) * 512], ALU.add)
            bd.wait("V", "sV")
            for half in range(2):
                bd.v("tensor_tensor", ob[:, half * 512:(half + 1) * 512],
                     ob[:, half * 512:(half + 1) * 512],
                     tl["on"][ln][:, half * 512:(half + 1) * 512], ALU.add)
            tmk["use"][idx] = bd.mark("sV")
            bd.wait("Y", "sV", tmk["use"][idx])
            bd.ydma(ouv[b_, s_, :, :], ob, sem="sY")
            tmk["outw"][idx] = tmk["use"][idx]

        bd.wait("T", "sV")
        bd.wait("S", "sV")
        bd.wait("Y", "sT")
        bd.wait("Y", "sGD")
        for idx in range(B * NCH):
            tail_slice(idx)

        bd.wait("Y", "sY", bd.mark("sY"))

    bd0 = Bd("C", None, {k: None for k in SEMS}, loop_delta)
    bds = {role: None for role in ROLES}

    def run_role(role, fn):
        def f(eng):
            if bds[role] is None:
                bds[role] = Bd(role, eng, sems, loop_delta)
            else:
                bds[role].eng = eng
            fn(bds[role])
        return f

    with es:
        with es_ab:
            describe_ab(bd0)
            with nc.Block() as block:
                block.tensor(run_role("T", describe_ab))
                block.vector(run_role("V", describe_ab))
                block.scalar(run_role("S", describe_ab))
                block.gpsimd(run_role("G", describe_ab))
                block.sync(run_role("Y", describe_ab))
        with es_p2:
            alloc_p2()
            describe_p2(bd0)
            with nc.Block() as block:
                block.tensor(run_role("T", describe_p2))
                block.vector(run_role("V", describe_p2))
                block.scalar(run_role("S", describe_p2))
                block.gpsimd(run_role("G", describe_p2))
                block.sync(run_role("Y", describe_p2))
    return nc


_CACHE = {}


def _consts():
    ident = np.eye(128, dtype=np.float32)
    t = np.tril(np.ones((128, 128), np.float32)).T
    tri16 = (t / 16.0).astype(np.float32)
    triS = (t - np.eye(128, dtype=np.float32))
    onesr = np.ones((1, 128), np.float32)
    ov16 = np.full((128, 1), 1 / 16.0, np.float32)
    io4 = np.tile(np.arange(4, dtype=np.float32)[None, :], (128, 1))
    io8 = np.tile(np.arange(8, dtype=np.float32)[None, :], (128, 1))
    m4 = np.ascontiguousarray(np.tile(t, (1, 4)))
    misc = np.tile(np.array([[1e-5, -0.5, -1.0, 1.0 / 256.0]], np.float32), (128, 1))
    pidx = np.arange(128, dtype=np.float32).reshape(128, 1)
    th = np.tile((np.arange(16, dtype=np.float32) * 512.0)[None, :], (128, 1))
    return dict(c_id=ident, c_t16=tri16, c_tS=triS, c_or=onesr, c_ov16=ov16,
                c_io4=io4, c_io8=io8, c_m4=m4, c_misc=misc, c_pidx=pidx, c_th=th)


def make_inmap(x_shard, ins, B, S):
    c = np.ascontiguousarray
    m = dict(
        xs=c(x_shard.reshape(B * S, D)),
        w_gate=ins["L0_attn_w_gate"],
        w_q=ins["L0_attn_w_q"],
        w_k=ins["L0_attn_w_k"],
        w_v=ins["L0_attn_w_v"],
        w_gk1T=c(np.asarray(ins["L0_attn_w_gk1"]).transpose(0, 2, 1)),
        w_gk2=ins["L0_attn_w_gk2"],
        b_gk=ins["L0_attn_b_gk"],
        w_ks=ins["L0_attn_w_ks"],
        w_vs=ins["L0_attn_w_vs"],
        w_gk1sT=c(np.asarray(ins["L0_attn_w_gk1s"]).T),
        w_gk2s=ins["L0_attn_w_gk2s"],
        b_gks=np.asarray(ins["L0_attn_b_gks"]).reshape(1, -1),
        w_g=ins["L0_attn_w_g"],
        w_o=ins["L0_attn_w_o"],
        normw4=np.tile(np.asarray(ins["L0_attn_norm_w"]), 4).reshape(1, -1),
        frouterT=c(np.asarray(ins["L0_ffn_router_w"]).T),
        fbias=np.asarray(ins["L0_ffn_routing_bias"]).reshape(1, -1),
    )
    m.update(_consts())
    m = {k: np.ascontiguousarray(np.asarray(v), dtype=np.float32) for k, v in m.items()}
    import ml_dtypes
    bf = lambda a: np.ascontiguousarray(np.asarray(a, dtype=np.float32)).astype(ml_dtypes.bfloat16)
    m["fsh_gT"] = bf(np.asarray(ins["L0_ffn_shared_gate"]).T)
    m["fsh_uT"] = bf(np.asarray(ins["L0_ffn_shared_up"]).T)
    m["fsh_dT"] = bf(np.asarray(ins["L0_ffn_shared_down"]).T)
    m["fgate"] = bf(ins["L0_ffn_gate_w"])
    m["fup"] = bf(ins["L0_ffn_up_w"])
    m["fdown"] = bf(ins["L0_ffn_down_w"])
    return m


def kernel(**inputs):
    NCORES = 8
    x = np.asarray(inputs["L0_attn_x"])
    BT, S = x.shape[0], x.shape[1]
    B = BT // NCORES
    key = (B, S)
    if key not in _CACHE:
        _CACHE[key] = build_program(B, S)
    nc = _CACHE[key]
    in_maps = [make_inmap(x[i * B:(i + 1) * B], inputs, B, S) for i in range(NCORES)]
    res = run_bass_kernel_spmd(nc, in_maps, core_ids=list(range(NCORES)))
    return np.concatenate([r["out"].reshape(B, S, D) for r in res.results], axis=0)


# version 32 (final)
# speedup vs baseline: 1.0117x; 1.0116x over previous
import math
import numpy as np
from contextlib import ExitStack

import concourse.bass as bass
import concourse.mybir as mybir
from concourse.bass_types import DynSlice as ds
from concourse.bass_utils import run_bass_kernel_spmd
from concourse.ordered_engine_set import OrderedEngineSet
from concourse.expressions import smin

F32 = mybir.dt.float32
BF16 = mybir.dt.bfloat16
U32 = mybir.dt.uint32
AF = mybir.ActivationFunctionType
ALU = mybir.AluOpType
AX = mybir.AxisListType

D = 1024
DK = 512
DV = 1024
H = 4
HDK = 128
HDV = 256
NM = 4
NCHAN = 5
LOW = 16
NE = 8
FI = 2048
EPS = 1e-5
SCALE = HDK ** -0.5

ROLES = ("T", "V", "S", "G", "Y")
SEMS = ("sT", "sV", "sS", "sG", "sY", "sWA", "sWB", "sGC", "sGD", "sWC", "sWD")
BIG = float(1 << 24)


class Aff:
    __slots__ = ("c", "m")

    def __init__(self, c=0, m=None):
        self.c = c
        self.m = m or {}

    def add(self, n):
        return Aff(self.c + n, self.m)

    def addreg(self, reg, coef):
        m2 = dict(self.m)
        k = id(reg)
        r0, c0 = m2.get(k, (reg, 0))
        m2[k] = (reg, c0 + coef)
        return Aff(self.c, m2)

    def expr(self):
        e = self.c
        for reg, co in self.m.values():
            if co:
                e = reg * co + e
        return e


class Bd:

    def __init__(self, role, eng, sems, loop_delta):
        self.role = role
        self.eng = eng
        self.sems = sems
        self.cnt = {k: Aff() for k in SEMS}
        self.loop_delta = loop_delta
        self._loop_path = []
        self._r1 = None
        self._r2 = None

    def _scratch(self):
        if self._r1 is None:
            self._r1 = self.eng.alloc_register("bd_w1")
            self._r2 = self.eng.alloc_register("bd_w2")
        return self._r1, self._r2

    def bump(self, sem, n):
        self.cnt[sem] = self.cnt[sem].add(n)

    def mark(self, sem):
        return self.cnt[sem]

    def wait(self, role, sem, val=None):
        if self.role != role:
            return
        v = self.cnt[sem] if val is None else val
        if not isinstance(v, Aff):
            self.eng.wait_ge(self.sems[sem], v)
            return
        terms = [(reg, co) for reg, co in v.m.values() if co]
        if not terms:
            self.eng.wait_ge(self.sems[sem], v.c)
            return
        r1, r2 = self._scratch()
        reg0, co0 = terms[0]
        self.eng.reg_alu(r1, reg0, co0, ALU.mult)
        for reg, co in terms[1:]:
            self.eng.reg_alu(r2, reg, co, ALU.mult)
            self.eng.reg_alu(r1, r1, r2, ALU.add)
        if v.c:
            self.eng.reg_alu(r1, r1, v.c, ALU.add)
        self.eng.wait_ge(self.sems[sem], r1)

    def _fin(self, sem, inc, ins):
        if inc:
            self.bump(sem, inc)
            if ins is not None:
                ins.then_inc(self.sems[sem], inc)

    def mm(self, out, lhsT, rhs, start=True, stop=True, inc=0):
        ins = self.eng.matmul(out, lhsT, rhs, start=start, stop=stop) if self.role == "T" else None
        self._fin("sT", inc, ins)

    def tr(self, out, in_, ident, inc=0):
        ins = self.eng.transpose(out, in_, ident) if self.role == "T" else None
        self._fin("sT", inc, ins)

    def v(self, meth, *args, inc=1, **kw):
        ins = getattr(self.eng, meth)(*args, **kw) if self.role == "V" else None
        self._fin("sV", inc, ins)

    def s(self, meth, *args, inc=1, **kw):
        ins = getattr(self.eng, meth)(*args, **kw) if self.role == "S" else None
        self._fin("sS", inc, ins)

    def gv(self, meth, *args, inc=1, **kw):
        ins = getattr(self.eng, meth)(*args, **kw) if self.role == "G" else None
        self._fin("sGC", inc, ins)

    def gdma(self, out, in_, inc=16, sem="sG", **kw):
        self.wait("G", sem)
        ins = self.eng.dma_start(out=out, in_=in_, **kw) if self.role == "G" else None
        self._fin(sem, inc, ins)

    def ydma(self, out, in_, inc=16, sem="sY", **kw):
        self.wait("Y", sem)
        ins = self.eng.dma_start(out=out, in_=in_, **kw) if self.role == "Y" else None
        self._fin(sem, inc, ins)

    def sdma(self, out, in_, inc=16, sem="sS", **kw):
        self.wait("S", sem)
        ins = self.eng.dma_start(out=out, in_=in_, **kw) if self.role == "S" else None
        self._fin(sem, inc, ins)

    def gind(self, out, out_offset, in_, in_offset, bounds, inc=16, element_offset=0):
        from concourse.bass import IndirectOffsetOnAxis
        ins = None
        if self.role == "G":
            if not hasattr(self, "_bcregs"):
                self._bcregs = {}
            if bounds not in self._bcregs:
                r = self.eng.alloc_register(f"g_bc{len(self._bcregs)}")
                self.eng.reg_mov(r, bounds)
                self._bcregs[bounds] = r
            oo = IndirectOffsetOnAxis(ap=out_offset, axis=0) if out_offset is not None else None
            io = IndirectOffsetOnAxis(ap=in_offset, axis=0) if in_offset is not None else None
            ins = self.eng.indirect_dma_start(
                out=out, out_offset=oo, in_=in_, in_offset=io,
                element_offset=element_offset,
                bounds_check=self._bcregs[bounds], oob_is_err=False)
        self._fin("sGD", inc, ins)

    def eload(self, role, key, ap, maxv, par=0):
        if self.role != role:
            return 0
        nc_ = self.eng.bass
        if not hasattr(self, "_eregs"):
            self._eregs = {}
        kp = (key, par)
        if kp not in self._eregs:
            es_ = OrderedEngineSet([self.eng.engine])
            regs = nc_.alloc_registers(f"el_{self.role}_{key}_{par}", engines=es_)
            rv = nc_.snap(regs, engines=es_, min_val=0, max_val=maxv)
            self._eregs[kp] = (regs, rv)
        regs, rv = self._eregs[kp]
        self.eng.reg_load(regs.handles[0], ap)
        return rv

    def ploop(self, trips, body):
        for i in range(trips):
            body(i)

    def loop(self, key, trips, body):
        key = "/".join(self._loop_path + [key])
        start = dict(self.cnt)
        if self.role == "C":
            self._loop_path.append(key)
            body(0)
            self._loop_path.pop()
            delta = {k: self.cnt[k].c - start[k].c for k in SEMS}
            self.loop_delta[key] = delta
            self.cnt = {k: start[k].add(trips * delta[k]) for k in SEMS}
        else:
            delta = self.loop_delta[key]
            self._loop_path.append(key)
            depth = len(self._loop_path)
            nc_ = self.eng.bass
            engines = OrderedEngineSet([self.eng.engine])
            if not hasattr(self, "_lregs"):
                self._lregs = {}
            if depth not in self._lregs:
                self._lregs[depth] = nc_.alloc_registers(f"bdl_{self.role}_{depth}",
                                                         engines=engines)
            regs = self._lregs[depth]
            lid = nc_.next_id()
            lstart = f"bdl{lid}_s"
            lend = f"bdl{lid}_e"
            nc_.regs_mov(regs, 0)
            nc_.br(lstart, engines=engines)
            with nc_.body(lstart, valid_engines=engines):
                reg = nc_.snap(regs, engines=engines, min_val=0, max_val=trips - 1)
                self.cnt = {k: start[k].addreg(reg, delta[k]) for k in SEMS}
                body(reg)
                for k in SEMS:
                    got = self.cnt[k].c - start[k].c
                    assert got == delta[k], f"loop {key} sem {k}: {got} != {delta[k]}"
                nc_.regs_alu(regs, regs, 1, op=ALU.add)
                nc_.br_lt(regs, trips, on_true=lstart, on_false=lend, engines=engines)
            nc_.switch_bb(lend)
            self._loop_path.pop()
            self.cnt = {k: start[k].add(trips * delta[k]) for k in SEMS}


def build_program(B=4, S=2048):
    NCH = S // 128
    NT = S // 512
    CAP = math.ceil(S * 0.5)
    SP8 = 8 * S

    nc = bass.Bass("TRN2", target_bir_lowering=False)

    A = lambda name, shape, dt=F32: nc.dram_tensor(name, shape, dt, kind="ExternalInput").ap()
    xs = A("xs", [B * S, D])
    w_gate = A("w_gate", [D, NM])
    w_q = A("w_q", [D, DK])
    w_k = A("w_k", [NM, D, DK])
    w_v = A("w_v", [NM, D, DV])
    w_gk1T = A("w_gk1T", [NM, LOW, D])
    w_gk2 = A("w_gk2", [NM, LOW, DK])
    b_gk = A("b_gk", [NM, DK])
    w_ks = A("w_ks", [D, DK])
    w_vs = A("w_vs", [D, DV])
    w_gk1sT = A("w_gk1sT", [LOW, D])
    w_gk2s = A("w_gk2s", [LOW, DK])
    b_gks = A("b_gks", [1, DK])
    w_g = A("w_g", [D, DV])
    w_o = A("w_o", [DV, D])
    normw4 = A("normw4", [1, DV])
    fsh_gT = A("fsh_gT", [D, FI], BF16)
    fsh_uT = A("fsh_uT", [D, FI], BF16)
    fsh_dT = A("fsh_dT", [FI, D], BF16)
    fgate = A("fgate", [NE, D, FI], BF16)
    fup = A("fup", [NE, D, FI], BF16)
    fdown = A("fdown", [NE, FI, D], BF16)
    frouterT = A("frouterT", [D, NE])
    fbias = A("fbias", [1, NE])
    c_id = A("c_id", [128, 128])
    c_t16 = A("c_t16", [128, 128])
    c_tS = A("c_tS", [128, 128])
    c_or = A("c_or", [1, 128])
    c_ov16 = A("c_ov16", [128, 1])
    c_io4 = A("c_io4", [128, 4])
    c_io8 = A("c_io8", [128, 8])
    c_m4 = A("c_m4", [128, 512])
    c_misc = A("c_misc", [128, 4])

    out = nc.dram_tensor("out", [B * S, D], F32, kind="ExternalOutput").ap()

    xnT_sc = nc.dram_tensor("xnT_sc", [128, B * SP8], BF16, kind="Internal").ap()
    oacc_sc = nc.dram_tensor("oacc_sc", [128, B * NCH * DV], BF16, kind="Internal").ap()
    wgk_sc = nc.dram_tensor("wgk_sc", [128, NCHAN * 8 * DK], BF16, kind="Internal").ap()

    T_TOK = B * S
    NTILE = -(-(2 * T_TOK + NE * 511) // 512)
    NSLOT = NTILE * 512
    NGC = B * NCH
    xn_tok = nc.dram_tensor("xn_tok", [T_TOK + 1, D], BF16, kind="Internal").ap()
    disp = nc.dram_tensor("disp", [NSLOT, 4], F32, kind="Internal").ap()
    comb = nc.dram_tensor("comb", [2 * T_TOK, DV], BF16, kind="Internal").ap()
    cnt_sc = nc.dram_tensor("cnt_sc", [NGC, NE], F32, kind="Internal").ap()
    rank_sc = nc.dram_tensor("rank_sc", [128, NGC * NE], F32, kind="Internal").ap()
    vals_sc = nc.dram_tensor("vals_sc", [128, NGC * NE * 4], F32, kind="Internal").ap()
    fb_sc = nc.dram_tensor("fb_sc", [1, (NGC + 1) * NE], F32, kind="Internal").ap()
    eidf_sc = nc.dram_tensor("eidf_sc", [1, NTILE], F32, kind="Internal").ap()
    facc_sc = nc.dram_tensor("facc_sc", [128, B * NCH * DV], BF16, kind="Internal").ap()
    c_pidx = A("c_pidx", [128, 1])
    c_th = A("c_th", [128, 16])

    es = ExitStack()
    es_ab = ExitStack()
    es_p2 = ExitStack()
    sb = lambda name, shape, dt: es.enter_context(nc.sbuf_tensor(name, shape, dt))
    sba = lambda name, shape, dt: es_ab.enter_context(nc.sbuf_tensor(name, shape, dt))
    sbp = lambda name, shape, dt: es_p2.enter_context(nc.sbuf_tensor(name, shape, dt))
    pt = lambda name, shape, dt=F32: es.enter_context(nc.psum_tensor(name, shape, dt))

    k_id = sb("k_id", [128, 128], BF16)
    k_t16 = sb("k_t16", [128, 128], BF16)
    k_tS = sb("k_tS", [128, 128], F32)
    k_or = sb("k_or", [1, 128], BF16)
    k_orf = sb("k_orf", [1, 128], F32)
    k_ov16 = sb("k_ov16", [128, 1], BF16)
    k_ocf = sb("k_ocf", [128, 1], F32)
    k_io4 = sb("k_io4", [128, 4], F32)
    k_io8 = sb("k_io8", [128, 8], F32)
    k_nw = sb("k_nw", [128, DV], BF16)
    xstat = sb("xstat", [128, 8 * 128], BF16)
    k_misc = sb("k_misc", [128, 4], F32)
    lg8 = sb("lg8", [128, 8], F32)
    m8 = sb("m8", [128, 8], F32)
    idxu = sb("idxu", [128, 8], U32)
    idxf = sb("idxf", [128, 8], F32)
    d12 = sb("d12", [128, 1], F32)
    rwt = sb("rwt", [128, 1], F32)
    nc.const_aps.register(F32, EPS, k_misc[:, 0:1])
    nc.const_aps.register(F32, -0.5, k_misc[:, 1:2])
    nc.const_aps.register(F32, -1.0, k_misc[:, 2:3])
    nc.const_aps.register(F32, 1.0 / HDV, k_misc[:, 3:4])
    k_pidx = sb("k_pidx", [128, 1], F32)
    k_th = sb("k_th", [128, 16], F32)
    k_idf = sb("k_idf", [128, 128], F32)

    k_m4 = sba("k_m4", [128, 512], BF16)
    k_nwrow = sba("k_nwrow", [1, DV], F32)
    k_bgl = [sba(f"k_bg{i}", [1, DK], BF16) for i in range(NCHAN)]
    k_fb = sba("k_fb", [1, NE], BF16)
    k_g1 = sba("k_g1", [16, NCHAN * D], BF16)
    k_g2 = sba("k_g2", [16, NCHAN * DK], BF16)
    k_wgate = sba("k_wgate", [128, 8 * NM], BF16)
    k_frT = sba("k_frT", [128, 8 * NE], BF16)
    xbuf = sba("xbuf", [128, 4 * D], F32)
    bnraw = sba("bnraw", [128, 12], F32)
    bnst = sba("bnst", [128, 2], F32)
    r1c = sba("r1c", [128, 1], F32)
    rstd = sba("rstd", [128, 1], F32)
    xnT = sba("xnT", [128, SP8], BF16)
    q_b = sba("q_b", [128, NCH * 512], BF16)
    r_se1 = sba("r_se1", [128, NCH * 4], F32)
    r_se2 = sba("r_se2", [128, NCH * 4], F32)
    r_sel = sba("r_sel", [128, NCH * 4], F32)
    r_rank = sba("r_rank", [128, NCH * 4], F32)
    r_rw = sba("r_rw", [128, NCH * 2], F32)
    r_kept = sba("r_kept", [128, NCH * 4], F32)
    r_wm = sba("r_wm", [128, NCH * 4], F32)
    cnt_st = sba("cnt_st", [1, NCH * 4], F32)
    tot_r = sba("tot_r", [1, 4], F32)
    th0_r = sba("th0_r", [1, 4], F32)
    th_r = sba("th_r", [1, 4], F32)
    base_r = sba("base_r", [1, 4], F32)
    wch = sba("wch", [128, 8 * 2048], BF16)
    kcb0 = sba("kcb0", [128, 512], BF16)
    vcb0 = sba("vcb0", [128, DV], BF16)
    gkcb0 = sba("gkcb0", [128, 512], BF16)
    kcb1 = sba("kcb1", [128, 512], BF16)
    vcb1 = sba("vcb1", [128, DV], BF16)
    gkcb1 = sba("gkcb1", [128, 512], BF16)
    kgv = [(kcb0, vcb0, gkcb0), (kcb1, vcb1, gkcb1)]
    sig1 = sba("sig1", [128, 512], F32)
    sig2 = sba("sig2", [128, 512], F32)
    lam = sba("lam", [128, 512], F32)
    nlam = sba("nlam", [128, 512], F32)
    decay = sba("decay", [128, 4], F32)
    qt = sba("qt", [128, 512], BF16)
    kt = sba("kt", [128, 512], BF16)
    qtT = sba("qtT", [128, 512], BF16)
    ktT = sba("ktT", [128, 512], BF16)
    att = sba("att", [128, 512], BF16)
    Sbf = sba("Sbf", [128, 2 * 1024], BF16)
    t1 = sba("t1", [128, 512], F32)
    t1s = sba("t1s", [128, 512], F32)
    t1bs = sba("t1bs", [128, 512], F32)
    t1b = sba("t1b", [128, 512], F32)
    oac = sba("oac", [128, NCH * DV], BF16)
    flg8 = sba("flg8", [128, 8], F32)
    fm8 = m8
    fidxu = idxu
    fidxf = idxf
    fse8a = sig1[:, 0:8]
    fse8b = sig1[:, 8:16]
    fsel8 = sig1[:, 16:24]
    ftmp8 = sig1[:, 24:32]
    ftmp8b = sig1[:, 32:40]
    fsg = sig1[:, 40:42]
    fd1 = sig1[:, 42:43]
    frc = sig1[:, 43:44]
    fw2 = sig1[:, 44:46]
    tcol2 = sig1[:, 46:48]
    cnt16 = sig2[0:1, 0:NCH * 8]
    stag_rank = sba("stag_rank", [128, NCH * 8], F32)
    stag_vals = sba("stag_vals", [128, NCH * 32], F32)

    p2v = {}

    def alloc_p2():
        p2v["xnT2"] = sbp("xnT2", [128, SP8], BF16)
        p2v["gwb"] = sbp("gwb", [128, 8 * FI], BF16)
        p2v["uwb"] = sbp("uwb", [128, 8 * FI], BF16)
        p2v["dwb"] = sbp("dwb", [128, 16 * 1024], BF16)
        p2v["yg"] = sbp("yg", [128, 16 * 512], BF16)
        p2v["sgt"] = sbp("sgt", [128, 512], BF16)
        p2v["sgt2"] = sbp("sgt2", [128, 512], BF16)
        p2v["facc"] = sbp("facc", [128, NCH * DV], BF16)
        p2v["ssum"] = sbp("ssum", [128, 4], F32)
        p2v["rstd4"] = sbp("rstd4", [128, 4], F32)
        p2v["dispsb"] = sbp("dispsb", [128, 32], F32)
        p2v["widx1"] = sbp("widx1", [128, NTILE], U32)
        p2v["widx2"] = sbp("widx2", [128, NTILE], U32)
        p2v["eidb"] = sbp("eidb", [128, NTILE], F32)
        p2v["widxf"] = sbp("widxf", [128, NTILE], F32)
        p2v["eidrowF"] = sbp("eidrowF", [1, NTILE], F32)
        uwbF = p2v["uwb"].bitcast(F32)
        gwbF = p2v["gwb"].bitcast(F32)
        p2v["sq_t"] = uwbF[:, 5200:5456]
        p2v["x_t"] = uwbF[:, 6096:6096 + 2 * D]
        p2v["outsb"] = gwbF[:, 2048:2048 + 4 * D]
        p2v["oat"] = p2v["gwb"][:, 12288:12288 + 4 * DV]
        p2v["on_t"] = p2v["yg"][:, 4096:4096 + DV]
        p2v["ogT"] = p2v["yg"][:, 5120:5120 + DV]
        p2v["eidu"] = sbp("eidu", [128, 1], U32)
        p2v["posu"] = xstat.bitcast(U32)[:, 0:NGC * 8]
        p2v["fbrow"] = p2v["outsb"][0:1, 2560:2560 + (NGC + 1) * NE]
        p2v["fb2"] = p2v["outsb"][0:NE, 3084:3084 + NGC + 2]
        p2v["p8sb"] = p2v["outsb"][0:NE, 3150:3151]
        p2v["pbs"] = p2v["outsb"][0:NE, 3151:3152]
        p2v["fbTsb"] = p2v["outsb"][0:NGC + 1, 3152:3152 + NE]
        p2v["cnt64sb"] = p2v["outsb"][0:NGC, 3160:3160 + NE]
        p2v["fin16"] = p2v["sq_t"][:, 16:32]
        p2v["eidf"] = p2v["sq_t"][:, 33:34]
        p2v["ts512"] = p2v["sq_t"][:, 32:33]
        p2v["fin8b"] = p2v["sq_t"][:, 8:16]
        p2v["posf"] = p2v["sq_t"][:, 0:8]
        p2v["rankm_sb"] = p2v["outsb"][:, 2048:2560]
        p2v["valsb"] = p2v["outsb"][:, 0:2048]
        p2v["pattbig"] = p2v["x_t"][:, 0:(NSLOT // 128) * 4]
    wtmp = sba("wtmp", [128, 8 * 512], BF16)

    pf = [pt(f"pf{i}", [128, 512], F32) for i in range(7)]
    pb = pt("pb", [128, 1024], BF16)

    sems = {s_: es.enter_context(nc.semaphore(s_)) for s_ in SEMS}
    loop_delta = {}

    wk_src = [w_k[mi] for mi in range(NM)] + [w_ks]
    wv_src = [w_v[mi] for mi in range(NM)] + [w_vs]

    def describe_ab(bd: Bd):
        bd.gdma(k_id[:], c_id)
        bd.gdma(k_t16[:], c_t16)
        bd.ydma(k_tS[:], c_tS)
        bd.gdma(k_or[:], c_or)
        bd.ydma(k_orf[:], c_or)
        bd.gdma(k_ov16[:], c_ov16)
        bd.ydma(k_ocf[:], c_ov16)
        bd.ydma(k_io4[:], c_io4)
        bd.ydma(k_io8[:], c_io8)
        bd.gdma(k_m4[:], c_m4)
        for ch in range(NM):
            bd.gdma(k_bgl[ch][:], b_gk[ch:ch + 1, :])
        bd.gdma(k_bgl[4][:], b_gks)
        bd.gdma(k_fb[:], fbias)
        for ch in range(NM):
            bd.gdma(k_g1[:16, ch * D:(ch + 1) * D], w_gk1T[ch])
            bd.gdma(k_g2[:16, ch * DK:(ch + 1) * DK], w_gk2[ch])
        bd.gdma(k_g1[:16, 4 * D:5 * D], w_gk1sT)
        bd.gdma(k_g2[:16, 4 * DK:5 * DK], w_gk2s)
        bd.gdma(k_wgate[:].rearrange("p (dt e) -> p dt e", dt=8),
                w_gate.rearrange("(dt p) e -> p dt e", p=128))
        bd.gdma(k_frT[:].rearrange("p (dt e) -> p dt e", dt=8),
                frouterT.rearrange("(dt p) e -> p dt e", p=128))
        bd.ydma(k_nwrow[:], normw4)
        bd.ydma(k_misc[:], c_misc)
        bd.ydma(k_pidx[:], c_pidx)
        bd.ydma(k_th[:], c_th)
        bd.ydma(k_idf[:], c_id)
        bd.wait("V", "sY")
        bd.v("tensor_scalar", k_ocf[:], k_ocf[:], 16.0, None, ALU.mult)
        bd.v("memset", lg8[:], -1e30)
        bd.v("memset", stag_vals[:], 0.0)
        bd.v("memset", xbuf[0:1, 0:D], 0.0)
        bd.wait("G", "sV")
        bd.gdma(xn_tok[T_TOK:T_TOK + 1, :], xbuf[0:1, 0:D])
        bd.wait("Y", "sG")
        bd.wait("T", "sY")
        bd.wait("T", "sG")
        bd.wait("T", "sV")
        bd.mm(pf[0][:, 0:512], k_orf[:], k_nwrow[0:1, 0:512], inc=1)
        bd.mm(pf[1][:, 0:512], k_orf[:], k_nwrow[0:1, 512:1024], inc=1)
        bd.wait("V", "sT")
        bd.v("tensor_copy", k_nw[:, 0:512], pf[0][:, 0:512])
        bd.v("tensor_copy", k_nw[:, 512:1024], pf[1][:, 0:512])

        for ch in range(NCHAN):
            bd.wait("V", "sG")
            for dt_ in range(8):
                bd.wait("T", "sV")
                bd.mm(pf[dt_ % 2][:, 0:512], k_g1[:16, ch * D + dt_ * 128: ch * D + (dt_ + 1) * 128],
                      k_g2[:16, ch * DK:(ch + 1) * DK], inc=1)
                bd.wait("V", "sT")
                bd.v("tensor_copy", wtmp[:, dt_ * 512:(dt_ + 1) * 512], pf[dt_ % 2][:, 0:512])
            bd.wait("G", "sV")
            bd.gdma(wgk_sc[:, ch * 8 * DK:(ch + 1) * 8 * DK], wtmp[:])

        def phase_ab(b):
            bd.wait("V", "sG")
            bd.wait("V", "sT")
            bd.wait("T", "sG")
            bd.wait("S", "sG")
            bd.wait("S", "sY")

            def a_chunk(c):
                q4, c4 = c // 4, c % 4
                if c4 == 0:
                    bd.wait("Y", "sV")
                    bd.ydma(xbuf[:].rearrange("p (c d) -> p c d", c=4),
                            xs.rearrange("(b q c p) d -> p b q c d", q=NCH // 4, c=4, p=128)[:, ds(b, 1), q4, :, :])
                xb = xbuf[:, c4 * D:(c4 + 1) * D]
                bd.wait("V", "sY")
                bd.v("bn_stats", bnraw[:, 0:6], xb[:, 0:512])
                bd.v("bn_stats", bnraw[:, 6:12], xb[:, 512:1024])
                bd.wait("V", "sV")
                bd.v("bn_aggr", bnst[:], bnraw[:])
                bd.wait("S", "sV")
                bd.s("activation", r1c[:], bnst[:, 1:2], AF.Ln, bias=EPS)
                bd.wait("S", "sS")
                bd.s("activation", rstd[:], r1c[:], AF.Exp, scale=-0.5)
                bd.wait("V", "sS")
                bd.wait("V", "sG")
                xn_st = wch[:, c * D:(c + 1) * D]
                bd.v("tensor_scalar", xn_st, xb, bnst[:, 0:1], rstd[:], ALU.subtract, ALU.mult)
                bd.wait("T", "sV")
                for dt_ in range(8):
                    bd.tr(pb[:, dt_ * 128:(dt_ + 1) * 128], xn_st[:, dt_ * 128:(dt_ + 1) * 128], k_id[:],
                          inc=(1 if dt_ == 7 else 0))
                bd.wait("V", "sT")
                bd.v("tensor_copy", xstat[:], pb[:])
                bd.wait("V", "sV")
                bd.v("tensor_copy",
                     xnT[:].rearrange("p (dt c s) -> p dt c s", dt=8, s=128)[:, :, ds(c, 1), :],
                     xstat[:].rearrange("p (dt s) -> p dt s", dt=8))
                bd.wait("T", "sV")
                for dt_ in range(8):
                    bd.mm(pf[0][:, 0:4], xstat[:, dt_ * 128:(dt_ + 1) * 128],
                          k_wgate[:, dt_ * 4:(dt_ + 1) * 4], start=(dt_ == 0), stop=(dt_ == 7),
                          inc=(1 if dt_ == 7 else 0))
                bd.wait("V", "sT")
                bd.v("tensor_copy", lg8[:, 0:4], pf[0][:, 0:4])
                bd.wait("V", "sV")
                bd.v("max", m8[:], lg8[:])
                bd.wait("V", "sV")
                bd.v("max_index", idxu[:], m8[:], lg8[:])
                bd.wait("V", "sV")
                bd.v("tensor_copy", idxf[:], idxu[:])
                bd.wait("V", "sV")
                bd.v("tensor_single_scalar", r_se1[:].rearrange("p (c m) -> p c m", m=4)[:, ds(c, 1), :], k_io4[:], idxf[:, 0:1], ALU.is_equal)
                bd.v("tensor_single_scalar", r_se2[:].rearrange("p (c m) -> p c m", m=4)[:, ds(c, 1), :], k_io4[:], idxf[:, 1:2], ALU.is_equal)
                bd.v("tensor_sub", d12[:], m8[:, 0:1], m8[:, 1:2])
                bd.wait("V", "sV")
                bd.v("tensor_add", r_sel[:].rearrange("p (c m) -> p c m", m=4)[:, ds(c, 1), :], r_se1[:].rearrange("p (c m) -> p c m", m=4)[:, ds(c, 1), :], r_se2[:].rearrange("p (c m) -> p c m", m=4)[:, ds(c, 1), :])
                bd.wait("S", "sV")
                bd.s("activation", rwt[:], d12[:], AF.Sigmoid)
                bd.wait("V", "sS")
                bd.v("tensor_scalar", r_rw[:].rearrange("p (c j) -> p c j", j=2)[:, ds(c, 1), 0:1], rwt[:], SCALE, None, ALU.mult)
                bd.wait("V", "sV")
                bd.v("tensor_scalar", r_rw[:].rearrange("p (c j) -> p c j", j=2)[:, ds(c, 1), 1:2],
                     r_rw[:].rearrange("p (c j) -> p c j", j=2)[:, ds(c, 1), 0:1], -1.0, SCALE,
                     ALU.mult, ALU.add)
                bd.wait("T", "sV")
                bd.mm(pf[1][:, 0:4], k_tS[:], r_sel[:].rearrange("p (c m) -> p c m", m=4)[:, ds(c, 1), :], inc=1)
                bd.mm(pf[2][0:1, 0:4], k_ocf[:], r_sel[:].rearrange("p (c m) -> p c m", m=4)[:, ds(c, 1), :], inc=1)
                bd.wait("V", "sT")
                bd.v("tensor_copy", r_rank[:].rearrange("p (c m) -> p c m", m=4)[:, ds(c, 1), :], pf[1][:, 0:4])
                bd.v("tensor_copy", cnt_st[0:1, :].rearrange("p (c m) -> p c m", m=4)[:, ds(c, 1), :], pf[2][0:1, 0:4])

                bd.wait("T", "sV")
                for dt_ in range(8):
                    bd.mm(pf[3][:, 0:8], xstat[:, dt_ * 128:(dt_ + 1) * 128],
                          k_frT[:, dt_ * 8:(dt_ + 1) * 8], start=(dt_ == 0), stop=False)
                bd.mm(pf[3][:, 0:8], k_or[:], k_fb[:], start=False, stop=True, inc=1)
                bd.wait("V", "sT")
                bd.v("tensor_copy", flg8[:], pf[3][:, 0:8])
                bd.wait("V", "sV")
                bd.v("max", fm8[:], flg8[:])
                bd.wait("V", "sV")
                bd.v("max_index", fidxu[:], fm8[:], flg8[:])
                bd.wait("S", "sV")
                bd.s("activation", fsg[:], fm8[:, 0:2], AF.Sigmoid)
                bd.wait("V", "sV")
                bd.v("tensor_copy", fidxf[:], fidxu[:])
                bd.wait("V", "sV")
                bd.v("tensor_single_scalar", fse8a[:], k_io8[:], fidxf[:, 0:1], ALU.is_equal)
                bd.v("tensor_single_scalar", fse8b[:], k_io8[:], fidxf[:, 1:2], ALU.is_equal)
                bd.wait("V", "sS")
                bd.v("tensor_add", fd1[:], fsg[:, 0:1], fsg[:, 1:2])
                bd.wait("V", "sV")
                bd.v("reciprocal", frc[:], fd1[:])
                bd.v("tensor_add", fsel8[:], fse8a[:], fse8b[:])
                bd.wait("V", "sV")
                bd.v("tensor_scalar", fw2[:], fsg[:], frc[:], None, ALU.mult)
                bd.wait("T", "sV")
                bd.mm(pf[3][:, 0:8], k_tS[:], fsel8[:], inc=1)
                bd.mm(pf[4][0:1, 0:8], k_ocf[:], fsel8[:], inc=1)
                bd.wait("V", "sT")
                bd.v("tensor_copy", ftmp8[:], pf[3][:, 0:8])
                bd.v("tensor_copy", cnt16[0:1, c * 8:(c + 1) * 8], pf[4][0:1, 0:8])
                bd.wait("V", "sV")
                bd.v("tensor_scalar", tcol2[:, 0:1], k_pidx[:], float(c * 128), None, ALU.add)
                bd.wait("V", "sV")
                bd.v("tensor_scalar", tcol2[:, 1:2], tcol2[:, 0:1], 2.0, None, ALU.mult)
                vv = stag_vals[:].rearrange("p (c e v) -> p c v e", c=NCH, e=8, v=4)
                bd.wait("V", "sV")
                bd.v("tensor_scalar", vv[:, ds(c, 1), 0, :], k_io8[:], 0.0, tcol2[:, 0:1],
                     ALU.mult, ALU.add)
                bd.v("tensor_scalar", ftmp8b[:], fse8a[:], fw2[:, 0:1], None, ALU.mult)
                bd.wait("V", "sV")
                bd.v("scalar_tensor_tensor", vv[:, ds(c, 1), 1, :], fse8b[:], fw2[:, 1:2],
                     ftmp8b[:], ALU.mult, ALU.add)
                bd.v("tensor_scalar", vv[:, ds(c, 1), 2, :], fse8b[:], 1.0, tcol2[:, 1:2],
                     ALU.mult, ALU.add)
                bd.wait("V", "sV")
                bd.v("tensor_scalar", ftmp8[:], ftmp8[:], -BIG, None, ALU.add)
                bd.wait("V", "sV")
                bd.v("tensor_mul", ftmp8[:], ftmp8[:], fsel8[:])
                bd.wait("V", "sV")
                bd.v("tensor_scalar", stag_rank[:, c * 8:(c + 1) * 8], ftmp8[:], BIG, None, ALU.add)

            bd.ploop(NCH, a_chunk)

            bd.wait("G", "sV")
            bd.gdma(xnT_sc.rearrange("p (b x) -> p b x", x=SP8)[:, ds(b, 1), :], xnT[:])
            bd.gdma(xn_tok[0:T_TOK, :].rearrange("(b c p) h -> b p c h", c=NCH, p=128)[ds(b, 1), :, :, :],
                    wch[:].rearrange("p (c h) -> p c h", c=NCH))
            bd.gdma(cnt_sc.rearrange("(b x) e -> b (x e)", x=NCH)[ds(b, 1), :], cnt16)
            bd.gdma(rank_sc.rearrange("p (b x) -> p b x", x=NCH * 8)[:, ds(b, 1), :], stag_rank[:])
            bd.gdma(vals_sc.rearrange("p (b x) -> p b x", x=NCH * 32)[:, ds(b, 1), :], stag_vals[:])

            bd.wait("V", "sV")
            bd.v("tensor_reduce", tot_r[0:1, 0:4],
                 cnt_st[0:1, :].rearrange("p (c m) -> p m c", m=4), AX.X, ALU.add)
            bd.v("memset", base_r[:], 0.0)
            bd.wait("V", "sV")
            bd.v("tensor_scalar", th0_r[:], tot_r[:], float(-CAP), None, ALU.add)
            for c in range(NCH):
                bd.wait("V", "sV")
                bd.v("tensor_sub", th_r[:], th0_r[:], base_r[:])
                bd.wait("T", "sV")
                bd.mm(pf[3][:, 0:4], k_orf[:], th_r[0:1, 0:4], inc=1)
                bd.wait("V", "sT")
                bd.v("tensor_tensor", r_kept[:, c * 4:(c + 1) * 4], r_rank[:, c * 4:(c + 1) * 4],
                     pf[3][:, 0:4], ALU.is_ge)
                bd.wait("V", "sV")
                bd.v("tensor_mul", r_kept[:, c * 4:(c + 1) * 4], r_kept[:, c * 4:(c + 1) * 4],
                     r_sel[:, c * 4:(c + 1) * 4])
                bd.v("tensor_scalar", r_wm[:, c * 4:(c + 1) * 4], r_se1[:, c * 4:(c + 1) * 4],
                     r_rw[:, c * 2:c * 2 + 1], None, ALU.mult)
                bd.wait("V", "sV")
                bd.v("scalar_tensor_tensor", r_wm[:, c * 4:(c + 1) * 4], r_se2[:, c * 4:(c + 1) * 4],
                     r_rw[:, c * 2 + 1:c * 2 + 2], r_wm[:, c * 4:(c + 1) * 4], ALU.mult, ALU.add)
                bd.wait("V", "sV")
                bd.v("tensor_mul", r_wm[:, c * 4:(c + 1) * 4], r_wm[:, c * 4:(c + 1) * 4],
                     r_kept[:, c * 4:(c + 1) * 4])
                bd.v("tensor_add", base_r[:], base_r[:], cnt_st[0:1, c * 4:(c + 1) * 4])

            bd.wait("G", "sT")
            bd.gdma(wch[:].rearrange("p (dt k) -> p dt k", dt=8)[:, :, 0:512],
                    w_q.rearrange("(dt p) k -> p dt k", p=128))

            def q_chunk(c):
                bd.wait("V", "sT")
                bd.v("tensor_copy", xstat[:].rearrange("p (dt s) -> p dt s", dt=8),
                     xnT[:].rearrange("p (dt c s) -> p dt c s", dt=8, s=128)[:, :, ds(c, 1), :])
                bd.wait("T", "sG")
                bd.wait("T", "sV")
                for dt_ in range(8):
                    bd.mm(pf[6][:, 0:512], xstat[:, dt_ * 128:(dt_ + 1) * 128],
                          wch[:, dt_ * 2048:dt_ * 2048 + 512], start=(dt_ == 0),
                          stop=(dt_ == 7), inc=(1 if dt_ == 7 else 0))
                bd.wait("V", "sT")
                bd.v("tensor_copy", q_b[:].rearrange("p (c x) -> p c x", x=512)[:, ds(c, 1), :], pf[6][:, 0:512])

            bd.ploop(NCH, q_chunk)

            last_wch_use = [bd.mark("sT")]
            for ch in range(NCHAN):
                routed = ch < NM
                bd.wait("G", "sT", last_wch_use[0])
                bd.wait("G", "sG")
                wv4 = wch[:].rearrange("p (dt k) -> p dt k", dt=8)
                bd.gdma(wv4[:, :, 0:512], wk_src[ch].rearrange("(dt p) k -> p dt k", p=128), sem="sWA")
                bd.gdma(wv4[:, :, 512:1536], wv_src[ch].rearrange("(dt p) k -> p dt k", p=128), sem="sWB")
                bd.gdma(wv4[:, :, 1536:2048],
                        wgk_sc[:, ch * 8 * DK:(ch + 1) * 8 * DK].rearrange("p (dt k) -> p dt k", dt=8))
                wch_ld = (bd.mark("sWA"), bd.mark("sWB"), bd.mark("sG"))
                bd.wait("V", "sT")
                bd.wait("V", "sV")
                bd.v("memset", Sbf[:], 0.0)

                m = {"evac0": bd.mark("sV"), "evac1": bd.mark("sV"),
                     "sig": bd.mark("sS"), "lamS": bd.mark("sS"),
                     "gla_done0": bd.mark("sT"), "gla_done1": bd.mark("sT"),
                     "proj_done": {}, "oevac": bd.mark("sV"), "supd": bd.mark("sV"),
                     "evkg0": bd.mark("sV"), "evkg1": bd.mark("sV"),
                     "mask": bd.mark("sV"), "tcp": bd.mark("sV"), "kmm": {}, "gkmm": {},
                     "ds_ev": bd.mark("sS"), "supdA": bd.mark("sV")}

                def stage_xstat(c):
                    bd.wait("V", "sT", m["proj_done"].get(c - 1, 0))
                    bd.v("tensor_copy", xstat[:].rearrange("p (dt s) -> p dt s", dt=8),
                         xnT[:].rearrange("p (dt c s) -> p dt c s", dt=8, s=128)[:, :, ds(c, 1), :])
                    m["xstat"] = bd.mark("sV")

                def stage_gk_T(c):
                    bd.wait("T", "sWA", wch_ld[0])
                    bd.wait("T", "sWB", wch_ld[1])
                    bd.wait("T", "sG", wch_ld[2])
                    bd.wait("T", "sV", m["xstat"])
                    bd.wait("T", "sS", m["sig"])
                    for dt_ in range(8):
                        bd.mm(pf[0][:, 0:512], xstat[:, dt_ * 128:(dt_ + 1) * 128],
                              wch[:, dt_ * 2048 + 1536:(dt_ + 1) * 2048], start=(dt_ == 0), stop=False)
                    bd.mm(pf[0][:, 0:512], k_or[:], k_bgl[ch][:], start=False, stop=True, inc=1)
                    m["gkmm"][c] = bd.mark("sT")

                def stage_k_T(c):
                    bd.wait("T", "sV", m[f"evkg{1 - (c % 2)}"])
                    for dt_ in range(8):
                        bd.mm(pf[1][:, 0:512], xstat[:, dt_ * 128:(dt_ + 1) * 128],
                              wch[:, dt_ * 2048:dt_ * 2048 + 512], start=(dt_ == 0), stop=(dt_ == 7),
                              inc=(1 if dt_ == 7 else 0))
                    m["kmm"][c] = bd.mark("sT")

                def stage_v_T(c, half):
                    if half == 0:
                        bd.wait("T", "sV", m[f"evac{1 - (c % 2)}"])
                    for dt_ in range(8):
                        bd.mm(pf[2 + half][:, 0:512], xstat[:, dt_ * 128:(dt_ + 1) * 128],
                              wch[:, dt_ * 2048 + 512 + half * 512:dt_ * 2048 + 1024 + half * 512],
                              start=(dt_ == 0), stop=(dt_ == 7), inc=(1 if dt_ == 7 else 0))
                    if half == 1:
                        m["proj_done"][c] = bd.mark("sT")

                def stage_sig_S(c):
                    bd.wait("S", "sT", m["gkmm"][c])
                    bd.wait("S", "sS", m["sig"])
                    bd.wait("S", "sV", m[f"evkg{1 - (c % 2)}"])
                    bd.s("activation", sig1[:], pf[0][:, 0:512], AF.Sigmoid)
                    bd.wait("S", "sS")
                    bd.s("activation", sig2[:], sig1[:], AF.Ln)
                    m["sig"] = bd.mark("sS")

                def stage_evac_kg_V(c):
                    par = c % 2
                    kcb_, vcb_, gkcb_ = kgv[par]
                    bd.wait("V", "sS", m["sig"])
                    bd.wait("V", "sT", m["kmm"][c])
                    bd.wait("V", "sT", m[f"gla_done{par}"])
                    kpc = r_kept[:].rearrange("p (c m) -> p c m", m=4)[:, ds(c, 1), ch:ch + 1] if routed else None
                    if routed:
                        bd.v("tensor_scalar", gkcb_[:], sig2[:], kpc, None, ALU.mult)
                        bd.v("tensor_scalar", kcb_[:], pf[1][:, 0:512], kpc, None, ALU.mult, inc=1)
                    else:
                        bd.v("tensor_copy", gkcb_[:], sig2[:])
                        bd.v("tensor_copy", kcb_[:], pf[1][:, 0:512], inc=1)
                    m[f"evkg{par}"] = bd.mark("sV")

                def stage_evac_v_V(c):
                    par = c % 2
                    kcb_, vcb_, gkcb_ = kgv[par]
                    bd.wait("V", "sT", m["proj_done"][c])
                    bd.v("tensor_copy", vcb_[:, 0:512], pf[2][:, 0:512])
                    bd.v("tensor_copy", vcb_[:, 512:1024], pf[3][:, 0:512], inc=1)
                    m[f"evac{par}"] = bd.mark("sV")

                def stage_gla(c):
                    par = c % 2
                    kcb_, vcb_, gkcb_ = kgv[par]
                    have_next = c + 1 < NCH
                    bd.wait("T", "sV", m[f"evkg{par}"])
                    bd.wait("T", "sV", m[f"evac{par}"])
                    bd.wait("T", "sS", m["lamS"])
                    bd.wait("T", "sS", m["ds_ev"])
                    bd.mm(pf[4][:, 0:512], k_t16[:], gkcb_[:], inc=1)
                    for hh in range(H):
                        bd.mm(pf[5][:, hh:hh + 1], gkcb_[:, hh * 128:(hh + 1) * 128], k_ov16[:],
                              inc=(1 if hh == 3 else 0))
                    gmm = bd.mark("sT")
                    bd.wait("S", "sT", gmm)
                    bd.wait("S", "sV", m["supd"])
                    bd.s("activation", lam[:], pf[4][:, 0:512], AF.Exp)
                    bd.s("activation", nlam[:], pf[4][:, 0:512], AF.Exp, scale=-1.0)
                    bd.s("activation", decay[:], pf[5][:, 0:4], AF.Exp)
                    m["lamS"] = bd.mark("sS")
                    if have_next:
                        stage_xstat(c + 1)
                        stage_gk_T(c + 1)
                        stage_sig_S(c + 1)
                    bd.wait("V", "sS", m["lamS"])
                    bd.wait("V", "sV", m["tcp"])
                    bd.wait("V", "sT", m[f"gla_done{1 - (c % 2)}"])
                    if routed:
                        bd.v("scalar_tensor_tensor", qt[:], q_b[:].rearrange("p (c x) -> p c x", x=512)[:, ds(c, 1), :],
                             r_wm[:].rearrange("p (c m) -> p c m", m=4)[:, ds(c, 1), ch:ch + 1],
                             lam[:], ALU.mult, ALU.mult)
                    else:
                        bd.v("scalar_tensor_tensor", qt[:], q_b[:].rearrange("p (c x) -> p c x", x=512)[:, ds(c, 1), :],
                             SCALE, lam[:], ALU.mult, ALU.mult)
                    bd.v("tensor_tensor", kt[:], kcb_[:], nlam[:], ALU.mult, inc=1)
                    m["qtkt"] = bd.mark("sV")
                    bd.wait("T", "sV", m["qtkt"])
                    for hh in range(H):
                        bd.tr(pb[:, hh * 128:(hh + 1) * 128], qt[:, hh * 128:(hh + 1) * 128], k_id[:])
                        bd.tr(pb[:, 512 + hh * 128:512 + (hh + 1) * 128], kt[:, hh * 128:(hh + 1) * 128],
                              k_id[:], inc=(1 if hh == 3 else 0))
                    trm = bd.mark("sT")
                    if have_next:
                        stage_k_T(c + 1)
                    bd.wait("V", "sT", trm)
                    bd.v("tensor_copy", qtT[:], pb[:, 0:512])
                    bd.v("tensor_copy", ktT[:], pb[:, 512:1024], inc=1)
                    m["tcp"] = bd.mark("sV")
                    bd.wait("T", "sV", m["tcp"])
                    bd.wait("T", "sV", m["oevac"])
                    for hh in range(H):
                        bd.mm(pf[6][:, hh * 128:(hh + 1) * 128], ktT[:, hh * 128:(hh + 1) * 128],
                              qtT[:, hh * 128:(hh + 1) * 128], inc=(1 if hh == 3 else 0))
                    atm = bd.mark("sT")
                    if have_next:
                        stage_v_T(c + 1, 0)
                        stage_evac_kg_V(c + 1)
                    bd.wait("V", "sT", atm)
                    bd.v("tensor_tensor", att[:], pf[6][:, 0:512], k_m4[:], ALU.mult, inc=1)
                    m["mask"] = bd.mark("sV")
                    par_s = c % 2
                    scur = Sbf[:, par_s * 1024:(par_s + 1) * 1024]
                    snew = Sbf[:, (1 - par_s) * 1024:(2 - par_s) * 1024]
                    bd.wait("T", "sV", m["mask"])
                    bd.wait("T", "sS", m["lamS"])
                    for hh in range(2):
                        bd.mm(pf[6][:, hh * 256:(hh + 1) * 256], att[:, hh * 128:(hh + 1) * 128],
                              vcb_[:, hh * 256:(hh + 1) * 256], start=True, stop=False)
                        bd.mm(pf[6][:, hh * 256:(hh + 1) * 256], qtT[:, hh * 128:(hh + 1) * 128],
                              scur[:, hh * 256:(hh + 1) * 256], start=False, stop=True,
                              inc=(1 if hh == 1 else 0))
                    o01 = bd.mark("sT")
                    for hh in range(2):
                        bd.mm(pf[4][:, hh * 256:(hh + 1) * 256], kt[:, hh * 128:(hh + 1) * 128],
                              vcb_[:, hh * 256:(hh + 1) * 256], inc=(1 if hh == 1 else 0))
                    dsa = bd.mark("sT")
                    if have_next:
                        stage_v_T(c + 1, 1)
                    bd.wait("V", "sT", o01)
                    oview = oac[:].rearrange("p (c h x) -> p c h x", h=2, x=512)
                    if ch == 0:
                        bd.v("tensor_copy", oview[:, ds(c, 1), 0, :], pf[6][:, 0:512], inc=1)
                    else:
                        bd.v("tensor_tensor", oview[:, ds(c, 1), 0, :], pf[6][:, 0:512],
                             oview[:, ds(c, 1), 0, :], ALU.add, inc=1)
                    m["oevac01"] = bd.mark("sV")
                    bd.wait("T", "sV", m["oevac01"])
                    for hh in range(2, 4):
                        bd.mm(pf[6][:, (hh - 2) * 256:(hh - 1) * 256], att[:, hh * 128:(hh + 1) * 128],
                              vcb_[:, hh * 256:(hh + 1) * 256], start=True, stop=False)
                        bd.mm(pf[6][:, (hh - 2) * 256:(hh - 1) * 256], qtT[:, hh * 128:(hh + 1) * 128],
                              scur[:, hh * 256:(hh + 1) * 256], start=False, stop=True,
                              inc=(1 if hh == 3 else 0))
                    o23 = bd.mark("sT")
                    for hh in range(2, 4):
                        bd.mm(pf[5][:, (hh - 2) * 256:(hh - 1) * 256], kt[:, hh * 128:(hh + 1) * 128],
                              vcb_[:, hh * 256:(hh + 1) * 256], inc=(1 if hh == 3 else 0))
                    dsb = bd.mark("sT")
                    m[f"gla_done{par}"] = bd.mark("sT")
                    bd.wait("S", "sT", dsa)
                    bd.wait("S", "sV", m["supdA"])
                    bd.s("copy", t1s[:], pf[4][:, 0:512])
                    ds_ev1 = bd.mark("sS")
                    bd.wait("V", "sS", ds_ev1)
                    bd.v("tensor_add", t1[:], t1s[:], scur[:, 0:512])
                    bd.wait("V", "sV")
                    for hh in range(2):
                        bd.v("tensor_scalar", snew[:, hh * 256:(hh + 1) * 256],
                             t1[:, hh * 256:(hh + 1) * 256], decay[:, hh:hh + 1], None, ALU.mult)
                    bd.wait("V", "sT", o23)
                    if ch == 0:
                        bd.v("tensor_copy", oview[:, ds(c, 1), 1, :], pf[6][:, 0:512], inc=1)
                    else:
                        bd.v("tensor_tensor", oview[:, ds(c, 1), 1, :], pf[6][:, 0:512],
                             oview[:, ds(c, 1), 1, :], ALU.add, inc=1)
                    m["oevac"] = bd.mark("sV")
                    if have_next:
                        stage_evac_v_V(c + 1)
                    bd.wait("S", "sT", dsb)
                    bd.s("copy", t1bs[:], pf[5][:, 0:512])
                    m["ds_ev"] = bd.mark("sS")
                    bd.wait("V", "sS", m["ds_ev"])
                    bd.v("tensor_add", t1b[:], t1bs[:], scur[:, 512:1024])
                    bd.wait("V", "sV")
                    for hh in range(2, 4):
                        bd.v("tensor_scalar", snew[:, hh * 256:(hh + 1) * 256],
                             t1b[:, (hh - 2) * 256:(hh - 1) * 256], decay[:, hh:hh + 1], None, ALU.mult,
                             inc=(1 if hh == 3 else 0))
                    m["supd"] = bd.mark("sV")
                    m["supdA"] = m["supd"]

                stage_xstat(0)
                stage_gk_T(0)
                stage_sig_S(0)
                stage_k_T(0)
                stage_evac_kg_V(0)
                stage_v_T(0, 0)
                stage_v_T(0, 1)
                stage_evac_v_V(0)
                for c in range(NCH):
                    stage_gla(c)
                last_wch_use[0] = m["proj_done"][NCH - 1]

            bd.wait("G", "sV")
            bd.gdma(oacc_sc.rearrange("p (b x) -> p b x", x=NCH * DV)[:, ds(b, 1), :], oac[:])

        bd.loop("AB_batch", B, phase_ab)

    def describe_p2(bd: Bd):
        xnT2 = p2v["xnT2"]; gwb = p2v["gwb"]; uwb = p2v["uwb"]
        dwb = p2v["dwb"]; yg = p2v["yg"]; sgt = p2v["sgt"]; sgt2 = p2v["sgt2"]; facc = p2v["facc"]
        oat = p2v["oat"]; x_t = p2v["x_t"]; sq_t = p2v["sq_t"]; ssum = p2v["ssum"]
        rstd4 = p2v["rstd4"]; on_t = p2v["on_t"]
        og_t = p2v["on_t"]; ogT = p2v["ogT"]; outsb = p2v["outsb"]
        dispsb = p2v["dispsb"]; widx1 = p2v["widx1"]; widx2 = p2v["widx2"]; eidb = p2v["eidb"]
        cnt64sb = p2v["cnt64sb"]; fin16 = p2v["fin16"]; fb2 = p2v["fb2"]
        p8sb = p2v["p8sb"]; pbs = p2v["pbs"]; fbTsb = p2v["fbTsb"]; fbrow = p2v["fbrow"]
        eidu = p2v["eidu"]; eidf = p2v["eidf"]
        ts512 = p2v["ts512"]; fin8b = p2v["fin8b"]; posf = p2v["posf"]
        posu = p2v["posu"]; rankm_sb = p2v["rankm_sb"]; valsb = p2v["valsb"]
        pattbig = p2v["pattbig"]

        pbv = pattbig[:].rearrange("p (r v) -> p r v", v=4)
        bd.wait("V", "sG")
        bd.wait("V", "sY")
        bd.wait("S", "sG")
        bd.wait("S", "sY")
        bd.wait("V", "sV")
        bd.v("memset", pbv[:, :, 0:1], float(T_TOK))
        bd.v("memset", pbv[:, :, 1:2], 0.0)
        bd.v("memset", pbv[:, :, 2:3], BIG)
        bd.v("memset", pbv[:, :, 3:4], 0.0)
        bd.wait("Y", "sV")
        bd.ydma(disp.rearrange("(p r) v -> p (r v)", p=128), pattbig[:])
        bd.ydma(cnt64sb[:], cnt_sc)
        bd.ydma(rankm_sb[:], rank_sc)
        bd.ydma(valsb[:], vals_sc)
        bd.wait("T", "sY")
        bd.mm(pf[0][0:NE, 0:NGC], cnt64sb[:], k_tS[0:NGC, 0:NGC], inc=1)
        bd.mm(pf[1][0:NE, 0:1], cnt64sb[:], k_ocf[0:NGC, 0:1], inc=1)
        bd.wait("V", "sT")
        bd.v("tensor_copy", pbs[:], pf[1][0:NE, 0:1])
        bd.wait("V", "sV")
        bd.v("tensor_scalar", fin16[0:NE, :], k_th[0:NE, :], pbs[:, 0:1], None, ALU.is_lt)
        bd.wait("V", "sV")
        bd.v("tensor_reduce", p8sb[:], fin16[0:NE, 0:16], AX.X, ALU.add)
        bd.wait("V", "sV")
        bd.v("tensor_scalar", p8sb[:], p8sb[:], 512.0, None, ALU.mult)
        bd.wait("T", "sV")
        bd.mm(pf[2][0:NE, 0:1], k_tS[0:NE, 0:NE], p8sb[:], inc=1)
        bd.wait("V", "sT")
        bd.v("tensor_copy", pbs[:], pf[2][0:NE, 0:1])
        bd.wait("V", "sV")
        bd.v("tensor_scalar", fb2[:, 0:NGC], pf[0][0:NE, 0:NGC], pbs[:, 0:1], None, ALU.add)
        bd.wait("V", "sV")
        bd.v("tensor_scalar", fb2[:, NGC:NGC + 1], p8sb[:], pbs[:, 0:1], None, ALU.add)
        bd.wait("T", "sV")
        bd.tr(pf[3][0:NGC + 1, 0:NE], fb2[0:NE, 0:NGC + 1], k_idf[0:NE, 0:NE], inc=1)
        bd.wait("V", "sT")
        bd.v("tensor_copy", fbTsb[:], pf[3][0:NGC + 1, 0:NE])
        bd.wait("Y", "sV")
        bd.ydma(fb_sc.rearrange("o (g e) -> (o g) e", e=NE), fbTsb[:])
        bd.ydma(fbrow[:], fb_sc)
        bd.wait("T", "sY")
        bd.mm(pf[4][:, 0:NE], k_orf[:], fbrow[0:1, NGC * NE:(NGC + 1) * NE], inc=1)
        bd.wait("V", "sV")
        bd.v("tensor_scalar", ts512[:], k_pidx[:], 512.0, None, ALU.mult)
        bd.wait("V", "sT")
        bd.wait("V", "sV")
        bd.v("tensor_scalar", fin8b[:], pf[4][:, 0:NE], ts512[:, 0:1], None, ALU.is_le)
        bd.wait("V", "sV")
        bd.v("tensor_reduce", eidf[:], fin8b[:], AX.X, ALU.add)
        bd.wait("V", "sV")
        bd.v("tensor_scalar", eidf[:], eidf[:], 7.0, None, ALU.min)
        bd.wait("V", "sV")
        bd.v("tensor_copy", eidu[:], eidf[:])
        bd.wait("Y", "sV")
        bd.ydma(eidf_sc.rearrange("o (t x) -> (o t) x", x=1), eidf[0:NTILE, 0:1])
        bd.ydma(p2v["eidrowF"][:], eidf_sc)
        bd.wait("T", "sY")
        bd.mm(pf[4][:, 0:NTILE], k_orf[:], p2v["eidrowF"][0:1, 0:NTILE], inc=1)
        bd.wait("V", "sT")
        bd.v("tensor_copy", eidb[:], pf[4][:, 0:NTILE])
        bd.wait("V", "sV")
        bd.v("tensor_scalar", p2v["widxf"][:], eidb[:], 1024.0, k_pidx[:, 0:1], ALU.mult, ALU.add)
        bd.wait("V", "sV")
        bd.v("tensor_copy", widx1[:], p2v["widxf"][:])
        bd.wait("V", "sV")
        bd.v("tensor_scalar", p2v["widxf"][:], eidb[:], 2048.0, k_pidx[:, 0:1], ALU.mult, ALU.add)
        bd.wait("V", "sV")
        bd.v("tensor_copy", widx2[:], p2v["widxf"][:])
        for b_ in range(1, B):
            vbv = valsb[:].rearrange("p (b x v) -> p b x v", b=B, v=4)
            bd.wait("V", "sV")
            bd.v("tensor_scalar", vbv[:, b_, :, 0:1], vbv[:, b_, :, 0:1],
                 float(b_ * S), None, ALU.add)
            bd.wait("V", "sV")
            bd.v("tensor_scalar", vbv[:, b_, :, 2:3], vbv[:, b_, :, 2:3],
                 float(2 * b_ * S), None, ALU.add)
        pos_m = {}
        for g in range(NGC):
            bk = pf[5] if g % 2 == 0 else pf[6]
            if g >= 2:
                bd.wait("T", "sV", pos_m[g - 2])
            else:
                bd.wait("T", "sV")
            bd.mm(bk[:, 0:NE], k_orf[:], fbrow[0:1, g * NE:(g + 1) * NE], inc=1)
            bd.wait("V", "sT")
            bd.wait("V", "sV")
            bd.v("tensor_tensor", posf[:], bk[:, 0:NE], rankm_sb[:, g * 8:(g + 1) * 8], ALU.add)
            bd.wait("V", "sV")
            bd.v("tensor_copy", posu[:, g * 8:(g + 1) * 8], posf[:])
            pos_m[g] = bd.mark("sV")
        bd.wait("G", "sY")
        bd.wait("G", "sV")
        for g in range(NGC):
            for e in range(NE):
                bd.gind(disp, posu[:, g * 8 + e:g * 8 + e + 1],
                        valsb[:, g * 32 + e * 4:g * 32 + e * 4 + 4], None, NSLOT - 1)
        disp_scat_done = bd.mark("sGD")

        def phase_shared(b):
            bd.wait("S", "sT")
            bd.sdma(xnT2[:], xnT_sc.rearrange("p (b x) -> p b x", x=SP8)[:, ds(b, 1), :],
                    sem="sGC")
            xld = bd.mark("sGC")
            bd.wait("T", "sGC", xld)
            gT, uT, dT = fsh_gT, fsh_uT, fsh_dT
            psem = ["sWC", "sWD"]
            wa0 = bd.mark("sWC")
            wb0 = bd.mark("sWD")
            mk = {"dwb_use": bd.mark("sT"), "guse0": bd.mark("sT"),
                  "guse1": bd.mark("sT"), "down": bd.mark("sT"),
                  "silu0": bd.mark("sS"), "silu1": bd.mark("sS"),
                  "mult0": bd.mark("sV"), "mult1": bd.mark("sV"),
                  "evac": bd.mark("sV"), "evac0d": bd.mark("sV"),
                  "evac1d": bd.mark("sV"), "gu_par": 0, "dn_par": 0}
            bd.wait("S", "sT", mk["dwb_use"])
            bd.sdma(dwb[:].rearrange("p (it h) -> p it h", it=16),
                    dT.rearrange("(it p) h -> p it h", p=128), sem="sGC")
            mk["dwb_ld"] = bd.mark("sGC")
            bd.wait("V", "sGC")
            for qr in range(4):
                for g8 in range(2):
                    par = g8 % 2
                    sw = psem[par]
                    if qr == 0:
                        ld_mark = wa0 if par == 0 else wb0
                    else:
                        bd.wait("Y", "sT", mk.get(f"guse{par}", 0))
                        bd.ydma(gwb[:, par * 2048:(par + 1) * 2048].rearrange("p (dt i) -> p dt i", dt=8),
                                gT[:, qr * 512 + g8 * 256: qr * 512 + (g8 + 1) * 256]
                                .rearrange("(dt p) i -> p dt i", p=128), sem=sw)
                        bd.ydma(uwb[:, par * 2048:(par + 1) * 2048].rearrange("p (dt i) -> p dt i", dt=8),
                                uT[:, qr * 512 + g8 * 256: qr * 512 + (g8 + 1) * 256]
                                .rearrange("(dt p) i -> p dt i", p=128), sem=sw)
                        ld_mark = bd.mark(sw)
                    for il in range(2):
                        islq = g8 * 2 + il
                        for tau in range(NT):
                            gp = mk.setdefault("gu_par", 0)
                            pg, pu = (pf[0], pf[1]) if gp == 0 else (pf[2], pf[3])
                            sgp = sgt[:, 0:512] if gp == 0 else sgt2[:, 0:512]
                            bd.wait("T", sw, ld_mark)
                            bd.wait("T", "sS", mk.get(f"silu{gp}", 0))
                            bd.wait("T", "sV", mk.get(f"mult{gp}", 0))
                            if gp == 1:
                                bd.wait("T", "sV", mk.get("evac", 0))
                            for dt_ in range(8):
                                bd.mm(pg[:, 0:512],
                                      gwb[:, par * 2048 + dt_ * 256 + il * 128:
                                          par * 2048 + dt_ * 256 + (il + 1) * 128],
                                      xnT2[:, dt_ * S + tau * 512: dt_ * S + (tau + 1) * 512],
                                      start=(dt_ == 0), stop=(dt_ == 7),
                                      inc=(1 if dt_ == 7 else 0))
                            gate_mark = bd.mark("sT")
                            for dt_ in range(8):
                                bd.mm(pu[:, 0:512],
                                      uwb[:, par * 2048 + dt_ * 256 + il * 128:
                                          par * 2048 + dt_ * 256 + (il + 1) * 128],
                                      xnT2[:, dt_ * S + tau * 512: dt_ * S + (tau + 1) * 512],
                                      start=(dt_ == 0), stop=(dt_ == 7),
                                      inc=(1 if dt_ == 7 else 0))
                            mk[f"guse{par}"] = bd.mark("sT")
                            bd.wait("S", "sT", gate_mark)
                            bd.wait("S", "sV", mk.get(f"mult{gp}", 0))
                            bd.s("activation", sgp, pg[:, 0:512], AF.Silu)
                            mk[f"silu{gp}"] = bd.mark("sS")
                            bd.wait("V", "sS", mk[f"silu{gp}"])
                            bd.wait("V", "sT", mk[f"guse{par}"])
                            bd.wait("V", "sT", mk.get("down", 0))
                            bd.v("tensor_tensor", yg[:, (islq * NT + tau) * 512:(islq * NT + tau + 1) * 512],
                                 sgp, pu[:, 0:512], ALU.mult)
                            mk[f"mult{gp}"] = bd.mark("sV")
                            if g8 == 1 and il == 1:
                                mk[f"multtau{tau}"] = bd.mark("sV")
                            mk["gu_par"] = 1 - gp
                bd.wait("T", "sGC", mk["dwb_ld"])
                first_dn = [True]
                for hh2 in range(2):
                    for tau in range(NT):
                        bd.wait("T", "sV", mk.get(f"multtau{tau}", 0))
                        for tp in range(2):
                            dp = mk.setdefault("dn_par", 0)
                            bank = 2 if dp == 0 else 4
                            if first_dn[0] and dp == 0:
                                bd.wait("T", "sS", mk.get("silu1", 0))
                                bd.wait("T", "sV", mk.get("mult1", 0))
                                first_dn[0] = False
                            bd.wait("T", "sV", mk.get(f"evac{dp}d", 0))
                            for islq in range(4):
                                for t2 in range(2):
                                    ts_ = tp * 2 + t2
                                    isl = qr * 4 + islq
                                    bd.mm(pf[bank + t2][:, 0:512],
                                          yg[:, (islq * NT + tau) * 512 + ts_ * 128:
                                             (islq * NT + tau) * 512 + (ts_ + 1) * 128],
                                          dwb[:, isl * 1024 + hh2 * 512:isl * 1024 + (hh2 + 1) * 512],
                                          start=(islq == 0), stop=(islq == 3),
                                          inc=(1 if (islq == 3 and t2 == 1) else 0))
                            mk["down"] = bd.mark("sT")
                            bd.wait("V", "sT", mk["down"])
                            for t2 in range(2):
                                ts_ = tp * 2 + t2
                                dst = facc[:, (tau * 4 + ts_) * 1024 + hh2 * 512:
                                            (tau * 4 + ts_) * 1024 + (hh2 + 1) * 512]
                                if qr == 0:
                                    bd.v("tensor_copy", dst, pf[bank + t2][:, 0:512],
                                         inc=(1 if t2 == 1 else 0))
                                else:
                                    bd.v("tensor_tensor", dst, pf[bank + t2][:, 0:512], dst, ALU.add,
                                         inc=(1 if t2 == 1 else 0))
                            mk[f"evac{dp}d"] = bd.mark("sV")
                            mk["evac"] = mk[f"evac{dp}d"]
                            mk["dn_par"] = 1 - dp
                mk["dwb_use"] = mk["down"]
                if qr == 3:
                    for g8 in range(2):
                        sw = psem[g8 % 2]
                        bd.wait("Y", "sT", mk.get(f"guse{g8 % 2}", 0))
                        bd.ydma(gwb[:, (g8 % 2) * 2048:(g8 % 2 + 1) * 2048].rearrange("p (dt i) -> p dt i", dt=8),
                                fsh_gT[:, g8 * 256:(g8 + 1) * 256].rearrange("(dt p) i -> p dt i", p=128),
                                sem=sw)
                        bd.ydma(uwb[:, (g8 % 2) * 2048:(g8 % 2 + 1) * 2048].rearrange("p (dt i) -> p dt i", dt=8),
                                fsh_uT[:, g8 * 256:(g8 + 1) * 256].rearrange("(dt p) i -> p dt i", p=128),
                                sem=sw)
            bd.wait("S", "sV", mk["evac"])
            bd.sdma(facc_sc.rearrange("p (b x) -> p b x", x=NCH * DV)[:, ds(b, 1), :], facc[:],
                    sem="sGC")
            bd.wait("S", "sGC")

        for g8 in range(2):
            sw = "sWC" if g8 == 0 else "sWD"
            bd.wait("Y", "sT")
            bd.ydma(gwb[:, g8 * 2048:(g8 + 1) * 2048].rearrange("p (dt i) -> p dt i", dt=8),
                    fsh_gT[:, g8 * 256:(g8 + 1) * 256].rearrange("(dt p) i -> p dt i", p=128),
                    sem=sw)
            bd.ydma(uwb[:, g8 * 2048:(g8 + 1) * 2048].rearrange("p (dt i) -> p dt i", dt=8),
                    fsh_uT[:, g8 * 256:(g8 + 1) * 256].rearrange("(dt p) i -> p dt i", p=128),
                    sem=sw)

        bd.loop("P2_shared", B, phase_shared)

        XG_OF = [0, 4096]
        XGT_OF = 8192
        CMB_OF = 12288
        dispv = dispsb
        tm = {"cast": {}, "gath": {}, "trd": {}, "dnd": {}, "dwb_ld": {}, "dsb": {},
              "wdone": {}, "scatlast": {}, "guall": {}, "wg_ld": {}}
        gu = {"par": 0, "silu0": bd.mark("sS"), "silu1": bd.mark("sS"),
              "mult0": bd.mark("sV"), "mult1": bd.mark("sV"),
              "guse0": bd.mark("sT"), "guse1": bd.mark("sT")}
        dnm = {"par": 0, "ev0": bd.mark("sV"), "ev1": bd.mark("sV"),
               "down": bd.mark("sT"), "scat0": bd.mark("sGD"), "scat1": bd.mark("sGD")}
        bd.wait("G", "sGD")
        bd.wait("G", "sT")
        bd.wait("G", "sY")
        bd.wait("G", "sGC")
        bd.wait("S", "sY")

        fgate_rows = fgate.rearrange("e d i -> (e d) i")
        fup_rows = fup.rearrange("e d i -> (e d) i")
        fdown_rows = fdown.rearrange("e i h -> (e i) h")

        def tile_fetch(t):
            if t >= NTILE:
                return
            par = t % 2
            bd.wait("G", "sV", tm["wdone"].get(t - 2, 0))
            bd.gdma(dispv[:, par * 16:(par + 1) * 16],
                    disp[t * 512:(t + 1) * 512, :].rearrange("(p r) v -> p (r v)", p=128))
            tm["dsb"][t] = bd.mark("sG")
            bd.wait("V", "sG", tm["dsb"][t])
            bd.wait("V", "sGD", tm["scatlast"].get(t - 2, disp_scat_done))
            dvv = dispv[:, par * 16:(par + 1) * 16].rearrange("p (r v) -> p r v", v=4)
            bd.wait("V", "sV")
            bd.v("tensor_copy", posu[:, par * 16:par * 16 + 4], dvv[:, :, 0])
            bd.v("tensor_copy", posu[:, par * 16 + 4:par * 16 + 8], dvv[:, :, 2])
            tm["cast"][t] = bd.mark("sV")
            bd.wait("G", "sV", tm["cast"][t])
            bd.wait("G", "sT", tm["trd"].get(t - 2, 0))
            for r in range(4):
                bd.gind(xnT2[:, XG_OF[par] + r * 1024:XG_OF[par] + (r + 1) * 1024], None,
                        xn_tok, posu[:, par * 16 + r:par * 16 + r + 1], T_TOK)
            tm["gath"][t] = bd.mark("sGD")

        def wg_fetch(t):
            if t >= NTILE:
                return
            bd.wait("G", "sT", tm["guall"].get(t - 1, 0))
            for j in range(8):
                bd.gind(gwb[:, j * FI:(j + 1) * FI], None, fgate_rows,
                        widx1[:, t:t + 1], NE * 1024 - 1, element_offset=j * 128 * FI)
            for j in range(8):
                bd.gind(uwb[:, j * FI:(j + 1) * FI], None, fup_rows,
                        widx1[:, t:t + 1], NE * 1024 - 1, element_offset=j * 128 * FI)
            tm["wg_ld"][t] = bd.mark("sGD")

        def dwb_fetch(t):
            if t >= NTILE:
                return
            bd.wait("G", "sT", tm["dnd"].get(t - 1, 0))
            for j in range(16):
                bd.gind(dwb[:, j * 1024:(j + 1) * 1024], None, fdown_rows,
                        widx2[:, t:t + 1], NE * FI - 1, element_offset=j * 128 * 1024)
            tm["dwb_ld"][t] = bd.mark("sGD")

        def tile_compute(t):
            par = t % 2
            prev_ev = 0
            for r in range(4):
                bd.wait("T", "sGD", tm["gath"][t])
                bd.wait("T", "sV", prev_ev)
                for dt_ in range(8):
                    bd.tr(pb[:, dt_ * 128:(dt_ + 1) * 128],
                          xnT2[:, XG_OF[par] + r * 1024 + dt_ * 128:XG_OF[par] + r * 1024 + (dt_ + 1) * 128],
                          k_id[:], inc=(1 if dt_ == 7 else 0))
                trm = bd.mark("sT")
                bd.wait("V", "sT", trm)
                for dt_ in range(8):
                    bd.v("tensor_copy",
                         xnT2[:, XGT_OF + dt_ * 512 + r * 128:XGT_OF + dt_ * 512 + (r + 1) * 128],
                         pb[:, dt_ * 128:(dt_ + 1) * 128], inc=(1 if dt_ == 7 else 0))
                prev_ev = bd.mark("sV")
            tm["trd"][t] = bd.mark("sT")
            xgev_all = prev_ev
            for qr in range(4):
                for g8 in range(2):
                    qg = qr * 2 + g8
                    for il in range(2):
                        isl = qr * 4 + g8 * 2 + il
                        gp = gu["par"]
                        pg, pu = (pf[0], pf[1]) if gp == 0 else (pf[2], pf[3])
                        sgp = sgt[:, 0:512] if gp == 0 else sgt2[:, 0:512]
                        bd.wait("T", "sGD", tm["wg_ld"][t])
                        bd.wait("T", "sV", xgev_all)
                        bd.wait("T", "sS", gu[f"silu{gp}"])
                        bd.wait("T", "sV", gu[f"mult{gp}"])
                        for dt_ in range(8):
                            bd.mm(pg[:, 0:512],
                                  gwb[:, dt_ * FI + qg * 256 + il * 128:
                                      dt_ * FI + qg * 256 + (il + 1) * 128],
                                  xnT2[:, XGT_OF + dt_ * 512:XGT_OF + (dt_ + 1) * 512],
                                  start=(dt_ == 0), stop=(dt_ == 7), inc=(1 if dt_ == 7 else 0))
                        gate_mark = bd.mark("sT")
                        for dt_ in range(8):
                            bd.mm(pu[:, 0:512],
                                  uwb[:, dt_ * FI + qg * 256 + il * 128:
                                      dt_ * FI + qg * 256 + (il + 1) * 128],
                                  xnT2[:, XGT_OF + dt_ * 512:XGT_OF + (dt_ + 1) * 512],
                                  start=(dt_ == 0), stop=(dt_ == 7), inc=(1 if dt_ == 7 else 0))
                        bd.wait("S", "sT", gate_mark)
                        bd.wait("S", "sV", gu[f"mult{gp}"])
                        bd.s("activation", sgp, pg[:, 0:512], AF.Silu)
                        gu[f"silu{gp}"] = bd.mark("sS")
                        bd.wait("V", "sS", gu[f"silu{gp}"])
                        bd.wait("V", "sT", bd.mark("sT"))
                        bd.wait("V", "sT", dnm["down"])
                        bd.v("tensor_tensor", yg[:, isl * 512:(isl + 1) * 512], sgp, pu[:, 0:512],
                             ALU.mult)
                        gu[f"mult{gp}"] = bd.mark("sV")
                        gu["par"] = 1 - gp
            tm["guall"][t] = bd.mark("sT")
            wg_fetch(t + 1)
            yg_done = bd.mark("sV")
            bd.wait("T", "sGD", tm["dwb_ld"][t])
            bd.wait("T", "sV", yg_done)
            for q in range(4):
                cpar = q % 2
                for hh2 in range(2):
                    dp = dnm["par"]
                    bank = pf[4] if dp == 0 else pf[5]
                    bd.wait("T", "sV", dnm[f"ev{dp}"])
                    for isl in range(16):
                        bd.mm(bank[:, 0:512],
                              yg[:, isl * 512 + q * 128:isl * 512 + (q + 1) * 128],
                              dwb[:, isl * 1024 + hh2 * 512:isl * 1024 + (hh2 + 1) * 512],
                              start=(isl == 0), stop=(isl == 15), inc=(1 if isl == 15 else 0))
                    dnm["down"] = bd.mark("sT")
                    bd.wait("V", "sT", dnm["down"])
                    bd.wait("V", "sGD", dnm[f"scat{cpar}"])
                    wc = dispv[:, par * 16 + q * 4 + 1:par * 16 + q * 4 + 2]
                    bd.v("tensor_scalar",
                         xnT2[:, CMB_OF + cpar * 1024 + hh2 * 512:CMB_OF + cpar * 1024 + (hh2 + 1) * 512],
                         bank[:, 0:512], wc, None, ALU.mult)
                    dnm[f"ev{dp}"] = bd.mark("sV")
                    dnm["par"] = 1 - dp
                bd.wait("G", "sV", dnm[f"ev{1 - dnm['par']}"])
                bd.gind(comb, posu[:, par * 16 + 4 + q:par * 16 + 5 + q],
                        xnT2[:, CMB_OF + cpar * 1024:CMB_OF + (cpar + 1) * 1024], None,
                        2 * T_TOK - 1)
                dnm[f"scat{cpar}"] = bd.mark("sGD")
            tm["scatlast"][t] = bd.mark("sGD")
            tm["wdone"][t] = dnm[f"ev{1 - dnm['par']}"]
            tm["dnd"][t] = bd.mark("sT")
            dwb_fetch(t + 1)

        bd.wait("G", "sV")
        tile_fetch(0)
        wg_fetch(0)
        dwb_fetch(0)
        tile_fetch(1)
        for t in range(NTILE):
            tile_compute(t)
            tile_fetch(t + 2)

        gwbF2 = gwb.bitcast(F32)
        uwbF2 = uwb.bitcast(F32)
        tl = {
            "outsb": [gwbF2[:, 0:1024], gwbF2[:, 1024:2048]],
            "x": [gwbF2[:, 2048:3072], gwbF2[:, 3072:4096]],
            "oat": [gwb[:, 8192:9216], gwb[:, 9216:10240]],
            "xst": [gwb[:, 10240:11264], gwb[:, 11264:12288]],
            "on": [gwb[:, 12288:13312], gwb[:, 13312:14336]],
            "og": [gwb[:, 14336:15360], gwb[:, 15360:16384]],
            "facc": [uwb[:, 0:1024], uwb[:, 1024:2048]],
            "comb": [uwb[:, 2048:4096], uwb[:, 4096:6144]],
            "ogT": [uwb[:, 6144:7168], uwb[:, 7168:8192]],
            "sq": [uwbF2[:, 4096:4352], uwbF2[:, 4352:4608]],
            "ssum": [uwbF2[:, 4608:4612], uwbF2[:, 4612:4616]],
            "rstd": [uwbF2[:, 4616:4620], uwbF2[:, 4620:4624]],
            "sgt": [sgt[:, 0:512], sgt2[:, 0:512]],
        }
        bd.wait("G", "sT")
        bd.wait("G", "sGD")
        bd.gdma(dwb[:, 0:8 * 1024].rearrange("p (dt v) -> p dt v", dt=8),
                w_g.rearrange("(dt p) v -> p dt v", p=128))
        bd.gdma(dwb[:, 8 * 1024:16 * 1024].rearrange("p (vt d) -> p vt d", vt=8),
                w_o.rearrange("(vt p) d -> p vt d", p=128))
        wgwo_ld = bd.mark("sG")
        tmk = {"yld": {}, "gld": {}, "use": {}, "gmm": {}, "womm": {}, "silu": {},
               "oguse": {}, "outw": {}}
        xnv = xnT_sc.rearrange("p (b dt s k) -> p b dt s k", b=B, dt=8, s=NCH)
        oav = oacc_sc.rearrange("p (b s v) -> p b s v", b=B, s=NCH)
        fav = facc_sc.rearrange("p (b s v) -> p b s v", b=B, s=NCH)
        cmv = comb.rearrange("(b s p k) h -> b s p (k h)", s=NCH, p=128, k=2)
        xsv = xs.rearrange("(b s p) d -> b s p d", s=NCH, p=128)
        ouv = out.rearrange("(b s p) d -> b s p d", s=NCH, p=128)

        def tail_slice(idx):
            b_, s_ = idx % B, idx // B
            ln = idx % 2
            bd.wait("Y", "sV", tmk["use"].get(idx - 2, 0))
            bd.ydma(tl["xst"][ln], xnv[:, b_, :, s_, :], sem="sY")
            bd.ydma(tl["x"][ln], xsv[b_, s_, :, :], sem="sY")
            tmk["yld"][idx] = bd.mark("sY")
            bd.wait("G", "sV", tmk["use"].get(idx - 2, 0))
            bd.gdma(tl["oat"][ln], oav[:, b_, s_, :])
            bd.gdma(tl["facc"][ln], fav[:, b_, s_, :])
            bd.gdma(tl["comb"][ln], cmv[b_, s_, :, :])
            tmk["gld"][idx] = bd.mark("sG")
            b0 = 0 if ln == 0 else 4
            bd.wait("T", "sG", wgwo_ld)
            bd.wait("T", "sY", tmk["yld"][idx])
            bd.wait("T", "sS", tmk["silu"].get(idx - 2, 0))
            for half in range(2):
                for dt_ in range(8):
                    bd.mm(pf[b0 + half][:, 0:512], tl["xst"][ln][:, dt_ * 128:(dt_ + 1) * 128],
                          dwb[:, dt_ * 1024 + half * 512:dt_ * 1024 + (half + 1) * 512],
                          start=(dt_ == 0), stop=(dt_ == 7), inc=(1 if dt_ == 7 else 0))
            tmk["gmm"][idx] = bd.mark("sT")
            bd.wait("S", "sG", tmk["gld"][idx])
            bd.wait("S", "sV", tmk["use"].get(idx - 2, 0))
            for hh in range(H):
                bd.wait("S", "sS")
                bd.s("activation", tl["sq"][ln], tl["oat"][ln][:, hh * 256:(hh + 1) * 256],
                     AF.Square, accum_out=tl["ssum"][ln][:, hh:hh + 1])
            bd.wait("S", "sS")
            bd.s("activation", tl["rstd"][ln], tl["ssum"][ln], AF.Ln, scale=1.0 / HDV, bias=EPS)
            bd.wait("S", "sS")
            bd.s("activation", tl["rstd"][ln], tl["rstd"][ln], AF.Exp, scale=-0.5)
            rsm = bd.mark("sS")
            bd.wait("V", "sS", rsm)
            for hh in range(H):
                bd.v("tensor_scalar", tl["on"][ln][:, hh * 256:(hh + 1) * 256],
                     tl["oat"][ln][:, hh * 256:(hh + 1) * 256],
                     tl["rstd"][ln][:, hh:hh + 1], None, ALU.mult)
            bd.wait("V", "sV")
            bd.v("tensor_mul", tl["on"][ln][:, 0:512], tl["on"][ln][:, 0:512], k_nw[:, 0:512])
            bd.v("tensor_mul", tl["on"][ln][:, 512:1024], tl["on"][ln][:, 512:1024], k_nw[:, 512:1024])
            onm = bd.mark("sV")
            bd.wait("S", "sT", tmk["gmm"][idx])
            bd.wait("S", "sV", tmk["oguse"].get(idx - 2, 0))
            bd.s("activation", tl["sgt"][ln], pf[b0][:, 0:512], AF.Silu)
            s1 = bd.mark("sS")
            bd.wait("V", "sS", s1)
            bd.wait("V", "sV", onm)
            bd.v("tensor_tensor", tl["og"][ln][:, 0:512], tl["on"][ln][:, 0:512], tl["sgt"][ln], ALU.mult)
            bd.wait("S", "sV")
            bd.s("activation", tl["sgt"][ln], pf[b0 + 1][:, 0:512], AF.Silu)
            tmk["silu"][idx] = bd.mark("sS")
            bd.wait("V", "sS", tmk["silu"][idx])
            bd.v("tensor_tensor", tl["og"][ln][:, 512:1024], tl["on"][ln][:, 512:1024], tl["sgt"][ln], ALU.mult)
            ogm = bd.mark("sV")
            bd.wait("T", "sV", ogm)
            bd.wait("T", "sV", tmk["oguse"].get(idx - 1, 0))
            for vt in range(8):
                bd.tr(pb[:, vt * 128:(vt + 1) * 128], tl["og"][ln][:, vt * 128:(vt + 1) * 128],
                      k_id[:], inc=(1 if vt == 7 else 0))
            trm = bd.mark("sT")
            bd.wait("V", "sT", trm)
            bd.v("tensor_copy", tl["ogT"][ln], pb[:])
            tmk["oguse"][idx] = bd.mark("sV")
            bd.wait("T", "sV", tmk["oguse"][idx])
            bd.wait("T", "sV", tmk["outw"].get(idx - 1, 0))
            for half in range(2):
                for vt in range(8):
                    bd.mm(pf[2 + half][:, 0:512], tl["ogT"][ln][:, vt * 128:(vt + 1) * 128],
                          dwb[:, 8 * 1024 + vt * 1024 + half * 512:8 * 1024 + vt * 1024 + (half + 1) * 512],
                          start=(vt == 0), stop=(vt == 7), inc=(1 if vt == 7 else 0))
            tmk["womm"][idx] = bd.mark("sT")
            ob = tl["outsb"][ln]
            bd.wait("V", "sT", tmk["womm"][idx])
            bd.wait("V", "sY", tmk["yld"][idx])
            bd.wait("V", "sG", tmk["gld"][idx])
            for half in range(2):
                bd.v("tensor_tensor", ob[:, half * 512:(half + 1) * 512], pf[2 + half][:, 0:512],
                     tl["x"][ln][:, half * 512:(half + 1) * 512], ALU.add)
            bd.wait("V", "sV")
            for half in range(2):
                bd.v("tensor_tensor", ob[:, half * 512:(half + 1) * 512],
                     ob[:, half * 512:(half + 1) * 512],
                     tl["facc"][ln][:, half * 512:(half + 1) * 512], ALU.add)
            bd.wait("V", "sV")
            for half in range(2):
                bd.v("tensor_tensor", tl["on"][ln][:, half * 512:(half + 1) * 512],
                     tl["comb"][ln][:, half * 512:(half + 1) * 512],
                     tl["comb"][ln][:, 1024 + half * 512:1024 + (half + 1) * 512], ALU.add)
            bd.wait("V", "sV")
            for half in range(2):
                bd.v("tensor_tensor", ob[:, half * 512:(half + 1) * 512],
                     ob[:, half * 512:(half + 1) * 512],
                     tl["on"][ln][:, half * 512:(half + 1) * 512], ALU.add)
            tmk["use"][idx] = bd.mark("sV")
            bd.wait("Y", "sV", tmk["use"][idx])
            bd.ydma(ouv[b_, s_, :, :], ob, sem="sY")
            tmk["outw"][idx] = tmk["use"][idx]

        bd.wait("T", "sV")
        bd.wait("S", "sV")
        bd.wait("Y", "sT")
        bd.wait("Y", "sGD")
        for idx in range(B * NCH):
            tail_slice(idx)

        bd.wait("Y", "sY", bd.mark("sY"))

    bd0 = Bd("C", None, {k: None for k in SEMS}, loop_delta)
    bds = {role: None for role in ROLES}

    def run_role(role, fn):
        def f(eng):
            if bds[role] is None:
                bds[role] = Bd(role, eng, sems, loop_delta)
            else:
                bds[role].eng = eng
            fn(bds[role])
        return f

    with es:
        with es_ab:
            describe_ab(bd0)
            with nc.Block() as block:
                block.tensor(run_role("T", describe_ab))
                block.vector(run_role("V", describe_ab))
                block.scalar(run_role("S", describe_ab))
                block.gpsimd(run_role("G", describe_ab))
                block.sync(run_role("Y", describe_ab))
        with es_p2:
            alloc_p2()
            describe_p2(bd0)
            with nc.Block() as block:
                block.tensor(run_role("T", describe_p2))
                block.vector(run_role("V", describe_p2))
                block.scalar(run_role("S", describe_p2))
                block.gpsimd(run_role("G", describe_p2))
                block.sync(run_role("Y", describe_p2))
    return nc


_CACHE = {}


def _consts():
    ident = np.eye(128, dtype=np.float32)
    t = np.tril(np.ones((128, 128), np.float32)).T
    tri16 = (t / 16.0).astype(np.float32)
    triS = (t - np.eye(128, dtype=np.float32))
    onesr = np.ones((1, 128), np.float32)
    ov16 = np.full((128, 1), 1 / 16.0, np.float32)
    io4 = np.tile(np.arange(4, dtype=np.float32)[None, :], (128, 1))
    io8 = np.tile(np.arange(8, dtype=np.float32)[None, :], (128, 1))
    m4 = np.ascontiguousarray(np.tile(t, (1, 4)))
    misc = np.tile(np.array([[1e-5, -0.5, -1.0, 1.0 / 256.0]], np.float32), (128, 1))
    pidx = np.arange(128, dtype=np.float32).reshape(128, 1)
    th = np.tile((np.arange(16, dtype=np.float32) * 512.0)[None, :], (128, 1))
    return dict(c_id=ident, c_t16=tri16, c_tS=triS, c_or=onesr, c_ov16=ov16,
                c_io4=io4, c_io8=io8, c_m4=m4, c_misc=misc, c_pidx=pidx, c_th=th)


def make_inmap(x_shard, ins, B, S):
    c = np.ascontiguousarray
    m = dict(
        xs=c(x_shard.reshape(B * S, D)),
        w_gate=ins["L0_attn_w_gate"],
        w_q=ins["L0_attn_w_q"],
        w_k=ins["L0_attn_w_k"],
        w_v=ins["L0_attn_w_v"],
        w_gk1T=c(np.asarray(ins["L0_attn_w_gk1"]).transpose(0, 2, 1)),
        w_gk2=ins["L0_attn_w_gk2"],
        b_gk=ins["L0_attn_b_gk"],
        w_ks=ins["L0_attn_w_ks"],
        w_vs=ins["L0_attn_w_vs"],
        w_gk1sT=c(np.asarray(ins["L0_attn_w_gk1s"]).T),
        w_gk2s=ins["L0_attn_w_gk2s"],
        b_gks=np.asarray(ins["L0_attn_b_gks"]).reshape(1, -1),
        w_g=ins["L0_attn_w_g"],
        w_o=ins["L0_attn_w_o"],
        normw4=np.tile(np.asarray(ins["L0_attn_norm_w"]), 4).reshape(1, -1),
        frouterT=c(np.asarray(ins["L0_ffn_router_w"]).T),
        fbias=np.asarray(ins["L0_ffn_routing_bias"]).reshape(1, -1),
    )
    m.update(_consts())
    m = {k: np.ascontiguousarray(np.asarray(v), dtype=np.float32) for k, v in m.items()}
    import ml_dtypes
    bf = lambda a: np.ascontiguousarray(np.asarray(a, dtype=np.float32)).astype(ml_dtypes.bfloat16)
    m["fsh_gT"] = bf(np.asarray(ins["L0_ffn_shared_gate"]).T)
    m["fsh_uT"] = bf(np.asarray(ins["L0_ffn_shared_up"]).T)
    m["fsh_dT"] = bf(np.asarray(ins["L0_ffn_shared_down"]).T)
    m["fgate"] = bf(ins["L0_ffn_gate_w"])
    m["fup"] = bf(ins["L0_ffn_up_w"])
    m["fdown"] = bf(ins["L0_ffn_down_w"])
    return m


def kernel(**inputs):
    NCORES = 8
    x = np.asarray(inputs["L0_attn_x"])
    BT, S = x.shape[0], x.shape[1]
    B = BT // NCORES
    key = (B, S)
    if key not in _CACHE:
        _CACHE[key] = build_program(B, S)
    nc = _CACHE[key]
    in_maps = [make_inmap(x[i * B:(i + 1) * B], inputs, B, S) for i in range(NCORES)]
    res = run_bass_kernel_spmd(nc, in_maps, core_ids=list(range(NCORES)))
    return np.concatenate([r["out"].reshape(B, S, D) for r in res.results], axis=0)
